# version 03
# baseline (speedup 1.0000x reference)
import contextlib
import numpy as np
import ml_dtypes
import concourse.bass as bass
import concourse.mybir as mybir
from concourse.bass_utils import run_bass_kernel_spmd

F32 = mybir.dt.float32
BF16 = mybir.dt.bfloat16
AF = mybir.ActivationFunctionType
AX = mybir.AxisListType.X
OP = mybir.AluOpType

H = 1024
S = 8192
NT = 17
NB = 64
NLN = NT + NB
QR = NT * 128
EPS = 1e-5
INV_H = 1.0 / H
NCHUNKS = [(0, 512), (512, 512), (1024, 512), (1536, 512), (2048, 128)]


def build_nc(phases=4, dbg=False):
    nc = bass.Bass("TRN2", target_bir_lowering=False, detect_race_conditions=False)

    x_batch = nc.dram_tensor("x_batch", [S, H], F32, kind="ExternalInput")
    x_own = nc.dram_tensor("x_own", [QR, H], F32, kind="ExternalInput")
    wq = nc.dram_tensor("wq", [H, H], F32, kind="ExternalInput")
    wk = nc.dram_tensor("wk", [H, H], F32, kind="ExternalInput")
    wv = nc.dram_tensor("wv", [H, H], F32, kind="ExternalInput")
    woutT = nc.dram_tensor("woutT", [H, H], F32, kind="ExternalInput")
    wrT = nc.dram_tensor("wrT", [H, H], F32, kind="ExternalInput")
    wkfT = nc.dram_tensor("wkfT", [H, H], F32, kind="ExternalInput")
    wvfT = nc.dram_tensor("wvfT", [H, H], F32, kind="ExternalInput")
    mu_r_in = nc.dram_tensor("mu_r_in", [128, 8], F32, kind="ExternalInput")
    mu_k_in = nc.dram_tensor("mu_k_in", [128, 8], F32, kind="ExternalInput")
    identb_in = nc.dram_tensor("identb_in", [128, 128], BF16, kind="ExternalInput")
    identf_in = nc.dram_tensor("identf_in", [128, 128], F32, kind="ExternalInput")
    hmask_in = nc.dram_tensor("hmask_in", [128, 1], F32, kind="ExternalInput")

    o_out = nc.dram_tensor("o_out", [16 * 128, H], F32, kind="ExternalOutput")

    KD = "ExternalOutput" if dbg else "Internal"
    s_lnown = nc.dram_tensor("s_lnown", [QR, H], BF16, kind=KD)
    s_xt = nc.dram_tensor("s_xt", [S, H], BF16, kind=KD)
    s_p = nc.dram_tensor("s_p", [NT, 64, 128, 128], BF16, kind=KD)
    s_x1 = nc.dram_tensor("s_x1", [NT, 128, H], F32, kind=KD)
    s_ln1 = nc.dram_tensor("s_ln1", [QR, H], BF16, kind=KD)
    s_negb = nc.dram_tensor("s_negb", [NT, 128, 1], F32, kind=KD)
    s_qpf = nc.dram_tensor("s_qpf", [8, 128, QR], BF16, kind=KD)
    s_wvo = nc.dram_tensor("s_wvo", [128, 8 * H], BF16, kind="Internal")

    with contextlib.ExitStack() as st:
        xstage = [st.enter_context(nc.sbuf_tensor(f"xstage{p}", [128, H], F32)) for p in range(2)]
        dst_sb = [st.enter_context(nc.sbuf_tensor(f"dstsb{p}", [128, H], BF16)) for p in range(2)]
        scrf32 = [st.enter_context(nc.sbuf_tensor(f"scrf32{p}", [128, H], F32)) for p in range(2)]
        sq_dummy, x1_sb = scrf32[0], scrf32[1]
        out_sb = scrf32
        ctxX_sb = st.enter_context(nc.sbuf_tensor("ctxX", [128, H], BF16))
        ctxXf_sb = st.enter_context(nc.sbuf_tensor("ctxXf", [128, H], BF16))
        identb = st.enter_context(nc.sbuf_tensor("identb", [128, 128], BF16))
        ones_bf = st.enter_context(nc.sbuf_tensor("ones_bf", [128, 1], BF16))
        negb_cols = st.enter_context(nc.sbuf_tensor("negb_cols", [128, 4 * NT], F32))
        s_all = st.enter_context(nc.sbuf_tensor("s_all", [128, 4 * NT], F32))
        sinv_all = st.enter_context(nc.sbuf_tensor("sinv_all", [128, 4 * NT], F32))
        hmask = st.enter_context(nc.sbuf_tensor("hmask", [128, 1], F32))
        c_ssq = st.enter_context(nc.sbuf_tensor("c_ssq", [128, 8], F32))
        c_ngs = st.enter_context(nc.sbuf_tensor("c_ngs", [128, 8], F32))
        c_nm = st.enter_context(nc.sbuf_tensor("c_nm", [128, 8], F32))
        c_t1 = st.enter_context(nc.sbuf_tensor("c_t1", [128, 8], F32))
        c_m2 = st.enter_context(nc.sbuf_tensor("c_m2", [128, 8], F32))
        c_vpe = st.enter_context(nc.sbuf_tensor("c_vpe", [128, 8], F32))
        c_rin = st.enter_context(nc.sbuf_tensor("c_rin", [128, 8], F32))
        c_rsd = st.enter_context(nc.sbuf_tensor("c_rsd", [128, 8], F32))
        b_row = st.enter_context(nc.sbuf_tensor("b_row", [1, QR], F32))

        sem_names = ["d_ld0", "d_ld1", "d_st0", "d_st1", "d_wmov", "d_wcol0",
                     "d_wcol1", "d_misc", "d_wf", "d_xfo", "d_xf", "d_xt",
                     "d_nb", "d_wvost0", "d_wvost1", "d_wvold", "d_pst",
                     "d_pt0", "d_pt1", "d_xo20", "d_xo21", "d_x1st", "d_lnf",
                     "d_x130", "d_x131", "d_ost0", "d_ost1", "d_qst0", "d_qst1",
                     "d_q2l0", "d_q2l1", "d_qt0", "d_qt1", "s_pvh",
                     "s_sq", "s_var", "s_rstd", "s_app",
                     "s_fmm", "s_fev", "s_qmm", "s_qev", "s_q2", "s_bmm",
                     "s_bsq", "s_bneg", "s_sc", "s_exp", "s_sv",
                     "s_pv", "s_ctx", "s_trsT", "s_trsV", "s_op", "s_x1v",
                     "s_mask", "s_xr", "s_rp", "s_sig", "s_kp", "s_rel",
                     "s_ka", "s_vp", "s_ff", "s_tr3", "s_outv"]
        sem = {n: st.enter_context(nc.semaphore(n)) for n in sem_names}

        def ln_src(k):
            return x1_sb if k >= NLN else xstage[k % 2]

        def ln_dst_dram(k):
            if k < NT:
                return s_lnown[k * 128:(k + 1) * 128, :]
            if k < NLN:
                kb = k - NT
                return s_xt[kb * 128:(kb + 1) * 128, :]
            kc = k - NLN
            return s_ln1[kc * 128:(kc + 1) * 128, :]

        def ln_vec_apply(v, j):
            pj = j % 2
            q = 4 * pj
            v.wait_ge(sem["s_rstd"], j + 1)
            if j >= 2:
                v.wait_ge(sem[f"d_st{pj}"], 16 * (j // 2))
            v.tensor_scalar(dst_sb[pj][:, :], ln_src(j)[:, :],
                            c_nm[:, q:q + 1], c_rsd[:, q:q + 1],
                            OP.add, OP.mult).then_inc(sem["s_app"], 1)

        def ln_vec(v, k, do_prev=True, last=False):
            q = 4 * (k % 2)
            v.reduce_sum(c_ngs[:, q:q + 1], ln_src(k)[:, :], axis=AX, negate=True)
            v.wait_ge(sem["s_sq"], k + 1)
            v.tensor_scalar(c_t1[:, q:q + 1], c_ssq[:, q:q + 1], INV_H, EPS, OP.mult, OP.add)
            v.tensor_scalar_mul(c_nm[:, q:q + 1], c_ngs[:, q:q + 1], INV_H)
            v.tensor_copy(c_ngs[:, 2:3], c_t1[:, q:q + 1])
            v.tensor_mul(c_m2[:, q:q + 1], c_nm[:, q:q + 1], c_nm[:, q:q + 1])
            v.tensor_copy(c_ngs[:, 3:4], c_t1[:, q:q + 1])
            v.tensor_sub(c_vpe[:, q:q + 1], c_t1[:, q:q + 1], c_m2[:, q:q + 1])
            v.tensor_copy(c_ngs[:, 6:7], c_t1[:, q:q + 1])
            v.reciprocal(c_rin[:, q:q + 1], c_vpe[:, q:q + 1]).then_inc(sem["s_var"], 1)
            if do_prev and k >= 1:
                ln_vec_apply(v, k - 1)
            if last:
                ln_vec_apply(v, k)

        def ln_scal_sqrt_store(scl, j):
            pj = j % 2
            q = 4 * pj
            scl.wait_ge(sem["s_var"], j + 1)
            scl.activation(c_rsd[:, q:q + 1], c_rin[:, q:q + 1],
                           AF.Sqrt).then_inc(sem["s_rstd"], 1)
            scl.wait_ge(sem["s_app"], j + 1)
            scl.dma_start(ln_dst_dram(j), dst_sb[pj][:, :]).then_inc(sem[f"d_st{pj}"], 16)

        def ln_scal(scl, k, do_prev=True, last=False):
            q = 4 * (k % 2)
            scl.activation(sq_dummy[:, :], ln_src(k)[:, :], AF.Square,
                           accum_out=c_ssq[:, q:q + 1]).then_inc(sem["s_sq"], 1)
            if do_prev and k >= 1:
                ln_scal_sqrt_store(scl, k - 1)
            if last:
                ln_scal_sqrt_store(scl, k)

        rstk = contextlib.ExitStack()
        qpf_tile = [rstk.enter_context(nc.sbuf_tensor(f"qpt{p}", [128, 8 * 128], BF16, side="right"))
                    for p in range(2)]
        p_sb = rstk.enter_context(nc.sbuf_tensor("p_sb", [128, S], BF16, side="right"))

        pstk = contextlib.ExitStack()
        ps8 = [pstk.enter_context(nc.psum_tensor(f"ps8_{i}", [128, 512], F32)) for i in range(8)]

        prostk = contextlib.ExitStack()
        wqkS = prostk.enter_context(nc.sbuf_tensor("wqkS", [128, 8 * H], BF16))
        movstage = prostk.enter_context(nc.sbuf_tensor("movstage", [128, 8 * H], BF16))
        colbuf = [prostk.enter_context(nc.sbuf_tensor(f"colbuf{p}", [128, 8 * 128], BF16)) for p in range(2)]
        wvo_stage = [prostk.enter_context(nc.sbuf_tensor(f"wvostage{p}", [128, H], BF16)) for p in range(2)]
        xfo = prostk.enter_context(nc.sbuf_tensor("xfo", [128, 8 * QR], BF16))
        q2buf = [prostk.enter_context(nc.sbuf_tensor(f"q2buf{p}", [128, QR], BF16)) for p in range(2)]
        qstage = [prostk.enter_context(nc.sbuf_tensor(f"qstage{p}", [128, 512], BF16)) for p in range(2)]
        q2ld = [prostk.enter_context(nc.sbuf_tensor(f"q2ld{p}", [128, QR], BF16)) for p in range(2)]

        with nc.Block() as blk:
            @blk.sync
            def _(s):
                for k in range(NLN):
                    p = k % 2
                    if k >= 2:
                        s.wait_ge(sem["s_app"], k - 1)
                    if k < NT:
                        src = x_own[k * 128:(k + 1) * 128, :]
                    else:
                        kb = k - NT
                        src = x_batch[kb * 128:(kb + 1) * 128, :]
                    s.dma_start(xstage[p][:, :], src).then_inc(sem[f"d_ld{p}"], 16)
                    if k == NT - 1:
                        s.wait_ge(sem["d_st0"], 16 * 9)
                        s.wait_ge(sem["d_st1"], 16 * 8)
                        for g in range(8):
                            s.dma_start_transpose(
                                xfo[:, g * QR:(g + 1) * QR],
                                s_lnown.ap()[:, g * 128:(g + 1) * 128]
                            ).then_inc(sem["d_xfo"], 16)

            @blk.gpsimd
            def _(g):
                g.dma_start(movstage.rearrange("p (c f) -> p c f", f=H),
                            wk.rearrange("(c p) f -> p c f", p=128)).then_inc(sem["d_wmov"], 16)
                for gg in range(8):
                    if gg >= 2:
                        g.wait_ge(sem["s_fmm"], gg - 1)
                    g.dma_start(colbuf[gg % 2].rearrange("p (c f) -> p c f", f=128),
                                wq.rearrange("(c p) f -> p c f", p=128)[:, :, gg * 128:(gg + 1) * 128]
                                ).then_inc(sem[f"d_wcol{gg % 2}"], 16)
                g.dma_start(identb[:, :], identb_in[:, :]).then_inc(sem["d_misc"], 16)
                g.dma_start(hmask[:, :], hmask_in[:, :]).then_inc(sem["d_misc"], 16)
                g.wait_ge(sem["s_fmm"], 8)
                g.dma_start(movstage.rearrange("p (c f) -> p c f", f=H),
                            woutT.rearrange("(c p) f -> p c f", p=128)).then_inc(sem["d_wmov"], 16)
                for gg in range(8):
                    g.wait_ge(sem["s_fmm"], 8 if gg < 2 else 8 + gg - 1)
                    g.dma_start(colbuf[gg % 2].rearrange("p (c f) -> p c f", f=128),
                                wv.rearrange("(c p) f -> p c f", p=128)[:, :, gg * 128:(gg + 1) * 128]
                                ).then_inc(sem[f"d_wcol{gg % 2}"], 16)
                for gp in range(8):
                    for ncx, (off, nw) in enumerate(NCHUNKS):
                        idx = gp * 5 + ncx
                        g.wait_ge(sem["s_qev"], idx + 1)
                        g.dma_start(s_qpf[gp][:, off:off + nw],
                                    qstage[idx % 2][:, 0:nw]).then_inc(sem[f"d_qst{idx % 2}"], 16)
                g.wait_ge(sem["d_qst0"], 16 * 20)
                g.wait_ge(sem["d_qst1"], 16 * 20)
                for gp in range(8):
                    if gp >= 2:
                        g.wait_ge(sem["s_q2"], gp - 1)
                    g.dma_start(q2ld[gp % 2][:, :], s_qpf[gp]).then_inc(sem[f"d_q2l{gp % 2}"], 16)

            @blk.tensor
            def _(t):
                t.wait_ge(sem["d_wmov"], 16)
                for gg in range(8):
                    t.wait_ge(sem[f"d_wcol{gg % 2}"], 16 * (gg // 2 + 1))
                    if gg >= 2:
                        t.wait_ge(sem["s_fev"], gg - 1)
                    pe, po = ps8[(gg % 2) * 2], ps8[(gg % 2) * 2 + 1]
                    cb = colbuf[gg % 2]
                    for ig in range(8):
                        stat = cb[:, ig * 128:(ig + 1) * 128]
                        t.matmul(pe[:, :], stat, movstage[:, ig * H:ig * H + 512],
                                 start=(ig == 0), stop=(ig == 7))
                        mm = t.matmul(po[:, :], stat, movstage[:, ig * H + 512:(ig + 1) * H],
                                      start=(ig == 0), stop=(ig == 7))
                        if ig == 7:
                            mm.then_inc(sem["s_fmm"], 1)
                t.wait_ge(sem["d_wmov"], 32)
                for gg in range(8):
                    t.wait_ge(sem[f"d_wcol{gg % 2}"], 16 * (4 + gg // 2 + 1))
                    if gg >= 2:
                        t.wait_ge(sem["s_fev"], 8 + gg - 1)
                    pe, po = ps8[(gg % 2) * 2], ps8[(gg % 2) * 2 + 1]
                    cb = colbuf[gg % 2]
                    for ig in range(8):
                        stat = cb[:, ig * 128:(ig + 1) * 128]
                        t.matmul(pe[:, :], stat, movstage[:, ig * H:ig * H + 512],
                                 start=(ig == 0), stop=(ig == 7))
                        mm = t.matmul(po[:, :], stat, movstage[:, ig * H + 512:(ig + 1) * H],
                                      start=(ig == 0), stop=(ig == 7))
                        if ig == 7:
                            mm.then_inc(sem["s_fmm"], 1)
                t.wait_ge(sem["d_xfo"], 16 * 8)
                t.wait_ge(sem["s_fev"], 16)
                for gp in range(8):
                    if gp >= 1:
                        t.wait_ge(sem["s_qev"], 5 * gp)
                    for hg in range(8):
                        stat = wqkS[:, hg * H + gp * 128: hg * H + (gp + 1) * 128]
                        for ncx, (off, nw) in enumerate(NCHUNKS):
                            mm = t.matmul(ps8[ncx][:, 0:nw], stat,
                                          xfo[:, hg * QR + off: hg * QR + off + nw],
                                          start=(hg == 0), stop=(hg == 7))
                            if hg == 7 and ncx == 4:
                                mm.then_inc(sem["s_qmm"], 1)
                t.wait_ge(sem["s_qev"], 40)
                for gp in range(8):
                    t.wait_ge(sem["s_q2"], gp + 1)
                    for ncx, (off, nw) in enumerate(NCHUNKS):
                        mm = t.matmul(ps8[ncx][0:1, 0:nw], ones_bf[:, 0:1],
                                      q2buf[gp % 2][:, off:off + nw],
                                      start=(gp == 0), stop=(gp == 7))
                        if ncx == 4:
                            mm.then_inc(sem["s_bmm"], 1)

            @blk.vector
            def _(v):
                v.memset(ones_bf[:, :], 1.0)
                for k in range(NT):
                    p = k % 2
                    v.wait_ge(sem[f"d_ld{p}"], 16 * (k // 2 + 1))
                    ln_vec(v, k, last=(k == NT - 1))

                evq = []
                for gg in range(8):
                    def ev_wqk(v, gg=gg):
                        v.wait_ge(sem["s_fmm"], gg + 1)
                        pe, po = ps8[(gg % 2) * 2], ps8[(gg % 2) * 2 + 1]
                        v.tensor_scalar(wqkS[:, gg * H:gg * H + 512], pe[:, :],
                                        1.0 / 32.0, None, OP.mult)
                        v.tensor_scalar(wqkS[:, gg * H + 512:(gg + 1) * H], po[:, :],
                                        1.0 / 32.0, None, OP.mult).then_inc(sem["s_fev"], 1)
                    evq.append(ev_wqk)
                for gg in range(8):
                    def ev_wvo(v, gg=gg):
                        v.wait_ge(sem["s_fmm"], 8 + gg + 1)
                        if gg >= 2:
                            v.wait_ge(sem[f"d_wvost{gg % 2}"], 16 * (gg // 2))
                        pe, po = ps8[(gg % 2) * 2], ps8[(gg % 2) * 2 + 1]
                        v.tensor_copy(wvo_stage[gg % 2][:, 0:512], pe[:, :])
                        v.tensor_copy(wvo_stage[gg % 2][:, 512:H], po[:, :]).then_inc(sem["s_fev"], 1)
                    evq.append(ev_wvo)
                for gp in range(8):
                    def ev_qp(v, gp=gp):
                        v.wait_ge(sem["s_qmm"], gp + 1)
                        for ncx, (off, nw) in enumerate(NCHUNKS):
                            idx = gp * 5 + ncx
                            if idx >= 2:
                                v.wait_ge(sem[f"d_qst{idx % 2}"], 16 * (idx // 2))
                            v.tensor_copy(qstage[idx % 2][:, 0:nw],
                                          ps8[ncx][:, 0:nw]).then_inc(sem["s_qev"], 1)
                    evq.append(ev_qp)


                for k in range(NT, NLN):
                    p = k % 2
                    if evq:
                        evq.pop(0)(v)
                    v.wait_ge(sem[f"d_ld{p}"], 16 * (k // 2 + 1))
                    ln_vec(v, k, do_prev=(k > NT), last=(k == NLN - 1))
                while evq:
                    evq.pop(0)(v)
                v.wait_ge(sem["d_nb"], 16 * 18)
                v.tensor_scalar_mul(negb_cols[:, :], negb_cols[:, :], -1.0
                                    ).then_inc(sem["s_bneg"], 1)

            @blk.scalar
            def _(scl):
                for k in range(NT):
                    p = k % 2
                    scl.wait_ge(sem[f"d_ld{p}"], 16 * (k // 2 + 1))
                    ln_scal(scl, k, last=(k == NT - 1))

                popmap = {}
                for gg in range(8):
                    def ev_wvost(scl, gg=gg):
                        scl.wait_ge(sem["s_fev"], 8 + gg + 1)
                        scl.dma_start(s_wvo.ap()[:, gg * H:(gg + 1) * H],
                                      wvo_stage[gg % 2][:, :]).then_inc(sem[f"d_wvost{gg % 2}"], 16)
                    popmap[26 + gg] = ev_wvost
                for gp in range(8):
                    def ev_q2(scl, gp=gp):
                        scl.wait_ge(sem[f"d_q2l{gp % 2}"], 16 * (gp // 2 + 1))
                        scl.activation(q2buf[gp % 2][:, :], q2ld[gp % 2][:, :],
                                       AF.Square).then_inc(sem["s_q2"], 1)
                    popmap[44 + gp] = ev_q2

                def ev_negb_s(scl):
                    scl.wait_ge(sem["s_bmm"], 8)
                    for ncx, (off, nw) in enumerate(NCHUNKS):
                        scl.activation(b_row[0:1, off:off + nw], ps8[ncx][0:1, 0:nw],
                                       AF.Sqrt, scale=float(H)).then_inc(sem["s_bsq"], 1)
                    scl.wait_ge(sem["s_bsq"], 5)
                    scl.dma_start(s_negb.ap().rearrange("a p o -> o (a p)"),
                                  b_row[0:1, :]).then_inc(sem["d_nb"], 16)
                    scl.wait_ge(sem["d_nb"], 16)
                    for i in range(NT):
                        scl.dma_start(negb_cols[:, 4 * i:4 * i + 1], s_negb[i]).then_inc(sem["d_nb"], 16)
                popmap[53] = ev_negb_s

                for k in range(NT, NLN):
                    p = k % 2
                    scl.wait_ge(sem[f"d_ld{p}"], 16 * (k // 2 + 1))
                    ln_scal(scl, k, do_prev=(k > NT), last=(k == NLN - 1))
                    if k in popmap:
                        popmap.pop(k)(scl)
                for k in sorted(popmap):
                    popmap.pop(k)(scl)

        prostk.close()

        arena_stk = contextlib.ExitStack()
        arena = arena_stk.enter_context(nc.sbuf_tensor("arena", [128, 8 * S], BF16))

        if phases >= 2:
          with nc.Block() as blk:
            @blk.sync
            def _(s):
                s.wait_ge(sem["d_st0"], 16 * 41)
                s.wait_ge(sem["d_st1"], 16 * 40)
                for g in range(8):
                    s.dma_start_transpose(arena[:, g * S:(g + 1) * S],
                                          s_xt.ap()[:, g * 128:(g + 1) * 128]
                                          ).then_inc(sem["d_xf"], 16)
                for i in range(NT):
                    p = i % 2
                    if i >= 2:
                        s.wait_ge(sem["s_sc"], 4 * (i - 1))
                    s.dma_start(qpf_tile[p].rearrange("p (g s) -> p g s", s=128),
                                s_qpf.ap()[:, :, i * 128:(i + 1) * 128].transpose([1, 0, 2])
                                ).then_inc(sem[f"d_qt{p}"], 16)

            @blk.tensor
            def _(t):
                t.wait_ge(sem["d_xf"], 16 * 8)
                for i in range(NT):
                    p = i % 2
                    t.wait_ge(sem[f"d_qt{p}"], 16 * (i // 2 + 1))
                    for q in range(4):
                        gq = 4 * i + q
                        grp = ps8[(q % 2) * 4:(q % 2) * 4 + 4]
                        if gq >= 2:
                            t.wait_ge(sem["s_exp"], gq - 1)
                        for gp in range(8):
                            stat = qpf_tile[p][:, gp * 128:(gp + 1) * 128]
                            for j in range(4):
                                kb = q * 2048 + j * 512
                                mm = t.matmul(grp[j][:, :], stat,
                                              arena[:, gp * S + kb: gp * S + kb + 512],
                                              start=(gp == 0), stop=(gp == 7))
                                if gp == 7 and j == 3:
                                    mm.then_inc(sem["s_sc"], 1)

            @blk.scalar
            def _(scl):
                scl.wait_ge(sem["s_bneg"], 1)
                for i in range(NT):
                    for q in range(4):
                        gq = 4 * i + q
                        grp = ps8[(q % 2) * 4:(q % 2) * 4 + 4]
                        scl.wait_ge(sem["s_sc"], gq + 1)
                        if q == 0 and i >= 1:
                            scl.wait_ge(sem["d_pst"], 256 * i)
                            scl.wait_ge(sem["s_sv"], i)
                        for j in range(4):
                            kb = q * 2048 + j * 512
                            act = scl.activation(p_sb[:, kb:kb + 512], grp[j][:, :],
                                                 AF.Exp, bias=negb_cols[:, 4 * i:4 * i + 1])
                            if j == 3:
                                act.then_inc(sem["s_exp"], 1)
                    scl.wait_ge(sem["s_exp"], 4 * (i + 1))
                    for w in range(16):
                        scl.dma_start(
                            s_p[i, 4 * w:4 * w + 4].transpose([1, 0, 2]),
                            p_sb.rearrange("s (j t) -> s j t", t=128)[:, 4 * w:4 * w + 4, :]
                        ).then_inc(sem["d_pst"], 16)

            @blk.vector
            def _(v):
                for i in range(NT):
                    v.wait_ge(sem["s_exp"], 4 * i + 4)
                    v.reduce_sum(s_all[:, 4 * i:4 * i + 1], p_sb[:, :], axis=AX
                                 ).then_inc(sem["s_sv"], 1)

        rstk.close()
        pstk.close()

        rstk2 = contextlib.ExitStack()
        pt_sb = [rstk2.enter_context(nc.sbuf_tensor(f"pt{p}", [128, S // 2], BF16, side="right"))
                 for p in range(2)]
        wvoS = rstk2.enter_context(nc.sbuf_tensor("wvoS", [128, 8 * H], BF16, side="right"))

        pstk2 = contextlib.ExitStack()
        ctx_ps = [[pstk2.enter_context(nc.psum_tensor(f"ctx{p}{h}", [128, 512], F32))
                   for h in range(2)] for p in range(2)]
        trs_ps = pstk2.enter_context(nc.psum_tensor("trs", [128, H], BF16))
        attn_ps = [pstk2.enter_context(nc.psum_tensor(f"attn{h}", [128, 512], F32))
                   for h in range(2)]

        if phases >= 3:
          with nc.Block() as blk:
            @blk.sync
            def _(s):
                for c in range(8):
                    s.dma_start(arena[:, c * S:(c + 1) * S].rearrange("p (g f) -> p g f", f=H),
                                s_xt.rearrange("(g p) f -> p g f", p=128)[:, c * 8:(c + 1) * 8, :]
                                ).then_inc(sem["d_xt"], 16)
                s.wait_ge(sem["d_wvost0"], 16 * 4)
                s.wait_ge(sem["d_wvost1"], 16 * 4)
                s.dma_start(wvoS[:, :], s_wvo.ap()).then_inc(sem["d_wvold"], 16)
                for i in range(NT):
                    p = i % 2
                    s.wait_ge(sem["d_pst"], 256 * (i + 1))
                    for half in range(2):
                        hx = 2 * i + half
                        if hx >= 2:
                            s.wait_ge(sem["s_pvh"], hx - 1)
                        s.dma_start_transpose(
                            pt_sb[half][:, :],
                            s_p[i, 32 * half:32 * half + 32].rearrange("a b c -> (a b) c")
                        ).then_inc(sem[f"d_pt{half}"], 16)
                    if i >= 2:
                        s.wait_ge(sem["s_x1v"], i - 1)
                    s.dma_start(xstage[p][:, :], x_own[i * 128:(i + 1) * 128, :]
                                ).then_inc(sem[f"d_xo2{p}"], 16)

            @blk.tensor
            def _(t):
                t.wait_ge(sem["d_xt"], 16 * 8)
                t.wait_ge(sem["d_wvold"], 16)
                t.wait_ge(sem["d_misc"], 32)
                for i in range(NT):
                    p = i % 2
                    if i >= 2:
                        t.wait_ge(sem["s_ctx"], i - 1)
                    for half in range(2):
                        hx = 2 * i + half
                        t.wait_ge(sem[f"d_pt{half}"], 16 * (i + 1))
                        for jj in range(32):
                            j = 32 * half + jj
                            stat = pt_sb[half][:, jj * 128:(jj + 1) * 128]
                            t.matmul(ctx_ps[p][0][:, :], stat, arena[:, j * H: j * H + 512],
                                     start=(j == 0), stop=(j == 63))
                            mm = t.matmul(ctx_ps[p][1][:, :], stat,
                                          arena[:, j * H + 512:(j + 1) * H],
                                          start=(j == 0), stop=(j == 63))
                            if jj == 31:
                                mm.then_inc(sem["s_pvh"], 1)
                    t.wait_ge(sem["s_ctx"], i + 1)
                    if i >= 1:
                        t.wait_ge(sem["s_trsV"], i)
                    for c in range(8):
                        tr = t.transpose(trs_ps[:, c * 128:(c + 1) * 128],
                                         ctxX_sb[:, c * 128:(c + 1) * 128], identb[:, :])
                        if c == 7:
                            tr.then_inc(sem["s_trsT"], 1)
                    t.wait_ge(sem["s_trsV"], i + 1)
                    if i >= 1:
                        t.wait_ge(sem["s_x1v"], i)
                    for g in range(8):
                        stat = ctxXf_sb[:, g * 128:(g + 1) * 128]
                        t.matmul(attn_ps[0][:, :], stat, wvoS[:, g * H: g * H + 512],
                                 start=(g == 0), stop=(g == 7))
                        mm = t.matmul(attn_ps[1][:, :], stat, wvoS[:, g * H + 512:(g + 1) * H],
                                      start=(g == 0), stop=(g == 7))
                        if g == 7:
                            mm.then_inc(sem["s_op"], 1)

            @blk.vector
            def _(v):
                for i in range(NT):
                    v.reciprocal(sinv_all[:, 4 * i:4 * i + 1], s_all[:, 4 * i:4 * i + 1])
                for i in range(NT):
                    p = i % 2
                    k = NLN + i
                    v.wait_ge(sem["s_pvh"], 2 * (i + 1))
                    v.tensor_scalar(ctxX_sb[:, 0:512], ctx_ps[p][0][:, :],
                                    sinv_all[:, 4 * i:4 * i + 1], None, OP.mult)
                    v.tensor_scalar(ctxX_sb[:, 512:H], ctx_ps[p][1][:, :],
                                    sinv_all[:, 4 * i:4 * i + 1], None, OP.mult).then_inc(sem["s_ctx"], 1)
                    v.wait_ge(sem["s_trsT"], i + 1)
                    v.tensor_copy(ctxXf_sb[:, :], trs_ps[:, :]).then_inc(sem["s_trsV"], 1)
                    v.wait_ge(sem["s_op"], i + 1)
                    v.wait_ge(sem[f"d_xo2{p}"], 16 * (i // 2 + 1))
                    if i >= 1:
                        v.wait_ge(sem["d_x1st"], 16 * max(0, i - 1))
                        v.wait_ge(sem["s_sq"], k)
                    v.tensor_add(x1_sb[:, 0:512], attn_ps[0][:, :], xstage[p][:, 0:512])
                    v.tensor_add(x1_sb[:, 512:H], attn_ps[1][:, :],
                                 xstage[p][:, 512:H]).then_inc(sem["s_x1v"], 1)
                    ln_vec(v, k, do_prev=False, last=True)

            @blk.scalar
            def _(scl):
                for i in range(NT):
                    k = NLN + i
                    scl.wait_ge(sem["s_x1v"], i + 1)
                    if i >= 1:
                        scl.dma_start(s_x1[i], x1_sb[:, :]).then_inc(sem["d_x1st"], 16)
                    ln_scal(scl, k, do_prev=False, last=True)

        rstk2.close()
        pstk2.close()
        arena_stk.close()

        cstk = contextlib.ExitStack()
        ln1f = cstk.enter_context(nc.sbuf_tensor("ln1f", [128, 8 * QR], BF16))
        wrTS = cstk.enter_context(nc.sbuf_tensor("wrTS", [128, 8 * H], BF16))
        wkfTS = cstk.enter_context(nc.sbuf_tensor("wkfTS", [128, 8 * H], BF16))
        wvfTS = cstk.enter_context(nc.sbuf_tensor("wvfTS", [128, 8 * H], BF16))
        d_sb = cstk.enter_context(nc.sbuf_tensor("d_sb", [128, 8 * 512], BF16))
        xr_sb = cstk.enter_context(nc.sbuf_tensor("xr_sb", [128, 8 * 512], BF16))
        xk_sb = cstk.enter_context(nc.sbuf_tensor("xk_sb", [128, 8 * 512], BF16))
        k_act = cstk.enter_context(nc.sbuf_tensor("k_act", [128, 8 * 512], BF16))
        rf = cstk.enter_context(nc.sbuf_tensor("rf", [128, 8 * 512], F32))
        relu_tmp = [cstk.enter_context(nc.sbuf_tensor(f"relu{p}", [128, 512], F32)) for p in range(2)]
        ffn_sb = cstk.enter_context(nc.sbuf_tensor("ffn_sb", [128, 8 * 512], F32))

        identf = cstk.enter_context(nc.sbuf_tensor("identf", [128, 128], F32))
        mu_sb = cstk.enter_context(nc.sbuf_tensor("mu_sb", [128, 16], F32))
        psC = [cstk.enter_context(nc.psum_tensor(f"psC{i}", [128, 512], F32)) for i in range(4)]
        fout_ps = [[cstk.enter_context(nc.psum_tensor(f"fout{p}{h}", [128, 512], F32))
                    for h in range(2)] for p in range(2)]

        if phases >= 4:
          with nc.Block() as blk:
            @blk.sync
            def _(s):
                s.wait_ge(sem["d_st0"], 16 * 49)
                s.wait_ge(sem["d_st1"], 16 * 49)
                for c in range(8):
                    s.dma_start_transpose(ln1f[:, c * QR:(c + 1) * QR],
                                          s_ln1.ap()[:, c * 128:(c + 1) * 128]
                                          ).then_inc(sem["d_lnf"], 16)
                s.wait_ge(sem["d_x1st"], 16 * 16)
                for u in range(16):
                    p = u % 2
                    if u >= 2:
                        s.wait_ge(sem["s_outv"], u - 1)
                    s.dma_start(xstage[p][:, :], s_x1[u + 1]).then_inc(sem[f"d_x13{p}"], 16)

            @blk.gpsimd
            def _(g):
                for w_dram, w_sb in ((wrT, wrTS), (wkfT, wkfTS), (wvfT, wvfTS)):
                    g.dma_start(w_sb.rearrange("p (c f) -> p c f", f=H),
                                w_dram.rearrange("(c p) f -> p c f", p=128)
                                ).then_inc(sem["d_wf"], 16)
                g.dma_start(mu_sb[:, 0:8], mu_r_in[:, :]).then_inc(sem["d_wf"], 16)
                g.dma_start(mu_sb[:, 8:16], mu_k_in[:, :]).then_inc(sem["d_wf"], 16)
                g.dma_start(identf[:, :], identf_in[:, :]).then_inc(sem["d_wf"], 16)

            @blk.tensor
            def _(t):
                t.wait_ge(sem["d_wf"], 16 * 6)
                for stt in range(4):
                    t.wait_ge(sem["s_xr"], stt + 1)
                    for d in range(8):
                        dd = 8 * stt + d
                        if dd >= 2:
                            t.wait_ge(sem["s_sig"], dd - 1)
                        for c in range(8):
                            mm = t.matmul(psC[d % 2][:, :],
                                          wrTS[:, c * H + d * 128: c * H + (d + 1) * 128],
                                          xr_sb[:, c * 512:(c + 1) * 512],
                                          start=(c == 0), stop=(c == 7))
                            if c == 7:
                                mm.then_inc(sem["s_rp"], 1)
                    for d in range(8):
                        dd = 8 * stt + d
                        if dd >= 2:
                            t.wait_ge(sem["s_rel"], dd - 1)
                        for c in range(8):
                            mm = t.matmul(psC[2 + d % 2][:, :],
                                          wkfTS[:, c * H + d * 128: c * H + (d + 1) * 128],
                                          xk_sb[:, c * 512:(c + 1) * 512],
                                          start=(c == 0), stop=(c == 7))
                            if c == 7:
                                mm.then_inc(sem["s_kp"], 1)
                    t.wait_ge(sem["s_ka"], 8 * (stt + 1))
                    t.wait_ge(sem["s_sig"], 8 * (stt + 1))
                    for e in range(8):
                        ee = 8 * stt + e
                        if ee >= 2:
                            t.wait_ge(sem["s_ff"], ee - 1)
                        for dd2 in range(8):
                            mm = t.matmul(psC[e % 2][:, :],
                                          wvfTS[:, dd2 * H + e * 128: dd2 * H + (e + 1) * 128],
                                          k_act[:, dd2 * 512:(dd2 + 1) * 512],
                                          start=(dd2 == 0), stop=(dd2 == 7))
                            if dd2 == 7:
                                mm.then_inc(sem["s_vp"], 1)
                    t.wait_ge(sem["s_ff"], 8 * (stt + 1))
                    for sub in range(4):
                        u = 4 * stt + sub
                        if u >= 2:
                            t.wait_ge(sem["s_outv"], u - 1)
                        for e in range(8):
                            tr = t.transpose(fout_ps[u % 2][e // 4][:, (e % 4) * 128:(e % 4 + 1) * 128],
                                             ffn_sb[:, e * 512 + sub * 128: e * 512 + sub * 128 + 128],
                                             identf[:, :])
                            if e == 7:
                                tr.then_inc(sem["s_tr3"], 1)

            @blk.vector
            def _(v):
                v.wait_ge(sem["d_lnf"], 16 * 8)
                v.wait_ge(sem["d_misc"], 32)
                v.wait_ge(sem["d_wf"], 16 * 6)
                for c in range(8):
                    mk = v.tensor_mul(ln1f[:, c * QR + 127: c * QR + 128],
                                      ln1f[:, c * QR + 127: c * QR + 128], hmask[:, 0:1])
                    if c == 7:
                        mk.then_inc(sem["s_mask"], 1)
                for stt in range(4):
                    if stt >= 1:
                        v.wait_ge(sem["s_rp"], 8 * stt)
                        v.wait_ge(sem["s_kp"], 8 * stt)
                    for c in range(8):
                        cur = ln1f[:, c * QR + 128 + stt * 512: c * QR + 128 + stt * 512 + 512]
                        prev = ln1f[:, c * QR + 127 + stt * 512: c * QR + 127 + stt * 512 + 512]
                        v.tensor_sub(d_sb[:, c * 512:(c + 1) * 512], cur, prev)
                        v.scalar_tensor_tensor(xr_sb[:, c * 512:(c + 1) * 512],
                                               d_sb[:, c * 512:(c + 1) * 512],
                                               mu_sb[:, c:c + 1], prev, OP.mult, OP.add)
                        stx = v.scalar_tensor_tensor(xk_sb[:, c * 512:(c + 1) * 512],
                                                     d_sb[:, c * 512:(c + 1) * 512],
                                                     mu_sb[:, 8 + c:9 + c], prev, OP.mult, OP.add)
                        if c == 7:
                            stx.then_inc(sem["s_xr"], 1)
                    for d in range(8):
                        dd = 8 * stt + d
                        v.wait_ge(sem["s_rel"], dd + 1)
                        ka = v.scalar_tensor_tensor(k_act[:, d * 512:(d + 1) * 512],
                                                    relu_tmp[d % 2][:, :], 1.0,
                                                    relu_tmp[d % 2][:, :], OP.mult, OP.mult)
                        ka.then_inc(sem["s_ka"], 1)
                    for e in range(8):
                        ee = 8 * stt + e
                        v.wait_ge(sem["s_vp"], ee + 1)
                        fe = v.tensor_tensor(ffn_sb[:, e * 512:(e + 1) * 512],
                                             psC[e % 2][:, :], rf[:, e * 512:(e + 1) * 512],
                                             OP.mult)
                        fe.then_inc(sem["s_ff"], 1)
                    for sub in range(4):
                        u = 4 * stt + sub
                        p = u % 2
                        v.wait_ge(sem["s_tr3"], u + 1)
                        v.wait_ge(sem[f"d_x13{p}"], 16 * (u // 2 + 1))
                        if u >= 2:
                            v.wait_ge(sem[f"d_ost{p}"], 16 * (u // 2))
                        v.tensor_add(out_sb[p][:, 0:512], fout_ps[p][0][:, :],
                                     xstage[p][:, 0:512])
                        v.tensor_add(out_sb[p][:, 512:H], fout_ps[p][1][:, :],
                                     xstage[p][:, 512:H]).then_inc(sem["s_outv"], 1)

            @blk.scalar
            def _(scl):
                for stt in range(4):
                    for d in range(8):
                        dd = 8 * stt + d
                        scl.wait_ge(sem["s_rp"], dd + 1)
                        if d == 0 and stt >= 1:
                            scl.wait_ge(sem["s_ff"], 8 * stt)
                        scl.activation(rf[:, d * 512:(d + 1) * 512], psC[d % 2][:, :],
                                       AF.Sigmoid).then_inc(sem["s_sig"], 1)
                    for d in range(8):
                        dd = 8 * stt + d
                        scl.wait_ge(sem["s_kp"], dd + 1)
                        if dd >= 2:
                            scl.wait_ge(sem["s_ka"], dd - 1)
                        scl.activation(relu_tmp[d % 2][:, :], psC[2 + d % 2][:, :],
                                       AF.Relu).then_inc(sem["s_rel"], 1)
                    for sub in range(4):
                        u = 4 * stt + sub
                        scl.wait_ge(sem["s_outv"], u + 1)
                        scl.dma_start(o_out[u * 128:(u + 1) * 128, :],
                                      out_sb[u % 2][:, :]).then_inc(sem[f"d_ost{u % 2}"], 16)
                scl.wait_ge(sem["d_ost0"], 16 * 8)
                scl.wait_ge(sem["d_ost1"], 16 * 8)

        cstk.close()

    return nc


_NC_CACHE = [None]


def kernel(L0_attn_x, L0_attn_W_qkv, L0_attn_W_out, L0_ffn_W_r, L0_ffn_W_k,
           L0_ffn_W_v, L0_ffn_mu_r, L0_ffn_mu_k):
    x = np.ascontiguousarray(np.asarray(L0_attn_x, dtype=np.float32))
    wqkv = np.asarray(L0_attn_W_qkv, dtype=np.float32)
    wout = np.asarray(L0_attn_W_out, dtype=np.float32)
    wr = np.asarray(L0_ffn_W_r, dtype=np.float32)
    wkf = np.asarray(L0_ffn_W_k, dtype=np.float32)
    wvf = np.asarray(L0_ffn_W_v, dtype=np.float32)
    mur = np.asarray(L0_ffn_mu_r, dtype=np.float32)
    muk = np.asarray(L0_ffn_mu_k, dtype=np.float32)

    wq_h = np.ascontiguousarray(wqkv[0::3, :])
    wk_h = np.ascontiguousarray(wqkv[1::3, :])
    wv_h = np.ascontiguousarray(wqkv[2::3, :])
    woutT_h = np.ascontiguousarray(wout.T)
    wrT_h = np.ascontiguousarray(wr.T)
    wkfT_h = np.ascontiguousarray(wkf.T)
    wvfT_h = np.ascontiguousarray(wvf.T)
    mur_h = np.ascontiguousarray(mur.reshape(8, 128).T)
    muk_h = np.ascontiguousarray(muk.reshape(8, 128).T)
    identb_h = np.eye(128, dtype=np.float32).astype(ml_dtypes.bfloat16)
    identf_h = np.eye(128, dtype=np.float32)

    if _NC_CACHE[0] is None:
        _NC_CACHE[0] = build_nc()
    nc = _NC_CACHE[0]

    in_maps = []
    for c in range(8):
        b, m = c // 4, c % 4
        s0 = 2048 * m
        if m == 0:
            xo = np.concatenate([x[b, 0:128], x[b, 0:2048]], axis=0)
            hm = np.zeros((128, 1), np.float32)
        else:
            xo = x[b, s0 - 128:s0 + 2048]
            hm = np.ones((128, 1), np.float32)
        in_maps.append({
            "x_batch": x[b],
            "x_own": np.ascontiguousarray(xo),
            "wq": wq_h, "wk": wk_h, "wv": wv_h, "woutT": woutT_h,
            "wrT": wrT_h, "wkfT": wkfT_h, "wvfT": wvfT_h,
            "mu_r_in": mur_h, "mu_k_in": muk_h,
            "identb_in": identb_h, "identf_in": identf_h,
            "hmask_in": hm,
        })

    res = run_bass_kernel_spmd(nc, in_maps, core_ids=list(range(8)))
    out = np.empty((2, 8192, 1024), np.float32)
    for c in range(8):
        b, m = c // 4, c % 4
        out[b, 2048 * m:2048 * (m + 1), :] = res.results[c]["o_out"]
    return out


# version 04
# speedup vs baseline: 1.0292x; 1.0292x over previous
import contextlib
import numpy as np
import ml_dtypes
import concourse.bass as bass
import concourse.mybir as mybir
from concourse.bass_utils import run_bass_kernel_spmd

F32 = mybir.dt.float32
BF16 = mybir.dt.bfloat16
AF = mybir.ActivationFunctionType
AX = mybir.AxisListType.X
OP = mybir.AluOpType

H = 1024
S = 8192
NT = 17
NB = 64
NLN = NT + NB
QR = NT * 128
EPS = 1e-5
INV_H = 1.0 / H
NCHUNKS = [(0, 512), (512, 512), (1024, 512), (1536, 512), (2048, 128)]


def build_nc(phases=4, dbg=False):
    nc = bass.Bass("TRN2", target_bir_lowering=False, detect_race_conditions=False)

    x_batch = nc.dram_tensor("x_batch", [S, H], F32, kind="ExternalInput")
    x_own = nc.dram_tensor("x_own", [QR, H], F32, kind="ExternalInput")
    wq = nc.dram_tensor("wq", [H, H], F32, kind="ExternalInput")
    wk = nc.dram_tensor("wk", [H, H], F32, kind="ExternalInput")
    wv = nc.dram_tensor("wv", [H, H], F32, kind="ExternalInput")
    woutT = nc.dram_tensor("woutT", [H, H], F32, kind="ExternalInput")
    wrT = nc.dram_tensor("wrT", [H, H], F32, kind="ExternalInput")
    wkfT = nc.dram_tensor("wkfT", [H, H], F32, kind="ExternalInput")
    wvfT = nc.dram_tensor("wvfT", [H, H], F32, kind="ExternalInput")
    mu_r_in = nc.dram_tensor("mu_r_in", [128, 8], F32, kind="ExternalInput")
    mu_k_in = nc.dram_tensor("mu_k_in", [128, 8], F32, kind="ExternalInput")
    identb_in = nc.dram_tensor("identb_in", [128, 128], BF16, kind="ExternalInput")
    identf_in = nc.dram_tensor("identf_in", [128, 128], F32, kind="ExternalInput")
    hmask_in = nc.dram_tensor("hmask_in", [128, 1], F32, kind="ExternalInput")

    o_out = nc.dram_tensor("o_out", [16 * 128, H], F32, kind="ExternalOutput")

    KD = "ExternalOutput" if dbg else "Internal"
    s_lnown = nc.dram_tensor("s_lnown", [QR, H], BF16, kind=KD)
    s_xt = nc.dram_tensor("s_xt", [S, H], BF16, kind=KD)
    s_p = nc.dram_tensor("s_p", [NT, 64, 128, 128], BF16, kind=KD)
    s_x1 = nc.dram_tensor("s_x1", [NT, 128, H], F32, kind=KD)
    s_ln1 = nc.dram_tensor("s_ln1", [QR, H], BF16, kind=KD)
    s_negb = nc.dram_tensor("s_negb", [NT, 128, 1], F32, kind=KD)
    s_qpf = nc.dram_tensor("s_qpf", [8, 128, QR], BF16, kind=KD)
    s_wvo = nc.dram_tensor("s_wvo", [128, 8 * H], BF16, kind="Internal")

    with contextlib.ExitStack() as st:
        xstage = [st.enter_context(nc.sbuf_tensor(f"xstage{p}", [128, H], F32)) for p in range(2)]
        dst_sb = [st.enter_context(nc.sbuf_tensor(f"dstsb{p}", [128, H], BF16)) for p in range(2)]
        scrf32 = [st.enter_context(nc.sbuf_tensor(f"scrf32{p}", [128, H], F32)) for p in range(2)]
        sq_dummy, x1_sb = scrf32[0], scrf32[1]
        out_sb = scrf32
        ctxX_sb = st.enter_context(nc.sbuf_tensor("ctxX", [128, H], BF16))
        ctxXf_sb = st.enter_context(nc.sbuf_tensor("ctxXf", [128, H], BF16))
        identb = st.enter_context(nc.sbuf_tensor("identb", [128, 128], BF16))
        ones_bf = st.enter_context(nc.sbuf_tensor("ones_bf", [128, 1], BF16))
        negb_cols = st.enter_context(nc.sbuf_tensor("negb_cols", [128, 4 * NT], F32))
        s_all = st.enter_context(nc.sbuf_tensor("s_all", [128, 4 * NT], F32))
        sinv_all = st.enter_context(nc.sbuf_tensor("sinv_all", [128, 4 * NT], F32))
        hmask = st.enter_context(nc.sbuf_tensor("hmask", [128, 1], F32))
        c_ssq = st.enter_context(nc.sbuf_tensor("c_ssq", [128, 8], F32))
        c_ngs = st.enter_context(nc.sbuf_tensor("c_ngs", [128, 8], F32))
        c_nm = st.enter_context(nc.sbuf_tensor("c_nm", [128, 8], F32))
        c_t1 = st.enter_context(nc.sbuf_tensor("c_t1", [128, 8], F32))
        c_m2 = st.enter_context(nc.sbuf_tensor("c_m2", [128, 8], F32))
        c_vpe = st.enter_context(nc.sbuf_tensor("c_vpe", [128, 8], F32))
        c_rin = st.enter_context(nc.sbuf_tensor("c_rin", [128, 8], F32))
        c_rsd = st.enter_context(nc.sbuf_tensor("c_rsd", [128, 8], F32))
        b_row = st.enter_context(nc.sbuf_tensor("b_row", [1, QR], F32))

        sem_names = ["d_ld0", "d_ld1", "d_st0", "d_st1", "d_wmov", "d_wcol0",
                     "d_wcol1", "d_misc", "d_wf", "d_xfo", "d_xf", "d_xt",
                     "d_nb", "d_wvost0", "d_wvost1", "d_wvold", "d_pst",
                     "d_pt0", "d_pt1", "d_xo20", "d_xo21", "d_x1st", "d_lnf",
                     "d_x130", "d_x131", "d_ost0", "d_ost1", "d_qst0", "d_qst1",
                     "d_q2l0", "d_q2l1", "d_qt0", "d_qt1", "s_pvh",
                     "s_sq", "s_var", "s_rstd", "s_app",
                     "s_fmm", "s_fev", "s_qmm", "s_qev", "s_q2", "s_bmm",
                     "s_bsq", "s_bneg", "s_sc", "s_exp", "s_sv",
                     "s_pv", "s_ctx", "s_trsT", "s_trsV", "s_op", "s_x1v",
                     "s_mask", "s_xr", "s_rp", "s_sig", "s_kp", "s_rel",
                     "s_ka", "s_vp", "s_ff", "s_tr3", "s_outv"]
        sem = {n: st.enter_context(nc.semaphore(n)) for n in sem_names}

        def ln_src(k):
            return x1_sb if k >= NLN else xstage[k % 2]

        def ln_dst_dram(k):
            if k < NT:
                return s_lnown[k * 128:(k + 1) * 128, :]
            if k < NLN:
                kb = k - NT
                return s_xt[kb * 128:(kb + 1) * 128, :]
            kc = k - NLN
            return s_ln1[kc * 128:(kc + 1) * 128, :]

        def ln_vec_apply(v, j):
            pj = j % 2
            q = 4 * pj
            v.wait_ge(sem["s_rstd"], j + 1)
            if j >= 2:
                v.wait_ge(sem[f"d_st{pj}"], 16 * (j // 2))
            v.tensor_scalar(dst_sb[pj][:, :], ln_src(j)[:, :],
                            c_nm[:, q:q + 1], c_rsd[:, q:q + 1],
                            OP.add, OP.mult).then_inc(sem["s_app"], 1)

        def ln_vec(v, k, do_prev=True, last=False):
            q = 4 * (k % 2)
            v.reduce_sum(c_ngs[:, q:q + 1], ln_src(k)[:, :], axis=AX, negate=True)
            v.wait_ge(sem["s_sq"], k + 1)
            v.tensor_scalar(c_t1[:, q:q + 1], c_ssq[:, q:q + 1], INV_H, EPS, OP.mult, OP.add)
            v.tensor_scalar_mul(c_nm[:, q:q + 1], c_ngs[:, q:q + 1], INV_H)
            v.tensor_copy(c_ngs[:, 2:3], c_t1[:, q:q + 1])
            v.tensor_mul(c_m2[:, q:q + 1], c_nm[:, q:q + 1], c_nm[:, q:q + 1])
            v.tensor_copy(c_ngs[:, 3:4], c_t1[:, q:q + 1])
            v.tensor_sub(c_vpe[:, q:q + 1], c_t1[:, q:q + 1], c_m2[:, q:q + 1])
            v.tensor_copy(c_ngs[:, 6:7], c_t1[:, q:q + 1])
            v.reciprocal(c_rin[:, q:q + 1], c_vpe[:, q:q + 1]).then_inc(sem["s_var"], 1)
            if do_prev and k >= 1:
                ln_vec_apply(v, k - 1)
            if last:
                ln_vec_apply(v, k)

        def ln_scal_sqrt_store(scl, j):
            pj = j % 2
            q = 4 * pj
            scl.wait_ge(sem["s_var"], j + 1)
            scl.activation(c_rsd[:, q:q + 1], c_rin[:, q:q + 1],
                           AF.Sqrt).then_inc(sem["s_rstd"], 1)
            scl.wait_ge(sem["s_app"], j + 1)
            scl.dma_start(ln_dst_dram(j), dst_sb[pj][:, :]).then_inc(sem[f"d_st{pj}"], 16)

        def ln_scal(scl, k, do_prev=True, last=False):
            q = 4 * (k % 2)
            scl.activation(sq_dummy[:, :], ln_src(k)[:, :], AF.Square,
                           accum_out=c_ssq[:, q:q + 1]).then_inc(sem["s_sq"], 1)
            if do_prev and k >= 1:
                ln_scal_sqrt_store(scl, k - 1)
            if last:
                ln_scal_sqrt_store(scl, k)

        rstk = contextlib.ExitStack()
        qpf_tile = [rstk.enter_context(nc.sbuf_tensor(f"qpt{p}", [128, 8 * 128], BF16, side="right"))
                    for p in range(2)]
        p_sb = rstk.enter_context(nc.sbuf_tensor("p_sb", [128, S], BF16, side="right"))

        pstk = contextlib.ExitStack()
        ps8 = [pstk.enter_context(nc.psum_tensor(f"ps8_{i}", [128, 512], F32)) for i in range(8)]

        prostk = contextlib.ExitStack()
        wqkS = prostk.enter_context(nc.sbuf_tensor("wqkS", [128, 8 * H], BF16))
        movstage = prostk.enter_context(nc.sbuf_tensor("movstage", [128, 8 * H], BF16))
        colbuf = [prostk.enter_context(nc.sbuf_tensor(f"colbuf{p}", [128, 8 * 128], BF16)) for p in range(2)]
        wvo_stage = [prostk.enter_context(nc.sbuf_tensor(f"wvostage{p}", [128, H], BF16)) for p in range(2)]
        xfo = prostk.enter_context(nc.sbuf_tensor("xfo", [128, 8 * QR], BF16))
        q2buf = [prostk.enter_context(nc.sbuf_tensor(f"q2buf{p}", [128, QR], BF16)) for p in range(2)]
        qstage = [prostk.enter_context(nc.sbuf_tensor(f"qstage{p}", [128, 512], BF16)) for p in range(2)]
        q2ld = [prostk.enter_context(nc.sbuf_tensor(f"q2ld{p}", [128, QR], BF16)) for p in range(2)]

        with nc.Block() as blk:
            @blk.sync
            def _(s):
                for k in range(NLN):
                    p = k % 2
                    if k >= 2:
                        s.wait_ge(sem["s_app"], k - 1)
                    if k < NT:
                        src = x_own[k * 128:(k + 1) * 128, :]
                    else:
                        kb = k - NT
                        src = x_batch[kb * 128:(kb + 1) * 128, :]
                    s.dma_start(xstage[p][:, :], src).then_inc(sem[f"d_ld{p}"], 16)
                    if k == NT - 1:
                        s.wait_ge(sem["d_st0"], 16 * 9)
                        s.wait_ge(sem["d_st1"], 16 * 8)
                        for g in range(8):
                            s.dma_start_transpose(
                                xfo[:, g * QR:(g + 1) * QR],
                                s_lnown.ap()[:, g * 128:(g + 1) * 128]
                            ).then_inc(sem["d_xfo"], 16)

            @blk.gpsimd
            def _(g):
                g.dma_start(movstage.rearrange("p (c f) -> p c f", f=H),
                            wk.rearrange("(c p) f -> p c f", p=128)).then_inc(sem["d_wmov"], 16)
                for gg in range(8):
                    if gg >= 2:
                        g.wait_ge(sem["s_fmm"], gg - 1)
                    g.dma_start(colbuf[gg % 2].rearrange("p (c f) -> p c f", f=128),
                                wq.rearrange("(c p) f -> p c f", p=128)[:, :, gg * 128:(gg + 1) * 128]
                                ).then_inc(sem[f"d_wcol{gg % 2}"], 16)
                g.dma_start(identb[:, :], identb_in[:, :]).then_inc(sem["d_misc"], 16)
                g.dma_start(hmask[:, :], hmask_in[:, :]).then_inc(sem["d_misc"], 16)
                g.wait_ge(sem["s_fmm"], 8)
                g.dma_start(movstage.rearrange("p (c f) -> p c f", f=H),
                            woutT.rearrange("(c p) f -> p c f", p=128)).then_inc(sem["d_wmov"], 16)
                for gg in range(8):
                    g.wait_ge(sem["s_fmm"], 8 if gg < 2 else 8 + gg - 1)
                    g.dma_start(colbuf[gg % 2].rearrange("p (c f) -> p c f", f=128),
                                wv.rearrange("(c p) f -> p c f", p=128)[:, :, gg * 128:(gg + 1) * 128]
                                ).then_inc(sem[f"d_wcol{gg % 2}"], 16)
                for gp in range(8):
                    for ncx, (off, nw) in enumerate(NCHUNKS):
                        idx = gp * 5 + ncx
                        g.wait_ge(sem["s_qev"], idx + 1)
                        g.dma_start(s_qpf[gp][:, off:off + nw],
                                    qstage[idx % 2][:, 0:nw]).then_inc(sem[f"d_qst{idx % 2}"], 16)
                g.wait_ge(sem["d_qst0"], 16 * 20)
                g.wait_ge(sem["d_qst1"], 16 * 20)
                for gp in range(8):
                    if gp >= 2:
                        g.wait_ge(sem["s_q2"], gp - 1)
                    g.dma_start(q2ld[gp % 2][:, :], s_qpf[gp]).then_inc(sem[f"d_q2l{gp % 2}"], 16)

            @blk.tensor
            def _(t):
                t.wait_ge(sem["d_wmov"], 16)
                for gg in range(8):
                    t.wait_ge(sem[f"d_wcol{gg % 2}"], 16 * (gg // 2 + 1))
                    if gg >= 2:
                        t.wait_ge(sem["s_fev"], gg - 1)
                    pe, po = ps8[(gg % 2) * 2], ps8[(gg % 2) * 2 + 1]
                    cb = colbuf[gg % 2]
                    for ig in range(8):
                        stat = cb[:, ig * 128:(ig + 1) * 128]
                        t.matmul(pe[:, :], stat, movstage[:, ig * H:ig * H + 512],
                                 start=(ig == 0), stop=(ig == 7))
                        mm = t.matmul(po[:, :], stat, movstage[:, ig * H + 512:(ig + 1) * H],
                                      start=(ig == 0), stop=(ig == 7))
                        if ig == 7:
                            mm.then_inc(sem["s_fmm"], 1)
                t.wait_ge(sem["d_wmov"], 32)
                for gg in range(8):
                    t.wait_ge(sem[f"d_wcol{gg % 2}"], 16 * (4 + gg // 2 + 1))
                    if gg >= 2:
                        t.wait_ge(sem["s_fev"], 8 + gg - 1)
                    pe, po = ps8[(gg % 2) * 2], ps8[(gg % 2) * 2 + 1]
                    cb = colbuf[gg % 2]
                    for ig in range(8):
                        stat = cb[:, ig * 128:(ig + 1) * 128]
                        t.matmul(pe[:, :], stat, movstage[:, ig * H:ig * H + 512],
                                 start=(ig == 0), stop=(ig == 7))
                        mm = t.matmul(po[:, :], stat, movstage[:, ig * H + 512:(ig + 1) * H],
                                      start=(ig == 0), stop=(ig == 7))
                        if ig == 7:
                            mm.then_inc(sem["s_fmm"], 1)
                t.wait_ge(sem["d_xfo"], 16 * 8)
                t.wait_ge(sem["s_fev"], 16)
                for gp in range(8):
                    if gp >= 1:
                        t.wait_ge(sem["s_qev"], 5 * gp)
                    for hg in range(8):
                        stat = wqkS[:, hg * H + gp * 128: hg * H + (gp + 1) * 128]
                        for ncx, (off, nw) in enumerate(NCHUNKS):
                            mm = t.matmul(ps8[ncx][:, 0:nw], stat,
                                          xfo[:, hg * QR + off: hg * QR + off + nw],
                                          start=(hg == 0), stop=(hg == 7))
                            if hg == 7 and ncx == 4:
                                mm.then_inc(sem["s_qmm"], 1)
                t.wait_ge(sem["s_qev"], 40)
                for gp in range(8):
                    t.wait_ge(sem["s_q2"], gp + 1)
                    for ncx, (off, nw) in enumerate(NCHUNKS):
                        mm = t.matmul(ps8[ncx][0:1, 0:nw], ones_bf[:, 0:1],
                                      q2buf[gp % 2][:, off:off + nw],
                                      start=(gp == 0), stop=(gp == 7))
                        if ncx == 4:
                            mm.then_inc(sem["s_bmm"], 1)

            @blk.vector
            def _(v):
                v.memset(ones_bf[:, :], 1.0)
                v.memset(negb_cols[:, :], 0.0)
                for k in range(NT):
                    p = k % 2
                    v.wait_ge(sem[f"d_ld{p}"], 16 * (k // 2 + 1))
                    ln_vec(v, k, last=(k == NT - 1))

                evq = []
                for gg in range(8):
                    def ev_wqk(v, gg=gg):
                        v.wait_ge(sem["s_fmm"], gg + 1)
                        pe, po = ps8[(gg % 2) * 2], ps8[(gg % 2) * 2 + 1]
                        v.tensor_scalar(wqkS[:, gg * H:gg * H + 512], pe[:, :],
                                        1.0 / 32.0, None, OP.mult)
                        v.tensor_scalar(wqkS[:, gg * H + 512:(gg + 1) * H], po[:, :],
                                        1.0 / 32.0, None, OP.mult).then_inc(sem["s_fev"], 1)
                    evq.append(ev_wqk)
                for gg in range(8):
                    def ev_wvo(v, gg=gg):
                        v.wait_ge(sem["s_fmm"], 8 + gg + 1)
                        if gg >= 2:
                            v.wait_ge(sem[f"d_wvost{gg % 2}"], 16 * (gg // 2))
                        pe, po = ps8[(gg % 2) * 2], ps8[(gg % 2) * 2 + 1]
                        v.tensor_copy(wvo_stage[gg % 2][:, 0:512], pe[:, :])
                        v.tensor_copy(wvo_stage[gg % 2][:, 512:H], po[:, :]).then_inc(sem["s_fev"], 1)
                    evq.append(ev_wvo)
                for gp in range(8):
                    def ev_qp(v, gp=gp):
                        v.wait_ge(sem["s_qmm"], gp + 1)
                        for ncx, (off, nw) in enumerate(NCHUNKS):
                            idx = gp * 5 + ncx
                            if idx >= 2:
                                v.wait_ge(sem[f"d_qst{idx % 2}"], 16 * (idx // 2))
                            v.tensor_copy(qstage[idx % 2][:, 0:nw],
                                          ps8[ncx][:, 0:nw]).then_inc(sem["s_qev"], 1)
                    evq.append(ev_qp)


                for k in range(NT, NLN):
                    p = k % 2
                    if evq:
                        evq.pop(0)(v)
                    v.wait_ge(sem[f"d_ld{p}"], 16 * (k // 2 + 1))
                    ln_vec(v, k, do_prev=(k > NT), last=(k == NLN - 1))
                while evq:
                    evq.pop(0)(v)
                v.wait_ge(sem["d_nb"], 16 * 18)
                v.tensor_scalar_mul(negb_cols[:, :], negb_cols[:, :], -1.0
                                    ).then_inc(sem["s_bneg"], 1)

            @blk.scalar
            def _(scl):
                for k in range(NT):
                    p = k % 2
                    scl.wait_ge(sem[f"d_ld{p}"], 16 * (k // 2 + 1))
                    ln_scal(scl, k, last=(k == NT - 1))

                popmap = {}
                for gg in range(8):
                    def ev_wvost(scl, gg=gg):
                        scl.wait_ge(sem["s_fev"], 8 + gg + 1)
                        scl.dma_start(s_wvo.ap()[:, gg * H:(gg + 1) * H],
                                      wvo_stage[gg % 2][:, :]).then_inc(sem[f"d_wvost{gg % 2}"], 16)
                    popmap[26 + gg] = ev_wvost
                for gp in range(8):
                    def ev_q2(scl, gp=gp):
                        scl.wait_ge(sem[f"d_q2l{gp % 2}"], 16 * (gp // 2 + 1))
                        scl.activation(q2buf[gp % 2][:, :], q2ld[gp % 2][:, :],
                                       AF.Square).then_inc(sem["s_q2"], 1)
                    popmap[44 + gp] = ev_q2

                def ev_negb_s(scl):
                    scl.wait_ge(sem["s_bmm"], 8)
                    for ncx, (off, nw) in enumerate(NCHUNKS):
                        scl.activation(b_row[0:1, off:off + nw], ps8[ncx][0:1, 0:nw],
                                       AF.Sqrt, scale=float(H)).then_inc(sem["s_bsq"], 1)
                    scl.wait_ge(sem["s_bsq"], 5)
                    scl.dma_start(s_negb.ap().rearrange("a p o -> o (a p)"),
                                  b_row[0:1, :]).then_inc(sem["d_nb"], 16)
                    scl.wait_ge(sem["d_nb"], 16)
                    for i in range(NT):
                        scl.dma_start(negb_cols[:, 4 * i:4 * i + 1], s_negb[i]).then_inc(sem["d_nb"], 16)
                popmap[53] = ev_negb_s

                for k in range(NT, NLN):
                    p = k % 2
                    scl.wait_ge(sem[f"d_ld{p}"], 16 * (k // 2 + 1))
                    ln_scal(scl, k, do_prev=(k > NT), last=(k == NLN - 1))
                    if k in popmap:
                        popmap.pop(k)(scl)
                for k in sorted(popmap):
                    popmap.pop(k)(scl)

        prostk.close()

        arena_stk = contextlib.ExitStack()
        arena = arena_stk.enter_context(nc.sbuf_tensor("arena", [128, 8 * S], BF16))

        if phases >= 2:
          with nc.Block() as blk:
            @blk.sync
            def _(s):
                s.wait_ge(sem["d_st0"], 16 * 41)
                s.wait_ge(sem["d_st1"], 16 * 40)
                for g in range(8):
                    s.dma_start_transpose(arena[:, g * S:(g + 1) * S],
                                          s_xt.ap()[:, g * 128:(g + 1) * 128]
                                          ).then_inc(sem["d_xf"], 16)
                for i in range(NT):
                    p = i % 2
                    if i >= 2:
                        s.wait_ge(sem["s_sc"], 4 * (i - 1))
                    s.dma_start(qpf_tile[p].rearrange("p (g s) -> p g s", s=128),
                                s_qpf.ap()[:, :, i * 128:(i + 1) * 128].transpose([1, 0, 2])
                                ).then_inc(sem[f"d_qt{p}"], 16)

            @blk.tensor
            def _(t):
                t.wait_ge(sem["d_xf"], 16 * 8)
                for i in range(NT):
                    p = i % 2
                    t.wait_ge(sem[f"d_qt{p}"], 16 * (i // 2 + 1))
                    for q in range(4):
                        gq = 4 * i + q
                        grp = ps8[(q % 2) * 4:(q % 2) * 4 + 4]
                        if gq >= 2:
                            t.wait_ge(sem["s_exp"], gq - 1)
                        for gp in range(8):
                            stat = qpf_tile[p][:, gp * 128:(gp + 1) * 128]
                            for j in range(4):
                                kb = q * 2048 + j * 512
                                mm = t.matmul(grp[j][:, :], stat,
                                              arena[:, gp * S + kb: gp * S + kb + 512],
                                              start=(gp == 0), stop=(gp == 7))
                                if gp == 7 and j == 3:
                                    mm.then_inc(sem["s_sc"], 1)

            @blk.scalar
            def _(scl):
                scl.wait_ge(sem["s_bneg"], 1)
                for i in range(NT):
                    for q in range(4):
                        gq = 4 * i + q
                        grp = ps8[(q % 2) * 4:(q % 2) * 4 + 4]
                        scl.wait_ge(sem["s_sc"], gq + 1)
                        if q == 0 and i >= 1:
                            scl.wait_ge(sem["d_pst"], 256 * i)
                            scl.wait_ge(sem["s_sv"], i)
                        for j in range(4):
                            kb = q * 2048 + j * 512
                            act = scl.activation(p_sb[:, kb:kb + 512], grp[j][:, :],
                                                 AF.Exp, bias=negb_cols[:, 4 * i:4 * i + 1])
                            if j == 3:
                                act.then_inc(sem["s_exp"], 1)
                    scl.wait_ge(sem["s_exp"], 4 * (i + 1))
                    for w in range(16):
                        scl.dma_start(
                            s_p[i, 4 * w:4 * w + 4].transpose([1, 0, 2]),
                            p_sb.rearrange("s (j t) -> s j t", t=128)[:, 4 * w:4 * w + 4, :]
                        ).then_inc(sem["d_pst"], 16)

            @blk.vector
            def _(v):
                for i in range(NT):
                    v.wait_ge(sem["s_exp"], 4 * i + 4)
                    v.reduce_sum(s_all[:, 4 * i:4 * i + 1], p_sb[:, :], axis=AX
                                 ).then_inc(sem["s_sv"], 1)

        rstk.close()
        pstk.close()

        rstk2 = contextlib.ExitStack()
        pt_sb = [rstk2.enter_context(nc.sbuf_tensor(f"pt{p}", [128, S // 2], BF16, side="right"))
                 for p in range(2)]
        wvoS = rstk2.enter_context(nc.sbuf_tensor("wvoS", [128, 8 * H], BF16, side="right"))

        pstk2 = contextlib.ExitStack()
        ctx_ps = [[pstk2.enter_context(nc.psum_tensor(f"ctx{p}{h}", [128, 512], F32))
                   for h in range(2)] for p in range(2)]
        trs_ps = pstk2.enter_context(nc.psum_tensor("trs", [128, H], BF16))
        attn_ps = [pstk2.enter_context(nc.psum_tensor(f"attn{h}", [128, 512], F32))
                   for h in range(2)]

        if phases >= 3:
          with nc.Block() as blk:
            @blk.sync
            def _(s):
                for c in range(8):
                    s.dma_start(arena[:, c * S:(c + 1) * S].rearrange("p (g f) -> p g f", f=H),
                                s_xt.rearrange("(g p) f -> p g f", p=128)[:, c * 8:(c + 1) * 8, :]
                                ).then_inc(sem["d_xt"], 16)
                s.wait_ge(sem["d_wvost0"], 16 * 4)
                s.wait_ge(sem["d_wvost1"], 16 * 4)
                s.dma_start(wvoS[:, :], s_wvo.ap()).then_inc(sem["d_wvold"], 16)
                for i in range(NT):
                    p = i % 2
                    s.wait_ge(sem["d_pst"], 256 * (i + 1))
                    for half in range(2):
                        hx = 2 * i + half
                        if hx >= 2:
                            s.wait_ge(sem["s_pvh"], hx - 1)
                        s.dma_start_transpose(
                            pt_sb[half][:, :],
                            s_p[i, 32 * half:32 * half + 32].rearrange("a b c -> (a b) c")
                        ).then_inc(sem[f"d_pt{half}"], 16)
                    if i >= 2:
                        s.wait_ge(sem["s_x1v"], i - 1)
                    s.dma_start(xstage[p][:, :], x_own[i * 128:(i + 1) * 128, :]
                                ).then_inc(sem[f"d_xo2{p}"], 16)

            @blk.tensor
            def _(t):
                t.wait_ge(sem["d_xt"], 16 * 8)
                t.wait_ge(sem["d_wvold"], 16)
                t.wait_ge(sem["d_misc"], 32)
                for i in range(NT):
                    p = i % 2
                    if i >= 2:
                        t.wait_ge(sem["s_ctx"], i - 1)
                    for half in range(2):
                        hx = 2 * i + half
                        t.wait_ge(sem[f"d_pt{half}"], 16 * (i + 1))
                        for jj in range(32):
                            j = 32 * half + jj
                            stat = pt_sb[half][:, jj * 128:(jj + 1) * 128]
                            t.matmul(ctx_ps[p][0][:, :], stat, arena[:, j * H: j * H + 512],
                                     start=(j == 0), stop=(j == 63))
                            mm = t.matmul(ctx_ps[p][1][:, :], stat,
                                          arena[:, j * H + 512:(j + 1) * H],
                                          start=(j == 0), stop=(j == 63))
                            if jj == 31:
                                mm.then_inc(sem["s_pvh"], 1)
                    t.wait_ge(sem["s_ctx"], i + 1)
                    if i >= 1:
                        t.wait_ge(sem["s_trsV"], i)
                    for c in range(8):
                        tr = t.transpose(trs_ps[:, c * 128:(c + 1) * 128],
                                         ctxX_sb[:, c * 128:(c + 1) * 128], identb[:, :])
                        if c == 7:
                            tr.then_inc(sem["s_trsT"], 1)
                    t.wait_ge(sem["s_trsV"], i + 1)
                    if i >= 1:
                        t.wait_ge(sem["s_x1v"], i)
                    for g in range(8):
                        stat = ctxXf_sb[:, g * 128:(g + 1) * 128]
                        t.matmul(attn_ps[0][:, :], stat, wvoS[:, g * H: g * H + 512],
                                 start=(g == 0), stop=(g == 7))
                        mm = t.matmul(attn_ps[1][:, :], stat, wvoS[:, g * H + 512:(g + 1) * H],
                                      start=(g == 0), stop=(g == 7))
                        if g == 7:
                            mm.then_inc(sem["s_op"], 1)

            @blk.vector
            def _(v):
                for i in range(NT):
                    v.reciprocal(sinv_all[:, 4 * i:4 * i + 1], s_all[:, 4 * i:4 * i + 1])
                for i in range(NT):
                    p = i % 2
                    k = NLN + i
                    v.wait_ge(sem["s_pvh"], 2 * (i + 1))
                    v.tensor_scalar(ctxX_sb[:, 0:512], ctx_ps[p][0][:, :],
                                    sinv_all[:, 4 * i:4 * i + 1], None, OP.mult)
                    v.tensor_scalar(ctxX_sb[:, 512:H], ctx_ps[p][1][:, :],
                                    sinv_all[:, 4 * i:4 * i + 1], None, OP.mult).then_inc(sem["s_ctx"], 1)
                    v.wait_ge(sem["s_trsT"], i + 1)
                    v.tensor_copy(ctxXf_sb[:, :], trs_ps[:, :]).then_inc(sem["s_trsV"], 1)
                    v.wait_ge(sem["s_op"], i + 1)
                    v.wait_ge(sem[f"d_xo2{p}"], 16 * (i // 2 + 1))
                    if i >= 1:
                        v.wait_ge(sem["d_x1st"], 16 * max(0, i - 1))
                        v.wait_ge(sem["s_sq"], k)
                    v.tensor_add(x1_sb[:, 0:512], attn_ps[0][:, :], xstage[p][:, 0:512])
                    v.tensor_add(x1_sb[:, 512:H], attn_ps[1][:, :],
                                 xstage[p][:, 512:H]).then_inc(sem["s_x1v"], 1)
                    ln_vec(v, k, do_prev=False, last=True)

            @blk.scalar
            def _(scl):
                for i in range(NT):
                    k = NLN + i
                    scl.wait_ge(sem["s_x1v"], i + 1)
                    if i >= 1:
                        scl.dma_start(s_x1[i], x1_sb[:, :]).then_inc(sem["d_x1st"], 16)
                    ln_scal(scl, k, do_prev=False, last=True)

        rstk2.close()
        pstk2.close()
        arena_stk.close()

        cstk = contextlib.ExitStack()
        ln1f = cstk.enter_context(nc.sbuf_tensor("ln1f", [128, 8 * QR], BF16))
        wrTS = cstk.enter_context(nc.sbuf_tensor("wrTS", [128, 8 * H], BF16))
        wkfTS = cstk.enter_context(nc.sbuf_tensor("wkfTS", [128, 8 * H], BF16))
        wvfTS = cstk.enter_context(nc.sbuf_tensor("wvfTS", [128, 8 * H], BF16))
        d_sb = cstk.enter_context(nc.sbuf_tensor("d_sb", [128, 8 * 512], BF16))
        xr_sb = cstk.enter_context(nc.sbuf_tensor("xr_sb", [128, 8 * 512], BF16))
        xk_sb = cstk.enter_context(nc.sbuf_tensor("xk_sb", [128, 8 * 512], BF16))
        k_act = cstk.enter_context(nc.sbuf_tensor("k_act", [128, 8 * 512], BF16))
        rf = cstk.enter_context(nc.sbuf_tensor("rf", [128, 8 * 512], F32))
        relu_tmp = [cstk.enter_context(nc.sbuf_tensor(f"relu{p}", [128, 512], F32)) for p in range(2)]
        ffn_sb = cstk.enter_context(nc.sbuf_tensor("ffn_sb", [128, 8 * 512], F32))

        identf = cstk.enter_context(nc.sbuf_tensor("identf", [128, 128], F32))
        mu_sb = cstk.enter_context(nc.sbuf_tensor("mu_sb", [128, 16], F32))
        psC = [cstk.enter_context(nc.psum_tensor(f"psC{i}", [128, 512], F32)) for i in range(4)]
        fout_ps = [[cstk.enter_context(nc.psum_tensor(f"fout{p}{h}", [128, 512], F32))
                    for h in range(2)] for p in range(2)]

        if phases >= 4:
          with nc.Block() as blk:
            @blk.sync
            def _(s):
                s.wait_ge(sem["d_st0"], 16 * 49)
                s.wait_ge(sem["d_st1"], 16 * 49)
                for c in range(8):
                    s.dma_start_transpose(ln1f[:, c * QR:(c + 1) * QR],
                                          s_ln1.ap()[:, c * 128:(c + 1) * 128]
                                          ).then_inc(sem["d_lnf"], 16)
                s.wait_ge(sem["d_x1st"], 16 * 16)
                for u in range(16):
                    p = u % 2
                    if u >= 2:
                        s.wait_ge(sem["s_outv"], u - 1)
                    s.dma_start(xstage[p][:, :], s_x1[u + 1]).then_inc(sem[f"d_x13{p}"], 16)

            @blk.gpsimd
            def _(g):
                for w_dram, w_sb in ((wrT, wrTS), (wkfT, wkfTS), (wvfT, wvfTS)):
                    g.dma_start(w_sb.rearrange("p (c f) -> p c f", f=H),
                                w_dram.rearrange("(c p) f -> p c f", p=128)
                                ).then_inc(sem["d_wf"], 16)
                g.dma_start(mu_sb[:, 0:8], mu_r_in[:, :]).then_inc(sem["d_wf"], 16)
                g.dma_start(mu_sb[:, 8:16], mu_k_in[:, :]).then_inc(sem["d_wf"], 16)
                g.dma_start(identf[:, :], identf_in[:, :]).then_inc(sem["d_wf"], 16)

            @blk.tensor
            def _(t):
                t.wait_ge(sem["d_wf"], 16 * 6)
                for stt in range(4):
                    t.wait_ge(sem["s_xr"], stt + 1)
                    for d in range(8):
                        dd = 8 * stt + d
                        if dd >= 2:
                            t.wait_ge(sem["s_sig"], dd - 1)
                        for c in range(8):
                            mm = t.matmul(psC[d % 2][:, :],
                                          wrTS[:, c * H + d * 128: c * H + (d + 1) * 128],
                                          xr_sb[:, c * 512:(c + 1) * 512],
                                          start=(c == 0), stop=(c == 7))
                            if c == 7:
                                mm.then_inc(sem["s_rp"], 1)
                    for d in range(8):
                        dd = 8 * stt + d
                        if dd >= 2:
                            t.wait_ge(sem["s_rel"], dd - 1)
                        for c in range(8):
                            mm = t.matmul(psC[2 + d % 2][:, :],
                                          wkfTS[:, c * H + d * 128: c * H + (d + 1) * 128],
                                          xk_sb[:, c * 512:(c + 1) * 512],
                                          start=(c == 0), stop=(c == 7))
                            if c == 7:
                                mm.then_inc(sem["s_kp"], 1)
                    t.wait_ge(sem["s_ka"], 8 * (stt + 1))
                    t.wait_ge(sem["s_sig"], 8 * (stt + 1))
                    for e in range(8):
                        ee = 8 * stt + e
                        if ee >= 2:
                            t.wait_ge(sem["s_ff"], ee - 1)
                        for dd2 in range(8):
                            mm = t.matmul(psC[e % 2][:, :],
                                          wvfTS[:, dd2 * H + e * 128: dd2 * H + (e + 1) * 128],
                                          k_act[:, dd2 * 512:(dd2 + 1) * 512],
                                          start=(dd2 == 0), stop=(dd2 == 7))
                            if dd2 == 7:
                                mm.then_inc(sem["s_vp"], 1)
                    t.wait_ge(sem["s_ff"], 8 * (stt + 1))
                    for sub in range(4):
                        u = 4 * stt + sub
                        if u >= 2:
                            t.wait_ge(sem["s_outv"], u - 1)
                        for e in range(8):
                            tr = t.transpose(fout_ps[u % 2][e // 4][:, (e % 4) * 128:(e % 4 + 1) * 128],
                                             ffn_sb[:, e * 512 + sub * 128: e * 512 + sub * 128 + 128],
                                             identf[:, :])
                            if e == 7:
                                tr.then_inc(sem["s_tr3"], 1)

            @blk.vector
            def _(v):
                v.wait_ge(sem["d_lnf"], 16 * 8)
                v.wait_ge(sem["d_misc"], 32)
                v.wait_ge(sem["d_wf"], 16 * 6)
                for c in range(8):
                    mk = v.tensor_mul(ln1f[:, c * QR + 127: c * QR + 128],
                                      ln1f[:, c * QR + 127: c * QR + 128], hmask[:, 0:1])
                    if c == 7:
                        mk.then_inc(sem["s_mask"], 1)
                for stt in range(4):
                    if stt >= 1:
                        v.wait_ge(sem["s_rp"], 8 * stt)
                        v.wait_ge(sem["s_kp"], 8 * stt)
                    for c in range(8):
                        cur = ln1f[:, c * QR + 128 + stt * 512: c * QR + 128 + stt * 512 + 512]
                        prev = ln1f[:, c * QR + 127 + stt * 512: c * QR + 127 + stt * 512 + 512]
                        v.tensor_sub(d_sb[:, c * 512:(c + 1) * 512], cur, prev)
                        v.scalar_tensor_tensor(xr_sb[:, c * 512:(c + 1) * 512],
                                               d_sb[:, c * 512:(c + 1) * 512],
                                               mu_sb[:, c:c + 1], prev, OP.mult, OP.add)
                        stx = v.scalar_tensor_tensor(xk_sb[:, c * 512:(c + 1) * 512],
                                                     d_sb[:, c * 512:(c + 1) * 512],
                                                     mu_sb[:, 8 + c:9 + c], prev, OP.mult, OP.add)
                        if c == 7:
                            stx.then_inc(sem["s_xr"], 1)
                    for d in range(8):
                        dd = 8 * stt + d
                        v.wait_ge(sem["s_rel"], dd + 1)
                        ka = v.scalar_tensor_tensor(k_act[:, d * 512:(d + 1) * 512],
                                                    relu_tmp[d % 2][:, :], 1.0,
                                                    relu_tmp[d % 2][:, :], OP.mult, OP.mult)
                        ka.then_inc(sem["s_ka"], 1)
                    for e in range(8):
                        ee = 8 * stt + e
                        v.wait_ge(sem["s_vp"], ee + 1)
                        fe = v.tensor_tensor(ffn_sb[:, e * 512:(e + 1) * 512],
                                             psC[e % 2][:, :], rf[:, e * 512:(e + 1) * 512],
                                             OP.mult)
                        fe.then_inc(sem["s_ff"], 1)
                    for sub in range(4):
                        u = 4 * stt + sub
                        p = u % 2
                        v.wait_ge(sem["s_tr3"], u + 1)
                        v.wait_ge(sem[f"d_x13{p}"], 16 * (u // 2 + 1))
                        if u >= 2:
                            v.wait_ge(sem[f"d_ost{p}"], 16 * (u // 2))
                        v.tensor_add(out_sb[p][:, 0:512], fout_ps[p][0][:, :],
                                     xstage[p][:, 0:512])
                        v.tensor_add(out_sb[p][:, 512:H], fout_ps[p][1][:, :],
                                     xstage[p][:, 512:H]).then_inc(sem["s_outv"], 1)

            @blk.scalar
            def _(scl):
                for stt in range(4):
                    for d in range(8):
                        dd = 8 * stt + d
                        scl.wait_ge(sem["s_rp"], dd + 1)
                        if d == 0 and stt >= 1:
                            scl.wait_ge(sem["s_ff"], 8 * stt)
                        scl.activation(rf[:, d * 512:(d + 1) * 512], psC[d % 2][:, :],
                                       AF.Sigmoid).then_inc(sem["s_sig"], 1)
                    for d in range(8):
                        dd = 8 * stt + d
                        scl.wait_ge(sem["s_kp"], dd + 1)
                        if dd >= 2:
                            scl.wait_ge(sem["s_ka"], dd - 1)
                        scl.activation(relu_tmp[d % 2][:, :], psC[2 + d % 2][:, :],
                                       AF.Relu).then_inc(sem["s_rel"], 1)
                    for sub in range(4):
                        u = 4 * stt + sub
                        scl.wait_ge(sem["s_outv"], u + 1)
                        scl.dma_start(o_out[u * 128:(u + 1) * 128, :],
                                      out_sb[u % 2][:, :]).then_inc(sem[f"d_ost{u % 2}"], 16)
                scl.wait_ge(sem["d_ost0"], 16 * 8)
                scl.wait_ge(sem["d_ost1"], 16 * 8)

        cstk.close()

    return nc


_NC_CACHE = [None]


def kernel(L0_attn_x, L0_attn_W_qkv, L0_attn_W_out, L0_ffn_W_r, L0_ffn_W_k,
           L0_ffn_W_v, L0_ffn_mu_r, L0_ffn_mu_k):
    x = np.ascontiguousarray(np.asarray(L0_attn_x, dtype=np.float32))
    wqkv = np.asarray(L0_attn_W_qkv, dtype=np.float32)
    wout = np.asarray(L0_attn_W_out, dtype=np.float32)
    wr = np.asarray(L0_ffn_W_r, dtype=np.float32)
    wkf = np.asarray(L0_ffn_W_k, dtype=np.float32)
    wvf = np.asarray(L0_ffn_W_v, dtype=np.float32)
    mur = np.asarray(L0_ffn_mu_r, dtype=np.float32)
    muk = np.asarray(L0_ffn_mu_k, dtype=np.float32)

    wq_h = np.ascontiguousarray(wqkv[0::3, :])
    wk_h = np.ascontiguousarray(wqkv[1::3, :])
    wv_h = np.ascontiguousarray(wqkv[2::3, :])
    woutT_h = np.ascontiguousarray(wout.T)
    wrT_h = np.ascontiguousarray(wr.T)
    wkfT_h = np.ascontiguousarray(wkf.T)
    wvfT_h = np.ascontiguousarray(wvf.T)
    mur_h = np.ascontiguousarray(mur.reshape(8, 128).T)
    muk_h = np.ascontiguousarray(muk.reshape(8, 128).T)
    identb_h = np.eye(128, dtype=np.float32).astype(ml_dtypes.bfloat16)
    identf_h = np.eye(128, dtype=np.float32)

    if _NC_CACHE[0] is None:
        _NC_CACHE[0] = build_nc()
    nc = _NC_CACHE[0]

    in_maps = []
    for c in range(8):
        b, m = c // 4, c % 4
        s0 = 2048 * m
        if m == 0:
            xo = np.concatenate([x[b, 0:128], x[b, 0:2048]], axis=0)
            hm = np.zeros((128, 1), np.float32)
        else:
            xo = x[b, s0 - 128:s0 + 2048]
            hm = np.ones((128, 1), np.float32)
        in_maps.append({
            "x_batch": x[b],
            "x_own": np.ascontiguousarray(xo),
            "wq": wq_h, "wk": wk_h, "wv": wv_h, "woutT": woutT_h,
            "wrT": wrT_h, "wkfT": wkfT_h, "wvfT": wvfT_h,
            "mu_r_in": mur_h, "mu_k_in": muk_h,
            "identb_in": identb_h, "identf_in": identf_h,
            "hmask_in": hm,
        })

    res = run_bass_kernel_spmd(nc, in_maps, core_ids=list(range(8)))
    out = np.empty((2, 8192, 1024), np.float32)
    for c in range(8):
        b, m = c // 4, c % 4
        out[b, 2048 * m:2048 * (m + 1), :] = res.results[c]["o_out"]
    return out


# version 06
# speedup vs baseline: 1.1782x; 1.1447x over previous
import contextlib
import numpy as np
import ml_dtypes
import concourse.bass as bass
import concourse.mybir as mybir
from concourse.bass_utils import run_bass_kernel_spmd

F32 = mybir.dt.float32
BF16 = mybir.dt.bfloat16
AF = mybir.ActivationFunctionType
AX = mybir.AxisListType.X
OP = mybir.AluOpType

H = 1024
S = 8192
NT = 17
NB = 64
NLN = NT + NB
QR = NT * 128
EPS = 1e-5
INV_H = 1.0 / H
NCHUNKS = [(0, 512), (512, 512), (1024, 512), (1536, 512), (2048, 128)]


def build_nc(phases=4, dbg=False):
    nc = bass.Bass("TRN2", target_bir_lowering=False, detect_race_conditions=False)

    x_batch = nc.dram_tensor("x_batch", [S, H], F32, kind="ExternalInput")
    x_own = nc.dram_tensor("x_own", [QR, H], F32, kind="ExternalInput")
    wq = nc.dram_tensor("wq", [H, H], F32, kind="ExternalInput")
    wk = nc.dram_tensor("wk", [H, H], F32, kind="ExternalInput")
    wv = nc.dram_tensor("wv", [H, H], F32, kind="ExternalInput")
    woutT = nc.dram_tensor("woutT", [H, H], F32, kind="ExternalInput")
    wrT = nc.dram_tensor("wrT", [H, H], F32, kind="ExternalInput")
    wkfT = nc.dram_tensor("wkfT", [H, H], F32, kind="ExternalInput")
    wvfT = nc.dram_tensor("wvfT", [H, H], F32, kind="ExternalInput")
    mu_r_in = nc.dram_tensor("mu_r_in", [128, 8], F32, kind="ExternalInput")
    mu_k_in = nc.dram_tensor("mu_k_in", [128, 8], F32, kind="ExternalInput")
    identb_in = nc.dram_tensor("identb_in", [128, 128], BF16, kind="ExternalInput")
    identf_in = nc.dram_tensor("identf_in", [128, 128], F32, kind="ExternalInput")
    hmask_in = nc.dram_tensor("hmask_in", [128, 1], F32, kind="ExternalInput")

    o_out = nc.dram_tensor("o_out", [16 * 128, H], F32, kind="ExternalOutput")

    KD = "ExternalOutput" if dbg else "Internal"
    s_lnown = nc.dram_tensor("s_lnown", [QR, H], BF16, kind=KD)
    s_xt = nc.dram_tensor("s_xt", [S, H], BF16, kind=KD)
    s_p = nc.dram_tensor("s_p", [NT, 64, 128, 128], BF16, kind=KD)
    s_x1 = nc.dram_tensor("s_x1", [NT, 128, H], F32, kind=KD)
    s_ln1 = nc.dram_tensor("s_ln1", [QR, H], BF16, kind=KD)
    s_negb = nc.dram_tensor("s_negb", [NT, 128, 1], F32, kind=KD)
    s_qpf = nc.dram_tensor("s_qpf", [8, 128, QR], BF16, kind=KD)
    s_wvo = nc.dram_tensor("s_wvo", [128, 8 * H], BF16, kind="Internal")

    with contextlib.ExitStack() as st:
        xpair = [st.enter_context(nc.sbuf_tensor(f"xpair{p}", [128, 2 * H], F32)) for p in range(3)]
        dst_sb = [st.enter_context(nc.sbuf_tensor(f"dstsb{p}", [128, H], BF16)) for p in range(4)]
        scrf32 = [st.enter_context(nc.sbuf_tensor(f"scrf32{p}", [128, H], F32)) for p in range(2)]
        sq_dummy, x1_sb = scrf32[0], scrf32[1]
        out_sb = scrf32
        ctxX_sb = st.enter_context(nc.sbuf_tensor("ctxX", [128, H], BF16))
        ctxXf_sb = st.enter_context(nc.sbuf_tensor("ctxXf", [128, H], BF16))
        identb = st.enter_context(nc.sbuf_tensor("identb", [128, 128], BF16))
        ones_bf = st.enter_context(nc.sbuf_tensor("ones_bf", [128, 1], BF16))
        negb_cols = st.enter_context(nc.sbuf_tensor("negb_cols", [128, 4 * NT], F32))
        s_all = st.enter_context(nc.sbuf_tensor("s_all", [128, 4 * NT], F32))
        s_part = st.enter_context(nc.sbuf_tensor("s_part", [128, 16], F32))
        sinv_all = st.enter_context(nc.sbuf_tensor("sinv_all", [128, 4 * NT], F32))
        hmask = st.enter_context(nc.sbuf_tensor("hmask", [128, 1], F32))
        c_ssq = st.enter_context(nc.sbuf_tensor("c_ssq", [128, 32], F32))
        c_ngs = st.enter_context(nc.sbuf_tensor("c_ngs", [128, 32], F32))
        c_nm = st.enter_context(nc.sbuf_tensor("c_nm", [128, 32], F32))
        c_t1 = st.enter_context(nc.sbuf_tensor("c_t1", [128, 32], F32))
        c_m2 = st.enter_context(nc.sbuf_tensor("c_m2", [128, 32], F32))
        c_vpe = st.enter_context(nc.sbuf_tensor("c_vpe", [128, 32], F32))
        c_rin = st.enter_context(nc.sbuf_tensor("c_rin", [128, 32], F32))
        c_rsd = st.enter_context(nc.sbuf_tensor("c_rsd", [128, 32], F32))

        sem_names = ["d_ld0", "d_ld1", "d_ld2", "d_ld3", "d_ld4", "d_ld5", "d_ld6", "d_ld7",
                     "d_st0", "d_st1", "d_st2", "d_st3", "d_st4", "d_st5", "d_st6", "d_st7", "d_wmov", "d_wcol0",
                     "d_wcol1", "d_misc", "d_wf", "d_xfo", "d_xf", "d_xt",
                     "d_nb", "d_wvost0", "d_wvost1", "d_wvold", "d_pst0", "d_pst1", "d_pst2", "d_pst3",
                     "d_pt0", "d_pt1", "d_xo20", "d_xo21", "d_x1st", "d_lnf",
                     "d_x130", "d_x131", "d_ost0", "d_ost1", "d_qst0", "d_qst1",
                     "d_q2l0", "d_q2l1", "d_qt0", "d_qt1", "s_pvh",
                     "s_sq", "s_var", "s_rstd", "s_app",
                     "s_fmm", "s_fev", "s_qmm", "s_qev", "s_q2", "s_bmm",
                     "s_bsq", "s_bneg", "s_sc", "s_exp", "s_sv",
                     "s_pv", "s_ctx", "s_trsT", "s_trsV", "s_op", "s_x1v",
                     "s_mask", "s_xr", "s_xk", "s_rp", "s_sig", "s_kp", "s_rel",
                     "s_ka", "s_vp", "s_ff", "s_tr3", "s_outv"]
        sem = {n: st.enter_context(nc.semaphore(n)) for n in sem_names}

        NG = (NLN + 3) // 4

        def stcnt(upto, s):
            return len([j for j in range(upto) if j % 4 == s])

        def pair_of(k):
            if k < 16:
                return k // 2, k % 2
            if k == 16:
                return 8, 0
            return 9 + (k - 17) // 2, (k - 17) % 2

        def xtile(k):
            pi, off = pair_of(k)
            return xpair[pi % 3][:, off * H:(off + 1) * H]

        def ld_wait(eng, k):
            pi, _ = pair_of(k)
            eng.wait_ge(sem[f"d_ld{pi % 3}"], 16 * (pi // 3 + 1))

        def grp_tiles(g):
            return range(4 * g, min(4 * g + 4, NLN))

        def cslot(k):
            return 4 * (k % 8)

        def cs4(t, g):
            base = 4 * ((4 * g) % 8)
            n = len(list(grp_tiles(g)))
            return t[:, base:base + 4 * n:4]

        def ln_dst_dram(k):
            if k < NT:
                return s_lnown[k * 128:(k + 1) * 128, :]
            if k < NLN:
                kb = k - NT
                return s_xt[kb * 128:(kb + 1) * 128, :]
            kc = k - NLN
            return s_ln1[kc * 128:(kc + 1) * 128, :]

        def ln2_vec(v, i):
            k = NLN + i
            q = 0
            v.reduce_sum(c_ngs[:, q:q + 1], x1_sb[:, :], axis=AX, negate=True)
            v.wait_ge(sem["s_sq"], k + 1)
            v.tensor_scalar(c_t1[:, q:q + 1], c_ssq[:, q:q + 1], INV_H, EPS, OP.mult, OP.add)
            v.tensor_scalar_mul(c_nm[:, q:q + 1], c_ngs[:, q:q + 1], INV_H)
            v.tensor_copy(c_ngs[:, 2:3], c_t1[:, q:q + 1])
            v.tensor_mul(c_m2[:, q:q + 1], c_nm[:, q:q + 1], c_nm[:, q:q + 1])
            v.tensor_copy(c_ngs[:, 3:4], c_t1[:, q:q + 1])
            v.tensor_sub(c_vpe[:, q:q + 1], c_t1[:, q:q + 1], c_m2[:, q:q + 1])
            v.tensor_copy(c_ngs[:, 6:7], c_t1[:, q:q + 1])
            v.reciprocal(c_rin[:, q:q + 1], c_vpe[:, q:q + 1]).then_inc(sem["s_var"], 1)
            v.wait_ge(sem["s_rstd"], NG + i + 1)
            ds = i % 4
            if i >= 4:
                v.wait_ge(sem[f"d_st{ds}"], 16 * (stcnt(NLN, ds) + len([j for j in range(i - 3) if j % 4 == ds])))
            v.tensor_scalar(dst_sb[ds][:, :], x1_sb[:, :],
                            c_nm[:, 0:1], c_rsd[:, 0:1],
                            OP.add, OP.mult).then_inc(sem["s_app"], 1)

        def ln2_scal(scl, i):
            k = NLN + i
            scl.activation(sq_dummy[:, :], x1_sb[:, :], AF.Square,
                           accum_out=c_ssq[:, 0:1]).then_inc(sem["s_sq"], 1)
            scl.wait_ge(sem["s_var"], NG + i + 1)
            scl.activation(c_rsd[:, 0:1], c_rin[:, 0:1],
                           AF.Sqrt).then_inc(sem["s_rstd"], 1)
            scl.wait_ge(sem["s_app"], k + 1)
            scl.dma_start(ln_dst_dram(k), dst_sb[i % 4][:, :]).then_inc(sem[f"d_st{i % 4}"], 16)

        rstk = contextlib.ExitStack()
        qpf_tile = [rstk.enter_context(nc.sbuf_tensor(f"qpt{p}", [128, 8 * 128], BF16, side="right"))
                    for p in range(2)]
        p_sb = rstk.enter_context(nc.sbuf_tensor("p_sb", [128, S], BF16, side="right"))

        pstk = contextlib.ExitStack()
        ps8 = [pstk.enter_context(nc.psum_tensor(f"ps8_{i}", [128, 512], F32)) for i in range(8)]

        prostk = contextlib.ExitStack()
        wqkS = prostk.enter_context(nc.sbuf_tensor("wqkS", [128, 8 * H], BF16))
        movstage = prostk.enter_context(nc.sbuf_tensor("movstage", [128, 8 * H], BF16))
        colbuf = [prostk.enter_context(nc.sbuf_tensor(f"colbuf{p}", [128, 8 * 128], BF16)) for p in range(2)]
        wvo_stage = [prostk.enter_context(nc.sbuf_tensor(f"wvostage{p}", [128, H], BF16)) for p in range(2)]
        xfo = prostk.enter_context(nc.sbuf_tensor("xfo", [128, 8 * QR], BF16))
        q2buf = [prostk.enter_context(nc.sbuf_tensor(f"q2buf{p}", [128, QR], BF16)) for p in range(2)]
        qstage = [prostk.enter_context(nc.sbuf_tensor(f"qstage{p}", [128, 512], BF16)) for p in range(2)]
        q2ld = [prostk.enter_context(nc.sbuf_tensor(f"q2ld{p}", [128, QR], BF16)) for p in range(2)]
        b_row = prostk.enter_context(nc.sbuf_tensor("b_row", [1, QR], F32))

        with nc.named_scope("phP"), nc.Block() as blk:
            @blk.sync
            def _(s):
                pairs = ([(2 * m, 2) for m in range(8)] + [(16, 1)]
                         + [(17 + 2 * m, 2) for m in range(32)])
                for pi, (k0, ntl) in enumerate(pairs):
                    if pi >= 3:
                        prev_k0, prev_n = pairs[pi - 3]
                        s.wait_ge(sem["s_app"], prev_k0 + prev_n)
                    if k0 < NT:
                        src = x_own[k0 * 128:(k0 + ntl) * 128, :]
                    else:
                        kb = k0 - NT
                        src = x_batch[kb * 128:(kb + ntl) * 128, :]
                    if ntl == 2:
                        s.dma_start(
                            xpair[pi % 3].rearrange("p (o f) -> p o f", f=H),
                            src.rearrange("(o p) f -> p o f", p=128)
                        ).then_inc(sem[f"d_ld{pi % 3}"], 16)
                    else:
                        s.dma_start(xpair[pi % 3][:, 0:H], src
                                    ).then_inc(sem[f"d_ld{pi % 3}"], 16)

            @blk.gpsimd
            def _(g):
                g.dma_start(movstage.rearrange("p (c f) -> p c f", f=H),
                            wk.rearrange("(c p) f -> p c f", p=128)).then_inc(sem["d_wmov"], 16)
                for gg in range(8):
                    if gg >= 2:
                        g.wait_ge(sem["s_fmm"], gg - 1)
                    g.dma_start(colbuf[gg % 2].rearrange("p (c f) -> p c f", f=128),
                                wq.rearrange("(c p) f -> p c f", p=128)[:, :, gg * 128:(gg + 1) * 128]
                                ).then_inc(sem[f"d_wcol{gg % 2}"], 16)
                g.dma_start(identb[:, :], identb_in[:, :]).then_inc(sem["d_misc"], 16)
                g.dma_start(hmask[:, :], hmask_in[:, :]).then_inc(sem["d_misc"], 16)
                g.wait_ge(sem["s_fmm"], 8)
                g.dma_start(movstage.rearrange("p (c f) -> p c f", f=H),
                            woutT.rearrange("(c p) f -> p c f", p=128)).then_inc(sem["d_wmov"], 16)
                for gg in range(8):
                    g.wait_ge(sem["s_fmm"], 8 if gg < 2 else 8 + gg - 1)
                    g.dma_start(colbuf[gg % 2].rearrange("p (c f) -> p c f", f=128),
                                wv.rearrange("(c p) f -> p c f", p=128)[:, :, gg * 128:(gg + 1) * 128]
                                ).then_inc(sem[f"d_wcol{gg % 2}"], 16)
                for gp in range(8):
                    for ncx, (off, nw) in enumerate(NCHUNKS):
                        idx = gp * 5 + ncx
                        g.wait_ge(sem["s_qev"], idx + 1)
                        g.dma_start(s_qpf[gp][:, off:off + nw],
                                    qstage[idx % 2][:, 0:nw]).then_inc(sem[f"d_qst{idx % 2}"], 16)
                g.wait_ge(sem["d_qst0"], 16 * 20)
                g.wait_ge(sem["d_qst1"], 16 * 20)
                for gp in range(8):
                    if gp >= 2:
                        g.wait_ge(sem["s_q2"], gp - 1)
                    g.dma_start(q2ld[gp % 2][:, :], s_qpf[gp]).then_inc(sem[f"d_q2l{gp % 2}"], 16)

            @blk.tensor
            def _(t):
                t.wait_ge(sem["d_wmov"], 16)
                for gg in range(8):
                    t.wait_ge(sem[f"d_wcol{gg % 2}"], 16 * (gg // 2 + 1))
                    if gg >= 2:
                        t.wait_ge(sem["s_fev"], gg - 1)
                    pe, po = ps8[(gg % 2) * 2], ps8[(gg % 2) * 2 + 1]
                    cb = colbuf[gg % 2]
                    for ig in range(8):
                        stat = cb[:, ig * 128:(ig + 1) * 128]
                        t.matmul(pe[:, :], stat, movstage[:, ig * H:ig * H + 512],
                                 start=(ig == 0), stop=(ig == 7))
                        mm = t.matmul(po[:, :], stat, movstage[:, ig * H + 512:(ig + 1) * H],
                                      start=(ig == 0), stop=(ig == 7))
                        if ig == 7:
                            mm.then_inc(sem["s_fmm"], 1)
                t.wait_ge(sem["d_wmov"], 32)
                for gg in range(8):
                    t.wait_ge(sem[f"d_wcol{gg % 2}"], 16 * (4 + gg // 2 + 1))
                    if gg >= 2:
                        t.wait_ge(sem["s_fev"], 8 + gg - 1)
                    pe, po = ps8[(gg % 2) * 2], ps8[(gg % 2) * 2 + 1]
                    cb = colbuf[gg % 2]
                    for ig in range(8):
                        stat = cb[:, ig * 128:(ig + 1) * 128]
                        t.matmul(pe[:, :], stat, movstage[:, ig * H:ig * H + 512],
                                 start=(ig == 0), stop=(ig == 7))
                        mm = t.matmul(po[:, :], stat, movstage[:, ig * H + 512:(ig + 1) * H],
                                      start=(ig == 0), stop=(ig == 7))
                        if ig == 7:
                            mm.then_inc(sem["s_fmm"], 1)
                t.wait_ge(sem["d_xfo"], 16 * 8)
                t.wait_ge(sem["s_fev"], 16)
                for gp in range(8):
                    if gp >= 1:
                        t.wait_ge(sem["s_qev"], 5 * gp)
                    for hg in range(8):
                        stat = wqkS[:, hg * H + gp * 128: hg * H + (gp + 1) * 128]
                        for ncx, (off, nw) in enumerate(NCHUNKS):
                            mm = t.matmul(ps8[ncx][:, 0:nw], stat,
                                          xfo[:, hg * QR + off: hg * QR + off + nw],
                                          start=(hg == 0), stop=(hg == 7))
                            if hg == 7 and ncx == 4:
                                mm.then_inc(sem["s_qmm"], 1)
                t.wait_ge(sem["s_qev"], 40)
                for gp in range(8):
                    t.wait_ge(sem["s_q2"], gp + 1)
                    for ncx, (off, nw) in enumerate(NCHUNKS):
                        mm = t.matmul(ps8[ncx][0:1, 0:nw], ones_bf[:, 0:1],
                                      q2buf[gp % 2][:, off:off + nw],
                                      start=(gp == 0), stop=(gp == 7))
                        if ncx == 4:
                            mm.then_inc(sem["s_bmm"], 1)

            @blk.vector
            def _(v):
                v.memset(ones_bf[:, :], 1.0)
                v.memset(negb_cols[:, :], 0.0)

                evq = []
                for gg in range(8):
                    def ev_wqk(v, gg=gg):
                        v.wait_ge(sem["s_fmm"], gg + 1)
                        pe, po = ps8[(gg % 2) * 2], ps8[(gg % 2) * 2 + 1]
                        v.tensor_scalar(wqkS[:, gg * H:gg * H + 512], pe[:, :],
                                        1.0 / 32.0, None, OP.mult)
                        v.tensor_scalar(wqkS[:, gg * H + 512:(gg + 1) * H], po[:, :],
                                        1.0 / 32.0, None, OP.mult).then_inc(sem["s_fev"], 1)
                    evq.append(ev_wqk)
                for gg in range(8):
                    def ev_wvo(v, gg=gg):
                        v.wait_ge(sem["s_fmm"], 8 + gg + 1)
                        if gg >= 2:
                            v.wait_ge(sem[f"d_wvost{gg % 2}"], 16 * (gg // 2))
                        pe, po = ps8[(gg % 2) * 2], ps8[(gg % 2) * 2 + 1]
                        v.tensor_copy(wvo_stage[gg % 2][:, 0:512], pe[:, :])
                        v.tensor_copy(wvo_stage[gg % 2][:, 512:H], po[:, :]).then_inc(sem["s_fev"], 1)
                    evq.append(ev_wvo)
                for gp in range(8):
                    def ev_qp(v, gp=gp):
                        v.wait_ge(sem["s_qmm"], gp + 1)
                        for ncx, (off, nw) in enumerate(NCHUNKS):
                            idx = gp * 5 + ncx
                            if idx >= 2:
                                v.wait_ge(sem[f"d_qst{idx % 2}"], 16 * (idx // 2))
                            v.tensor_copy(qstage[idx % 2][:, 0:nw],
                                          ps8[ncx][:, 0:nw]).then_inc(sem["s_qev"], 1)
                    evq.append(ev_qp)

                def v_apply(v, k, g):
                    ds = k % 4
                    v.wait_ge(sem["s_rstd"], g + 1)
                    if k >= 4:
                        v.wait_ge(sem[f"d_st{ds}"], 16 * stcnt(k - 3, ds))
                    v.tensor_scalar(dst_sb[ds][:, :], xtile(k)[:, :],
                                    c_nm[:, cslot(k):cslot(k) + 1],
                                    c_rsd[:, cslot(k):cslot(k) + 1],
                                    OP.add, OP.mult).then_inc(sem["s_app"], 1)

                for g in range(NG):
                    if g >= 1:
                        for k in grp_tiles(g - 1):
                            v_apply(v, k, g - 1)
                    if g >= 5:
                        for _ in range(2):
                            if evq:
                                evq.pop(0)(v)
                    for k in grp_tiles(g):
                        ld_wait(v, k)
                        v.reduce_sum(c_ngs[:, cslot(k):cslot(k) + 1],
                                     xtile(k)[:, :], axis=AX, negate=True)
                    v.wait_ge(sem["s_sq"], list(grp_tiles(g))[-1] + 1)
                    v.tensor_scalar(cs4(c_t1, g), cs4(c_ssq, g), INV_H, EPS, OP.mult, OP.add)
                    v.tensor_scalar_mul(cs4(c_nm, g), cs4(c_ngs, g), INV_H)
                    v.tensor_copy(c_ngs[:, 2:3], c_t1[:, 0:1])
                    v.tensor_mul(cs4(c_m2, g), cs4(c_nm, g), cs4(c_nm, g))
                    v.tensor_copy(c_ngs[:, 3:4], c_t1[:, 0:1])
                    v.tensor_sub(cs4(c_vpe, g), cs4(c_t1, g), cs4(c_m2, g))
                    v.tensor_copy(c_ngs[:, 6:7], c_t1[:, 0:1])
                    v.reciprocal(cs4(c_rin, g), cs4(c_vpe, g)).then_inc(sem["s_var"], 1)
                for k in grp_tiles(NG - 1):
                    v_apply(v, k, NG - 1)
                while evq:
                    evq.pop(0)(v)
                v.wait_ge(sem["d_nb"], 16 * 18)
                v.tensor_scalar_mul(negb_cols[:, :], negb_cols[:, :], -1.0
                                    ).then_inc(sem["s_bneg"], 1)

            @blk.scalar
            def _(scl):
                def emit_store(scl, k):
                    ds = k % 4
                    scl.wait_ge(sem["s_app"], k + 1)
                    scl.dma_start(ln_dst_dram(k), dst_sb[ds][:, :]).then_inc(sem[f"d_st{ds}"], 16)

                popmap = {}

                def ev_xfo(scl):
                    for s4 in range(4):
                        scl.wait_ge(sem[f"d_st{s4}"], 16 * stcnt(NT, s4))
                    for g in range(8):
                        scl.dma_start_transpose(
                            xfo[:, g * QR:(g + 1) * QR],
                            s_lnown.ap()[:, g * 128:(g + 1) * 128]
                        ).then_inc(sem["d_xfo"], 16)
                popmap[6] = [ev_xfo]

                for gg in range(8):
                    def ev_wvost(scl, gg=gg):
                        scl.wait_ge(sem["s_fev"], 8 + gg + 1)
                        scl.dma_start(s_wvo.ap()[:, gg * H:(gg + 1) * H],
                                      wvo_stage[gg % 2][:, :]).then_inc(sem[f"d_wvost{gg % 2}"], 16)
                    popmap.setdefault(10 + gg // 2, []).append(ev_wvost)

                for gp in range(8):
                    def ev_q2(scl, gp=gp):
                        scl.wait_ge(sem[f"d_q2l{gp % 2}"], 16 * (gp // 2 + 1))
                        scl.activation(q2buf[gp % 2][:, :], q2ld[gp % 2][:, :],
                                       AF.Square).then_inc(sem["s_q2"], 1)
                    popmap.setdefault(17 + gp // 2, []).append(ev_q2)

                def ev_negb_s(scl):
                    scl.wait_ge(sem["s_bmm"], 8)
                    for ncx, (off, nw) in enumerate(NCHUNKS):
                        scl.activation(b_row[0:1, off:off + nw], ps8[ncx][0:1, 0:nw],
                                       AF.Sqrt, scale=float(H)).then_inc(sem["s_bsq"], 1)
                    scl.wait_ge(sem["s_bsq"], 5)
                    scl.dma_start(s_negb.ap().rearrange("a p o -> o (a p)"),
                                  b_row[0:1, :]).then_inc(sem["d_nb"], 16)
                    scl.wait_ge(sem["d_nb"], 16)
                    for i in range(NT):
                        scl.dma_start(negb_cols[:, 4 * i:4 * i + 1], s_negb[i]).then_inc(sem["d_nb"], 16)

                for g in range(NG):
                    if g >= 1:
                        scl.wait_ge(sem["s_var"], g)
                        scl.activation(cs4(c_rsd, g - 1), cs4(c_rin, g - 1),
                                       AF.Sqrt).then_inc(sem["s_rstd"], 1)
                        for k in grp_tiles(g - 1):
                            emit_store(scl, k)
                    for k in grp_tiles(g):
                        ld_wait(scl, k)
                        scl.activation(sq_dummy[:, :], xtile(k)[:, :], AF.Square,
                                       accum_out=c_ssq[:, cslot(k):cslot(k) + 1]
                                       ).then_inc(sem["s_sq"], 1)
                    for fn in popmap.pop(g, []):
                        fn(scl)
                g = NG - 1
                scl.wait_ge(sem["s_var"], NG)
                scl.activation(cs4(c_rsd, g), cs4(c_rin, g),
                               AF.Sqrt).then_inc(sem["s_rstd"], 1)
                for k in grp_tiles(g):
                    emit_store(scl, k)
                for g in sorted(popmap):
                    for fn in popmap.pop(g, []):
                        fn(scl)
                ev_negb_s(scl)

        prostk.close()

        arena_stk = contextlib.ExitStack()
        arena = arena_stk.enter_context(nc.sbuf_tensor("arena", [128, 8 * S], BF16))

        if phases >= 2:
          with nc.named_scope("phA"), nc.Block() as blk:
            @blk.sync
            def _(s):
                for s4 in range(4):
                    s.wait_ge(sem[f"d_st{s4}"], 16 * stcnt(NLN, s4))
                for g in range(8):
                    s.dma_start_transpose(arena[:, g * S:(g + 1) * S],
                                          s_xt.ap()[:, g * 128:(g + 1) * 128]
                                          ).then_inc(sem["d_xf"], 16)
                for i in range(NT):
                    p = i % 2
                    if i >= 2:
                        s.wait_ge(sem["s_sc"], 4 * (i - 1))
                    s.dma_start(qpf_tile[p].rearrange("p (g s) -> p g s", s=128),
                                s_qpf.ap()[:, :, i * 128:(i + 1) * 128].transpose([1, 0, 2])
                                ).then_inc(sem[f"d_qt{p}"], 16)

            @blk.tensor
            def _(t):
                t.wait_ge(sem["d_xf"], 16 * 8)
                for i in range(NT):
                    p = i % 2
                    t.wait_ge(sem[f"d_qt{p}"], 16 * (i // 2 + 1))
                    for q in range(4):
                        gq = 4 * i + q
                        grp = ps8[(q % 2) * 4:(q % 2) * 4 + 4]
                        if gq >= 2:
                            t.wait_ge(sem["s_exp"], gq - 1)
                        for gp in range(8):
                            stat = qpf_tile[p][:, gp * 128:(gp + 1) * 128]
                            for j in range(4):
                                kb = q * 2048 + j * 512
                                mm = t.matmul(grp[j][:, :], stat,
                                              arena[:, gp * S + kb: gp * S + kb + 512],
                                              start=(gp == 0), stop=(gp == 7))
                                if gp == 7 and j == 3:
                                    mm.then_inc(sem["s_sc"], 1)

            @blk.scalar
            def _(scl):
                scl.wait_ge(sem["s_bneg"], 1)
                for i in range(NT):
                    for q in range(4):
                        gq = 4 * i + q
                        grp = ps8[(q % 2) * 4:(q % 2) * 4 + 4]
                        scl.wait_ge(sem["s_sc"], gq + 1)
                        if i >= 1:
                            scl.wait_ge(sem[f"d_pst{q}"], 64 * i)
                            scl.wait_ge(sem["s_sv"], 5 * (i - 1) + q + 1)
                        for j in range(4):
                            kb = q * 2048 + j * 512
                            act = scl.activation(p_sb[:, kb:kb + 512], grp[j][:, :],
                                                 AF.Exp, bias=negb_cols[:, 4 * i:4 * i + 1])
                            if j == 3:
                                act.then_inc(sem["s_exp"], 1)
                    scl.wait_ge(sem["s_exp"], 4 * (i + 1))
                    for w in range(16):
                        scl.dma_start(
                            s_p[i, 4 * w:4 * w + 4].transpose([1, 0, 2]),
                            p_sb.rearrange("s (j t) -> s j t", t=128)[:, 4 * w:4 * w + 4, :]
                        ).then_inc(sem[f"d_pst{w // 4}"], 16)

            @blk.vector
            def _(v):
                for i in range(NT):
                    for q in range(4):
                        v.wait_ge(sem["s_exp"], 4 * i + q + 1)
                        v.reduce_sum(s_part[:, 4 * q:4 * q + 1],
                                     p_sb[:, q * 2048:(q + 1) * 2048], axis=AX
                                     ).then_inc(sem["s_sv"], 1)
                    v.tensor_copy(s_part[:, 1:2], s_part[:, 0:1])
                    v.reduce_sum(s_all[:, 4 * i:4 * i + 1], s_part[:, 0:13:4], axis=AX
                                 ).then_inc(sem["s_sv"], 1)

        rstk.close()
        pstk.close()

        rstk2 = contextlib.ExitStack()
        pt_sb = [rstk2.enter_context(nc.sbuf_tensor(f"pt{p}", [128, S // 2], BF16, side="right"))
                 for p in range(2)]
        wvoS = rstk2.enter_context(nc.sbuf_tensor("wvoS", [128, 8 * H], BF16, side="right"))

        pstk2 = contextlib.ExitStack()
        ctx_ps = [[pstk2.enter_context(nc.psum_tensor(f"ctx{p}{h}", [128, 512], F32))
                   for h in range(2)] for p in range(2)]
        trs_ps = pstk2.enter_context(nc.psum_tensor("trs", [128, H], BF16))
        attn_ps = [pstk2.enter_context(nc.psum_tensor(f"attn{h}", [128, 512], F32))
                   for h in range(2)]

        if phases >= 3:
          with nc.named_scope("phB"), nc.Block() as blk:
            @blk.sync
            def _(s):
                for c in range(8):
                    s.dma_start(arena[:, c * S:(c + 1) * S].rearrange("p (g f) -> p g f", f=H),
                                s_xt.rearrange("(g p) f -> p g f", p=128)[:, c * 8:(c + 1) * 8, :]
                                ).then_inc(sem["d_xt"], 16)
                s.wait_ge(sem["d_wvost0"], 16 * 4)
                s.wait_ge(sem["d_wvost1"], 16 * 4)
                s.dma_start(wvoS[:, :], s_wvo.ap()).then_inc(sem["d_wvold"], 16)
                for i in range(NT):
                    p = i % 2
                    for qq in range(4):
                        s.wait_ge(sem[f"d_pst{qq}"], 64 * (i + 1))
                    for half in range(2):
                        hx = 2 * i + half
                        if hx >= 2:
                            s.wait_ge(sem["s_pvh"], hx - 1)
                        s.dma_start_transpose(
                            pt_sb[half][:, :],
                            s_p[i, 32 * half:32 * half + 32].rearrange("a b c -> (a b) c")
                        ).then_inc(sem[f"d_pt{half}"], 16)
                    if i >= 2:
                        s.wait_ge(sem["s_x1v"], i - 1)
                    s.dma_start(xpair[p][:, 0:H], x_own[i * 128:(i + 1) * 128, :]
                                ).then_inc(sem[f"d_xo2{p}"], 16)

            @blk.tensor
            def _(t):
                t.wait_ge(sem["d_wvold"], 16)
                t.wait_ge(sem["d_misc"], 32)
                for i in range(NT):
                    p = i % 2
                    if i >= 2:
                        t.wait_ge(sem["s_ctx"], i - 1)
                    for half in range(2):
                        hx = 2 * i + half
                        t.wait_ge(sem[f"d_pt{half}"], 16 * (i + 1))
                        for jj in range(32):
                            j = 32 * half + jj
                            if i == 0:
                                t.wait_ge(sem["d_xt"], 16 * (j // 8 + 1))
                            stat = pt_sb[half][:, jj * 128:(jj + 1) * 128]
                            t.matmul(ctx_ps[p][0][:, :], stat, arena[:, j * H: j * H + 512],
                                     start=(j == 0), stop=(j == 63))
                            mm = t.matmul(ctx_ps[p][1][:, :], stat,
                                          arena[:, j * H + 512:(j + 1) * H],
                                          start=(j == 0), stop=(j == 63))
                            if jj == 31:
                                mm.then_inc(sem["s_pvh"], 1)
                    t.wait_ge(sem["s_ctx"], i + 1)
                    if i >= 1:
                        t.wait_ge(sem["s_trsV"], i)
                    for c in range(8):
                        tr = t.transpose(trs_ps[:, c * 128:(c + 1) * 128],
                                         ctxX_sb[:, c * 128:(c + 1) * 128], identb[:, :])
                        if c == 7:
                            tr.then_inc(sem["s_trsT"], 1)
                    t.wait_ge(sem["s_trsV"], i + 1)
                    if i >= 1:
                        t.wait_ge(sem["s_x1v"], i)
                    for g in range(8):
                        stat = ctxXf_sb[:, g * 128:(g + 1) * 128]
                        t.matmul(attn_ps[0][:, :], stat, wvoS[:, g * H: g * H + 512],
                                 start=(g == 0), stop=(g == 7))
                        mm = t.matmul(attn_ps[1][:, :], stat, wvoS[:, g * H + 512:(g + 1) * H],
                                      start=(g == 0), stop=(g == 7))
                        if g == 7:
                            mm.then_inc(sem["s_op"], 1)

            @blk.vector
            def _(v):
                for i in range(NT):
                    v.reciprocal(sinv_all[:, 4 * i:4 * i + 1], s_all[:, 4 * i:4 * i + 1])
                for i in range(NT):
                    p = i % 2
                    k = NLN + i
                    v.wait_ge(sem["s_pvh"], 2 * (i + 1))
                    v.tensor_scalar(ctxX_sb[:, 0:512], ctx_ps[p][0][:, :],
                                    sinv_all[:, 4 * i:4 * i + 1], None, OP.mult)
                    v.tensor_scalar(ctxX_sb[:, 512:H], ctx_ps[p][1][:, :],
                                    sinv_all[:, 4 * i:4 * i + 1], None, OP.mult).then_inc(sem["s_ctx"], 1)
                    v.wait_ge(sem["s_trsT"], i + 1)
                    v.tensor_copy(ctxXf_sb[:, :], trs_ps[:, :]).then_inc(sem["s_trsV"], 1)
                    v.wait_ge(sem["s_op"], i + 1)
                    v.wait_ge(sem[f"d_xo2{p}"], 16 * (i // 2 + 1))
                    if i >= 1:
                        v.wait_ge(sem["d_x1st"], 16 * max(0, i - 1))
                        v.wait_ge(sem["s_sq"], k)
                    v.tensor_add(x1_sb[:, 0:512], attn_ps[0][:, :], xpair[p][:, 0:512])
                    v.tensor_add(x1_sb[:, 512:H], attn_ps[1][:, :],
                                 xpair[p][:, 512:H]).then_inc(sem["s_x1v"], 1)
                    ln2_vec(v, i)

            @blk.scalar
            def _(scl):
                for i in range(NT):
                    k = NLN + i
                    scl.wait_ge(sem["s_x1v"], i + 1)
                    if i >= 1:
                        scl.dma_start(s_x1[i], x1_sb[:, :]).then_inc(sem["d_x1st"], 16)
                    ln2_scal(scl, i)

        rstk2.close()
        pstk2.close()
        arena_stk.close()

        cstk = contextlib.ExitStack()
        ln1f = cstk.enter_context(nc.sbuf_tensor("ln1f", [128, 8 * QR], BF16))
        wrTS = cstk.enter_context(nc.sbuf_tensor("wrTS", [128, 8 * H], BF16))
        wkfTS = cstk.enter_context(nc.sbuf_tensor("wkfTS", [128, 8 * H], BF16))
        wvfTS = cstk.enter_context(nc.sbuf_tensor("wvfTS", [128, 8 * H], BF16))
        d_sb = cstk.enter_context(nc.sbuf_tensor("d_sb", [128, 8 * 512], BF16))
        xr_sb = cstk.enter_context(nc.sbuf_tensor("xr_sb", [128, 8 * 512], BF16))
        xk_sb = cstk.enter_context(nc.sbuf_tensor("xk_sb", [128, 8 * 512], BF16))
        k_act = cstk.enter_context(nc.sbuf_tensor("k_act", [128, 8 * 512], BF16))
        rf = cstk.enter_context(nc.sbuf_tensor("rf", [128, 8 * 512], F32))
        relu_tmp = [cstk.enter_context(nc.sbuf_tensor(f"relu{p}", [128, 512], F32)) for p in range(2)]
        ffn_sb = cstk.enter_context(nc.sbuf_tensor("ffn_sb", [128, 8 * 512], F32))

        identf = cstk.enter_context(nc.sbuf_tensor("identf", [128, 128], F32))
        mu_sb = cstk.enter_context(nc.sbuf_tensor("mu_sb", [128, 16], F32))
        psC = [cstk.enter_context(nc.psum_tensor(f"psC{i}", [128, 512], F32)) for i in range(4)]
        fout_ps = [[cstk.enter_context(nc.psum_tensor(f"fout{p}{h}", [128, 512], F32))
                    for h in range(2)] for p in range(2)]

        if phases >= 4:
          with nc.named_scope("phC"), nc.Block() as blk:
            @blk.sync
            def _(s):
                for s4 in range(4):
                    s.wait_ge(sem[f"d_st{s4}"],
                              16 * (stcnt(NLN, s4) + len([j for j in range(NT) if j % 4 == s4])))
                for c in range(8):
                    s.dma_start_transpose(ln1f[:, c * QR:(c + 1) * QR],
                                          s_ln1.ap()[:, c * 128:(c + 1) * 128]
                                          ).then_inc(sem["d_lnf"], 16)
                s.wait_ge(sem["d_x1st"], 16 * 16)
                for u in range(16):
                    p = u % 2
                    if u >= 2:
                        s.wait_ge(sem["s_outv"], u - 1)
                    s.dma_start(xpair[p][:, 0:H], s_x1[u + 1]).then_inc(sem[f"d_x13{p}"], 16)

            @blk.gpsimd
            def _(g):
                g.dma_start(mu_sb[:, 0:8], mu_r_in[:, :]).then_inc(sem["d_wf"], 16)
                g.dma_start(mu_sb[:, 8:16], mu_k_in[:, :]).then_inc(sem["d_wf"], 16)
                g.dma_start(identf[:, :], identf_in[:, :]).then_inc(sem["d_wf"], 16)
                for w_dram, w_sb in ((wrT, wrTS), (wkfT, wkfTS), (wvfT, wvfTS)):
                    g.dma_start(w_sb.rearrange("p (c f) -> p c f", f=H),
                                w_dram.rearrange("(c p) f -> p c f", p=128)
                                ).then_inc(sem["d_wf"], 16)

            @blk.tensor
            def _(t):
                t.wait_ge(sem["d_wf"], 16 * 4)
                for stt in range(4):
                    t.wait_ge(sem["s_xr"], stt + 1)
                    for d in range(8):
                        dd = 8 * stt + d
                        if dd >= 2:
                            t.wait_ge(sem["s_sig"], dd - 1)
                        for c in range(8):
                            mm = t.matmul(psC[d % 2][:, :],
                                          wrTS[:, c * H + d * 128: c * H + (d + 1) * 128],
                                          xr_sb[:, c * 512:(c + 1) * 512],
                                          start=(c == 0), stop=(c == 7))
                            if c == 7:
                                mm.then_inc(sem["s_rp"], 1)
                    if stt == 0:
                        t.wait_ge(sem["d_wf"], 16 * 5)
                    t.wait_ge(sem["s_xk"], stt + 1)
                    for d in range(8):
                        dd = 8 * stt + d
                        if dd >= 2:
                            t.wait_ge(sem["s_rel"], dd - 1)
                        for c in range(8):
                            mm = t.matmul(psC[2 + d % 2][:, :],
                                          wkfTS[:, c * H + d * 128: c * H + (d + 1) * 128],
                                          xk_sb[:, c * 512:(c + 1) * 512],
                                          start=(c == 0), stop=(c == 7))
                            if c == 7:
                                mm.then_inc(sem["s_kp"], 1)
                    if stt == 0:
                        t.wait_ge(sem["d_wf"], 16 * 6)
                    t.wait_ge(sem["s_ka"], 8 * (stt + 1))
                    t.wait_ge(sem["s_sig"], 8 * (stt + 1))
                    for e in range(8):
                        ee = 8 * stt + e
                        if ee >= 2:
                            t.wait_ge(sem["s_ff"], ee - 1)
                        for dd2 in range(8):
                            mm = t.matmul(psC[e % 2][:, :],
                                          wvfTS[:, dd2 * H + e * 128: dd2 * H + (e + 1) * 128],
                                          k_act[:, dd2 * 512:(dd2 + 1) * 512],
                                          start=(dd2 == 0), stop=(dd2 == 7))
                            if dd2 == 7:
                                mm.then_inc(sem["s_vp"], 1)
                    t.wait_ge(sem["s_ff"], 8 * (stt + 1))
                    for sub in range(4):
                        u = 4 * stt + sub
                        if u >= 2:
                            t.wait_ge(sem["s_outv"], u - 1)
                        for e in range(8):
                            tr = t.transpose(fout_ps[u % 2][e // 4][:, (e % 4) * 128:(e % 4 + 1) * 128],
                                             ffn_sb[:, e * 512 + sub * 128: e * 512 + sub * 128 + 128],
                                             identf[:, :])
                            if e == 7:
                                tr.then_inc(sem["s_tr3"], 1)

            @blk.vector
            def _(v):
                v.wait_ge(sem["d_lnf"], 16 * 8)
                v.wait_ge(sem["d_misc"], 32)
                v.wait_ge(sem["d_wf"], 16 * 6)
                for c in range(8):
                    mk = v.tensor_mul(ln1f[:, c * QR + 127: c * QR + 128],
                                      ln1f[:, c * QR + 127: c * QR + 128], hmask[:, 0:1])
                    if c == 7:
                        mk.then_inc(sem["s_mask"], 1)
                v.wait_ge(sem["d_wf"], 16 * 3)
                for stt in range(4):
                    if stt >= 1:
                        v.wait_ge(sem["s_rp"], 8 * stt)
                    for c in range(8):
                        cur = ln1f[:, c * QR + 128 + stt * 512: c * QR + 128 + stt * 512 + 512]
                        prev = ln1f[:, c * QR + 127 + stt * 512: c * QR + 127 + stt * 512 + 512]
                        v.tensor_sub(d_sb[:, c * 512:(c + 1) * 512], cur, prev)
                        sr = v.scalar_tensor_tensor(xr_sb[:, c * 512:(c + 1) * 512],
                                                    d_sb[:, c * 512:(c + 1) * 512],
                                                    mu_sb[:, c:c + 1], prev, OP.mult, OP.add)
                        if c == 7:
                            sr.then_inc(sem["s_xr"], 1)
                    if stt >= 1:
                        v.wait_ge(sem["s_kp"], 8 * stt)
                    for c in range(8):
                        prev = ln1f[:, c * QR + 127 + stt * 512: c * QR + 127 + stt * 512 + 512]
                        stx = v.scalar_tensor_tensor(xk_sb[:, c * 512:(c + 1) * 512],
                                                     d_sb[:, c * 512:(c + 1) * 512],
                                                     mu_sb[:, 8 + c:9 + c], prev, OP.mult, OP.add)
                        if c == 7:
                            stx.then_inc(sem["s_xk"], 1)
                    for d in range(8):
                        dd = 8 * stt + d
                        v.wait_ge(sem["s_rel"], dd + 1)
                        ka = v.scalar_tensor_tensor(k_act[:, d * 512:(d + 1) * 512],
                                                    relu_tmp[d % 2][:, :], 1.0,
                                                    relu_tmp[d % 2][:, :], OP.mult, OP.mult)
                        ka.then_inc(sem["s_ka"], 1)
                    for e in range(8):
                        ee = 8 * stt + e
                        v.wait_ge(sem["s_vp"], ee + 1)
                        fe = v.tensor_tensor(ffn_sb[:, e * 512:(e + 1) * 512],
                                             psC[e % 2][:, :], rf[:, e * 512:(e + 1) * 512],
                                             OP.mult)
                        fe.then_inc(sem["s_ff"], 1)
                    for sub in range(4):
                        u = 4 * stt + sub
                        p = u % 2
                        v.wait_ge(sem["s_tr3"], u + 1)
                        v.wait_ge(sem[f"d_x13{p}"], 16 * (u // 2 + 1))
                        if u >= 2:
                            v.wait_ge(sem[f"d_ost{p}"], 16 * (u // 2))
                        v.tensor_add(out_sb[p][:, 0:512], fout_ps[p][0][:, :],
                                     xpair[p][:, 0:512])
                        v.tensor_add(out_sb[p][:, 512:H], fout_ps[p][1][:, :],
                                     xpair[p][:, 512:H]).then_inc(sem["s_outv"], 1)

            @blk.scalar
            def _(scl):
                for stt in range(4):
                    for d in range(8):
                        dd = 8 * stt + d
                        scl.wait_ge(sem["s_rp"], dd + 1)
                        if d == 0 and stt >= 1:
                            scl.wait_ge(sem["s_ff"], 8 * stt)
                        scl.activation(rf[:, d * 512:(d + 1) * 512], psC[d % 2][:, :],
                                       AF.Sigmoid).then_inc(sem["s_sig"], 1)
                    for d in range(8):
                        dd = 8 * stt + d
                        scl.wait_ge(sem["s_kp"], dd + 1)
                        if dd >= 2:
                            scl.wait_ge(sem["s_ka"], dd - 1)
                        scl.activation(relu_tmp[d % 2][:, :], psC[2 + d % 2][:, :],
                                       AF.Relu).then_inc(sem["s_rel"], 1)
                    for sub in range(4):
                        u = 4 * stt + sub
                        scl.wait_ge(sem["s_outv"], u + 1)
                        scl.dma_start(o_out[u * 128:(u + 1) * 128, :],
                                      out_sb[u % 2][:, :]).then_inc(sem[f"d_ost{u % 2}"], 16)
                scl.wait_ge(sem["d_ost0"], 16 * 8)
                scl.wait_ge(sem["d_ost1"], 16 * 8)

        cstk.close()

    return nc


_NC_CACHE = [None]


def kernel(L0_attn_x, L0_attn_W_qkv, L0_attn_W_out, L0_ffn_W_r, L0_ffn_W_k,
           L0_ffn_W_v, L0_ffn_mu_r, L0_ffn_mu_k):
    x = np.ascontiguousarray(np.asarray(L0_attn_x, dtype=np.float32))
    wqkv = np.asarray(L0_attn_W_qkv, dtype=np.float32)
    wout = np.asarray(L0_attn_W_out, dtype=np.float32)
    wr = np.asarray(L0_ffn_W_r, dtype=np.float32)
    wkf = np.asarray(L0_ffn_W_k, dtype=np.float32)
    wvf = np.asarray(L0_ffn_W_v, dtype=np.float32)
    mur = np.asarray(L0_ffn_mu_r, dtype=np.float32)
    muk = np.asarray(L0_ffn_mu_k, dtype=np.float32)

    wq_h = np.ascontiguousarray(wqkv[0::3, :])
    wk_h = np.ascontiguousarray(wqkv[1::3, :])
    wv_h = np.ascontiguousarray(wqkv[2::3, :])
    woutT_h = np.ascontiguousarray(wout.T)
    wrT_h = np.ascontiguousarray(wr.T)
    wkfT_h = np.ascontiguousarray(wkf.T)
    wvfT_h = np.ascontiguousarray(wvf.T)
    mur_h = np.ascontiguousarray(mur.reshape(8, 128).T)
    muk_h = np.ascontiguousarray(muk.reshape(8, 128).T)
    identb_h = np.eye(128, dtype=np.float32).astype(ml_dtypes.bfloat16)
    identf_h = np.eye(128, dtype=np.float32)

    if _NC_CACHE[0] is None:
        _NC_CACHE[0] = build_nc()
    nc = _NC_CACHE[0]

    in_maps = []
    for c in range(8):
        b, m = c // 4, c % 4
        s0 = 2048 * m
        if m == 0:
            xo = np.concatenate([x[b, 0:128], x[b, 0:2048]], axis=0)
            hm = np.zeros((128, 1), np.float32)
        else:
            xo = x[b, s0 - 128:s0 + 2048]
            hm = np.ones((128, 1), np.float32)
        in_maps.append({
            "x_batch": x[b],
            "x_own": np.ascontiguousarray(xo),
            "wq": wq_h, "wk": wk_h, "wv": wv_h, "woutT": woutT_h,
            "wrT": wrT_h, "wkfT": wkfT_h, "wvfT": wvfT_h,
            "mu_r_in": mur_h, "mu_k_in": muk_h,
            "identb_in": identb_h, "identf_in": identf_h,
            "hmask_in": hm,
        })

    res = run_bass_kernel_spmd(nc, in_maps, core_ids=list(range(8)))
    out = np.empty((2, 8192, 1024), np.float32)
    for c in range(8):
        b, m = c // 4, c % 4
        out[b, 2048 * m:2048 * (m + 1), :] = res.results[c]["o_out"]
    return out


# version 07
# speedup vs baseline: 1.2253x; 1.0400x over previous
import contextlib
import numpy as np
import ml_dtypes
import concourse.bass as bass
import concourse.mybir as mybir
from concourse.bass_utils import run_bass_kernel_spmd

F32 = mybir.dt.float32
BF16 = mybir.dt.bfloat16
AF = mybir.ActivationFunctionType
AX = mybir.AxisListType.X
OP = mybir.AluOpType

H = 1024
S = 8192
NT = 17
NB = 64
NLN = NT + NB
QR = NT * 128
EPS = 1e-5
INV_H = 1.0 / H
NCHUNKS = [(0, 512), (512, 512), (1024, 512), (1536, 512), (2048, 128)]


def build_nc(phases=4, dbg=False):
    nc = bass.Bass("TRN2", target_bir_lowering=False, detect_race_conditions=False)

    x_batch = nc.dram_tensor("x_batch", [S, H], F32, kind="ExternalInput")
    x_own = nc.dram_tensor("x_own", [QR, H], F32, kind="ExternalInput")
    wq = nc.dram_tensor("wq", [H, H], F32, kind="ExternalInput")
    wk = nc.dram_tensor("wk", [H, H], F32, kind="ExternalInput")
    wv = nc.dram_tensor("wv", [H, H], F32, kind="ExternalInput")
    woutT = nc.dram_tensor("woutT", [H, H], F32, kind="ExternalInput")
    wrT = nc.dram_tensor("wrT", [H, H], F32, kind="ExternalInput")
    wkfT = nc.dram_tensor("wkfT", [H, H], F32, kind="ExternalInput")
    wvfT = nc.dram_tensor("wvfT", [H, H], F32, kind="ExternalInput")
    mu_r_in = nc.dram_tensor("mu_r_in", [128, 8], F32, kind="ExternalInput")
    mu_k_in = nc.dram_tensor("mu_k_in", [128, 8], F32, kind="ExternalInput")
    identb_in = nc.dram_tensor("identb_in", [128, 128], BF16, kind="ExternalInput")
    identf_in = nc.dram_tensor("identf_in", [128, 128], F32, kind="ExternalInput")
    hmask_in = nc.dram_tensor("hmask_in", [128, 1], F32, kind="ExternalInput")

    o_out = nc.dram_tensor("o_out", [16 * 128, H], F32, kind="ExternalOutput")

    KD = "ExternalOutput" if dbg else "Internal"
    s_lnown = nc.dram_tensor("s_lnown", [QR, H], BF16, kind=KD)
    s_xt = nc.dram_tensor("s_xt", [S, H], BF16, kind=KD)
    s_p = nc.dram_tensor("s_p", [NT, 64, 128, 128], BF16, kind=KD)
    s_x1 = nc.dram_tensor("s_x1", [NT, 128, H], F32, kind=KD)
    s_ln1 = nc.dram_tensor("s_ln1", [QR, H], BF16, kind=KD)
    s_negb = nc.dram_tensor("s_negb", [NT, 128, 1], F32, kind=KD)
    s_qpf = nc.dram_tensor("s_qpf", [8, 128, QR], BF16, kind=KD)
    s_wvo = nc.dram_tensor("s_wvo", [128, 8 * H], BF16, kind="Internal")

    with contextlib.ExitStack() as st:
        xpair = [st.enter_context(nc.sbuf_tensor(f"xpair{p}", [128, 2 * H], F32)) for p in range(3)]
        dst_sb = [st.enter_context(nc.sbuf_tensor(f"dstsb{p}", [128, H], BF16)) for p in range(4)]
        scrf32 = [st.enter_context(nc.sbuf_tensor(f"scrf32{p}", [128, H], F32)) for p in range(2)]
        sq_dummy, x1_sb = scrf32[0], scrf32[1]
        out_sb = scrf32
        ctxX_sb = st.enter_context(nc.sbuf_tensor("ctxX", [128, H], BF16))
        ctxXf_sb = st.enter_context(nc.sbuf_tensor("ctxXf", [128, H], BF16))
        identb = st.enter_context(nc.sbuf_tensor("identb", [128, 128], BF16))
        ones_bf = st.enter_context(nc.sbuf_tensor("ones_bf", [128, 1], BF16))
        negb_cols = st.enter_context(nc.sbuf_tensor("negb_cols", [128, 4 * NT], F32))
        s_all = st.enter_context(nc.sbuf_tensor("s_all", [128, 4 * NT], F32))
        s_part = st.enter_context(nc.sbuf_tensor("s_part", [128, 16], F32))
        sinv_all = st.enter_context(nc.sbuf_tensor("sinv_all", [128, 4 * NT], F32))
        hmask = st.enter_context(nc.sbuf_tensor("hmask", [128, 1], F32))
        c_ssq = st.enter_context(nc.sbuf_tensor("c_ssq", [128, 32], F32))
        c_ngs = st.enter_context(nc.sbuf_tensor("c_ngs", [128, 32], F32))
        c_nm = st.enter_context(nc.sbuf_tensor("c_nm", [128, 32], F32))
        c_t1 = st.enter_context(nc.sbuf_tensor("c_t1", [128, 32], F32))
        c_m2 = st.enter_context(nc.sbuf_tensor("c_m2", [128, 32], F32))
        c_vpe = st.enter_context(nc.sbuf_tensor("c_vpe", [128, 32], F32))
        c_rin = st.enter_context(nc.sbuf_tensor("c_rin", [128, 32], F32))
        c_rsd = st.enter_context(nc.sbuf_tensor("c_rsd", [128, 32], F32))

        sem_names = ["d_ld0", "d_ld1", "d_ld2", "d_ld3", "d_ld4", "d_ld5", "d_ld6", "d_ld7",
                     "d_st0", "d_st1", "d_st2", "d_st3", "d_st4", "d_st5", "d_st6", "d_st7", "d_wmov", "d_wcol0",
                     "d_wcol1", "d_misc", "d_wf", "d_xfo", "d_xf", "d_xt",
                     "d_nb", "d_wvost0", "d_wvost1", "d_wvold", "d_pst0", "d_pst1", "d_pst2", "d_pst3",
                     "d_pt0", "d_pt1", "d_xo20", "d_xo21", "d_x1st", "d_lnf",
                     "d_x130", "d_x131", "d_ost0", "d_ost1", "d_qst0", "d_qst1",
                     "d_q2l0", "d_q2l1", "d_qt0", "d_qt1", "s_pvh",
                     "s_sq", "s_var", "s_rstd", "s_app",
                     "s_fmm", "s_fev", "s_qmm", "s_qev", "s_q2", "s_bmm",
                     "s_bsq", "s_bneg", "s_sc", "s_exp", "s_sv",
                     "s_pv", "s_ctx", "s_trsT", "s_trsV", "s_op", "s_x1v",
                     "s_mask", "s_xr", "s_xk", "s_rp", "s_sig", "s_kp", "s_rel",
                     "s_ka", "s_vp", "s_ff", "s_tr3", "s_outv"]
        sem = {n: st.enter_context(nc.semaphore(n)) for n in sem_names}

        NG = (NLN + 3) // 4

        def stcnt(upto, s):
            return len([j for j in range(upto) if j % 4 == s])

        def pair_of(k):
            if k < 16:
                return k // 2, k % 2
            if k == 16:
                return 8, 0
            return 9 + (k - 17) // 2, (k - 17) % 2

        def xtile(k):
            pi, off = pair_of(k)
            return xpair[pi % 3][:, off * H:(off + 1) * H]

        def ld_wait(eng, k):
            pi, _ = pair_of(k)
            eng.wait_ge(sem[f"d_ld{pi % 3}"], 16 * (pi // 3 + 1))

        def grp_tiles(g):
            return range(4 * g, min(4 * g + 4, NLN))

        def cslot(k):
            return 4 * (k % 8)

        def cs4(t, g):
            base = 4 * ((4 * g) % 8)
            n = len(list(grp_tiles(g)))
            return t[:, base:base + 4 * n:4]

        def ln_dst_dram(k):
            if k < NT:
                return s_lnown[k * 128:(k + 1) * 128, :]
            if k < NLN:
                kb = k - NT
                return s_xt[kb * 128:(kb + 1) * 128, :]
            kc = k - NLN
            return s_ln1[kc * 128:(kc + 1) * 128, :]

        def ln2_vec(v, i):
            k = NLN + i
            q = 0
            v.reduce_sum(c_ngs[:, q:q + 1], x1_sb[:, :], axis=AX, negate=True)
            v.wait_ge(sem["s_sq"], k + 1)
            v.tensor_scalar(c_t1[:, q:q + 1], c_ssq[:, q:q + 1], INV_H, EPS, OP.mult, OP.add)
            v.tensor_scalar_mul(c_nm[:, q:q + 1], c_ngs[:, q:q + 1], INV_H)
            v.tensor_copy(c_ngs[:, 2:3], c_t1[:, q:q + 1])
            v.tensor_mul(c_m2[:, q:q + 1], c_nm[:, q:q + 1], c_nm[:, q:q + 1])
            v.tensor_copy(c_ngs[:, 3:4], c_t1[:, q:q + 1])
            v.tensor_sub(c_vpe[:, q:q + 1], c_t1[:, q:q + 1], c_m2[:, q:q + 1])
            v.tensor_copy(c_ngs[:, 6:7], c_t1[:, q:q + 1])
            v.reciprocal(c_rin[:, q:q + 1], c_vpe[:, q:q + 1]).then_inc(sem["s_var"], 1)
            v.wait_ge(sem["s_rstd"], NG + i + 1)
            ds = i % 4
            if i >= 4:
                v.wait_ge(sem[f"d_st{ds}"], 16 * (stcnt(NLN, ds) + len([j for j in range(i - 3) if j % 4 == ds])))
            v.tensor_scalar(dst_sb[ds][:, :], x1_sb[:, :],
                            c_nm[:, 0:1], c_rsd[:, 0:1],
                            OP.add, OP.mult).then_inc(sem["s_app"], 1)

        def ln2_scal(scl, i):
            k = NLN + i
            scl.activation(sq_dummy[:, :], x1_sb[:, :], AF.Square,
                           accum_out=c_ssq[:, 0:1]).then_inc(sem["s_sq"], 1)
            scl.wait_ge(sem["s_var"], NG + i + 1)
            scl.activation(c_rsd[:, 0:1], c_rin[:, 0:1],
                           AF.Sqrt).then_inc(sem["s_rstd"], 1)
            scl.wait_ge(sem["s_app"], k + 1)
            scl.dma_start(ln_dst_dram(k), dst_sb[i % 4][:, :]).then_inc(sem[f"d_st{i % 4}"], 16)

        rstk = contextlib.ExitStack()
        qpf_tile = [rstk.enter_context(nc.sbuf_tensor(f"qpt{p}", [128, 8 * 128], BF16, side="right"))
                    for p in range(2)]
        p_sb = rstk.enter_context(nc.sbuf_tensor("p_sb", [128, S], BF16, side="right"))

        pstk = contextlib.ExitStack()
        ps8 = [pstk.enter_context(nc.psum_tensor(f"ps8_{i}", [128, 512], F32)) for i in range(8)]

        prostk = contextlib.ExitStack()
        wqkS = prostk.enter_context(nc.sbuf_tensor("wqkS", [128, 8 * H], BF16))
        movstage = prostk.enter_context(nc.sbuf_tensor("movstage", [128, 8 * H], BF16))
        colbuf = [prostk.enter_context(nc.sbuf_tensor(f"colbuf{p}", [128, 8 * 128], BF16)) for p in range(2)]
        wvo_stage = [prostk.enter_context(nc.sbuf_tensor(f"wvostage{p}", [128, H], BF16)) for p in range(8)]
        xfo = prostk.enter_context(nc.sbuf_tensor("xfo", [128, 8 * QR], BF16))
        q2buf = [prostk.enter_context(nc.sbuf_tensor(f"q2buf{p}", [128, QR], BF16)) for p in range(2)]
        qstage = [prostk.enter_context(nc.sbuf_tensor(f"qstage{p}", [128, 512], BF16)) for p in range(10)]
        q2ld = [prostk.enter_context(nc.sbuf_tensor(f"q2ld{p}", [128, QR], BF16)) for p in range(2)]
        b_row = prostk.enter_context(nc.sbuf_tensor("b_row", [1, QR], F32))

        with nc.named_scope("phP"), nc.Block() as blk:
            @blk.sync
            def _(s):
                pairs = ([(2 * m, 2) for m in range(8)] + [(16, 1)]
                         + [(17 + 2 * m, 2) for m in range(32)])
                for pi, (k0, ntl) in enumerate(pairs):
                    if pi >= 3:
                        prev_k0, prev_n = pairs[pi - 3]
                        s.wait_ge(sem["s_app"], prev_k0 + prev_n)
                    if k0 < NT:
                        src = x_own[k0 * 128:(k0 + ntl) * 128, :]
                    else:
                        kb = k0 - NT
                        src = x_batch[kb * 128:(kb + ntl) * 128, :]
                    if ntl == 2:
                        s.dma_start(
                            xpair[pi % 3].rearrange("p (o f) -> p o f", f=H),
                            src.rearrange("(o p) f -> p o f", p=128)
                        ).then_inc(sem[f"d_ld{pi % 3}"], 16)
                    else:
                        s.dma_start(xpair[pi % 3][:, 0:H], src
                                    ).then_inc(sem[f"d_ld{pi % 3}"], 16)

            @blk.gpsimd
            def _(g):
                g.dma_start(movstage.rearrange("p (c f) -> p c f", f=H),
                            wk.rearrange("(c p) f -> p c f", p=128)).then_inc(sem["d_wmov"], 16)
                for gg in range(8):
                    if gg >= 2:
                        g.wait_ge(sem["s_fmm"], gg - 1)
                    g.dma_start(colbuf[gg % 2].rearrange("p (c f) -> p c f", f=128),
                                wq.rearrange("(c p) f -> p c f", p=128)[:, :, gg * 128:(gg + 1) * 128]
                                ).then_inc(sem[f"d_wcol{gg % 2}"], 16)
                g.dma_start(identb[:, :], identb_in[:, :]).then_inc(sem["d_misc"], 16)
                g.dma_start(hmask[:, :], hmask_in[:, :]).then_inc(sem["d_misc"], 16)
                g.wait_ge(sem["s_fmm"], 8)
                g.dma_start(movstage.rearrange("p (c f) -> p c f", f=H),
                            woutT.rearrange("(c p) f -> p c f", p=128)).then_inc(sem["d_wmov"], 16)
                for gg in range(8):
                    g.wait_ge(sem["s_fmm"], 8 if gg < 2 else 8 + gg - 1)
                    g.dma_start(colbuf[gg % 2].rearrange("p (c f) -> p c f", f=128),
                                wv.rearrange("(c p) f -> p c f", p=128)[:, :, gg * 128:(gg + 1) * 128]
                                ).then_inc(sem[f"d_wcol{gg % 2}"], 16)
                for gp in range(8):
                    for ncx, (off, nw) in enumerate(NCHUNKS):
                        idx = gp * 5 + ncx
                        g.wait_ge(sem["s_qev"], idx + 1)
                        g.dma_start(s_qpf[gp][:, off:off + nw],
                                    qstage[(gp % 2) * 5 + ncx][:, 0:nw]).then_inc(sem[f"d_qst{gp % 2}"], 16)
                g.wait_ge(sem["d_qst0"], 16 * 20)
                g.wait_ge(sem["d_qst1"], 16 * 20)
                for gp in range(8):
                    if gp >= 2:
                        g.wait_ge(sem["s_q2"], gp - 1)
                    g.dma_start(q2ld[gp % 2][:, :], s_qpf[gp]).then_inc(sem[f"d_q2l{gp % 2}"], 16)

            @blk.tensor
            def _(t):
                t.wait_ge(sem["d_wmov"], 16)
                for gg in range(8):
                    t.wait_ge(sem[f"d_wcol{gg % 2}"], 16 * (gg // 2 + 1))
                    if gg >= 2:
                        t.wait_ge(sem["s_fev"], gg - 1)
                    pe, po = ps8[(gg % 2) * 2], ps8[(gg % 2) * 2 + 1]
                    cb = colbuf[gg % 2]
                    for ig in range(8):
                        stat = cb[:, ig * 128:(ig + 1) * 128]
                        t.matmul(pe[:, :], stat, movstage[:, ig * H:ig * H + 512],
                                 start=(ig == 0), stop=(ig == 7))
                        mm = t.matmul(po[:, :], stat, movstage[:, ig * H + 512:(ig + 1) * H],
                                      start=(ig == 0), stop=(ig == 7))
                        if ig == 7:
                            mm.then_inc(sem["s_fmm"], 1)
                t.wait_ge(sem["d_wmov"], 32)
                for gg in range(8):
                    t.wait_ge(sem[f"d_wcol{gg % 2}"], 16 * (4 + gg // 2 + 1))
                    if gg >= 2:
                        t.wait_ge(sem["s_fev"], 8 + gg - 1)
                    pe, po = ps8[(gg % 2) * 2], ps8[(gg % 2) * 2 + 1]
                    cb = colbuf[gg % 2]
                    for ig in range(8):
                        stat = cb[:, ig * 128:(ig + 1) * 128]
                        t.matmul(pe[:, :], stat, movstage[:, ig * H:ig * H + 512],
                                 start=(ig == 0), stop=(ig == 7))
                        mm = t.matmul(po[:, :], stat, movstage[:, ig * H + 512:(ig + 1) * H],
                                      start=(ig == 0), stop=(ig == 7))
                        if ig == 7:
                            mm.then_inc(sem["s_fmm"], 1)
                t.wait_ge(sem["d_xfo"], 16 * 8)
                t.wait_ge(sem["s_fev"], 16)
                for gp in range(8):
                    if gp >= 1:
                        t.wait_ge(sem["s_qev"], 5 * gp)
                    for hg in range(8):
                        stat = wqkS[:, hg * H + gp * 128: hg * H + (gp + 1) * 128]
                        for ncx, (off, nw) in enumerate(NCHUNKS):
                            mm = t.matmul(ps8[ncx][:, 0:nw], stat,
                                          xfo[:, hg * QR + off: hg * QR + off + nw],
                                          start=(hg == 0), stop=(hg == 7))
                            if hg == 7 and ncx == 4:
                                mm.then_inc(sem["s_qmm"], 1)
                t.wait_ge(sem["s_qev"], 40)
                for gp in range(8):
                    t.wait_ge(sem["s_q2"], gp + 1)
                    for ncx, (off, nw) in enumerate(NCHUNKS):
                        mm = t.matmul(ps8[ncx][0:1, 0:nw], ones_bf[:, 0:1],
                                      q2buf[gp % 2][:, off:off + nw],
                                      start=(gp == 0), stop=(gp == 7))
                        if ncx == 4:
                            mm.then_inc(sem["s_bmm"], 1)

            @blk.vector
            def _(v):
                v.memset(ones_bf[:, :], 1.0)
                v.memset(negb_cols[:, :], 0.0)

                evq = []
                for gg in range(8):
                    def ev_wqk(v, gg=gg):
                        v.wait_ge(sem["s_fmm"], gg + 1)
                        pe, po = ps8[(gg % 2) * 2], ps8[(gg % 2) * 2 + 1]
                        v.tensor_scalar(wqkS[:, gg * H:gg * H + 512], pe[:, :],
                                        1.0 / 32.0, None, OP.mult)
                        v.tensor_scalar(wqkS[:, gg * H + 512:(gg + 1) * H], po[:, :],
                                        1.0 / 32.0, None, OP.mult).then_inc(sem["s_fev"], 1)
                    evq.append(ev_wqk)
                for gg in range(8):
                    def ev_wvo(v, gg=gg):
                        v.wait_ge(sem["s_fmm"], 8 + gg + 1)
                        pe, po = ps8[(gg % 2) * 2], ps8[(gg % 2) * 2 + 1]
                        v.tensor_copy(wvo_stage[gg][:, 0:512], pe[:, :])
                        v.tensor_copy(wvo_stage[gg][:, 512:H], po[:, :]).then_inc(sem["s_fev"], 1)
                    evq.append(ev_wvo)
                for gp in range(8):
                    def ev_qp(v, gp=gp):
                        v.wait_ge(sem["s_qmm"], gp + 1)
                        if gp >= 2:
                            v.wait_ge(sem[f"d_qst{gp % 2}"], 16 * 5 * (gp // 2))
                        for ncx, (off, nw) in enumerate(NCHUNKS):
                            v.tensor_copy(qstage[(gp % 2) * 5 + ncx][:, 0:nw],
                                          ps8[ncx][:, 0:nw]).then_inc(sem["s_qev"], 1)
                    evq.append(ev_qp)

                def v_apply(v, k, g):
                    ds = k % 4
                    v.wait_ge(sem["s_rstd"], g + 1)
                    if k >= 4:
                        v.wait_ge(sem[f"d_st{ds}"], 16 * stcnt(k - 3, ds))
                    v.tensor_scalar(dst_sb[ds][:, :], xtile(k)[:, :],
                                    c_nm[:, cslot(k):cslot(k) + 1],
                                    c_rsd[:, cslot(k):cslot(k) + 1],
                                    OP.add, OP.mult).then_inc(sem["s_app"], 1)

                for g in range(NG):
                    if g >= 1:
                        for k in grp_tiles(g - 1):
                            v_apply(v, k, g - 1)
                    if g >= 5:
                        for _ in range(2):
                            if evq:
                                evq.pop(0)(v)
                    for k in grp_tiles(g):
                        ld_wait(v, k)
                        v.reduce_sum(c_ngs[:, cslot(k):cslot(k) + 1],
                                     xtile(k)[:, :], axis=AX, negate=True)
                    v.wait_ge(sem["s_sq"], list(grp_tiles(g))[-1] + 1)
                    v.tensor_scalar(cs4(c_t1, g), cs4(c_ssq, g), INV_H, EPS, OP.mult, OP.add)
                    v.tensor_scalar_mul(cs4(c_nm, g), cs4(c_ngs, g), INV_H)
                    v.tensor_copy(c_ngs[:, 2:3], c_t1[:, 0:1])
                    v.tensor_mul(cs4(c_m2, g), cs4(c_nm, g), cs4(c_nm, g))
                    v.tensor_copy(c_ngs[:, 3:4], c_t1[:, 0:1])
                    v.tensor_sub(cs4(c_vpe, g), cs4(c_t1, g), cs4(c_m2, g))
                    v.tensor_copy(c_ngs[:, 6:7], c_t1[:, 0:1])
                    v.reciprocal(cs4(c_rin, g), cs4(c_vpe, g)).then_inc(sem["s_var"], 1)
                for k in grp_tiles(NG - 1):
                    v_apply(v, k, NG - 1)
                while evq:
                    evq.pop(0)(v)
                v.wait_ge(sem["d_nb"], 16 * 18)
                v.tensor_scalar_mul(negb_cols[:, :], negb_cols[:, :], -1.0
                                    ).then_inc(sem["s_bneg"], 1)

            @blk.scalar
            def _(scl):
                def emit_store(scl, k):
                    ds = k % 4
                    scl.wait_ge(sem["s_app"], k + 1)
                    scl.dma_start(ln_dst_dram(k), dst_sb[ds][:, :]).then_inc(sem[f"d_st{ds}"], 16)

                popmap = {}

                def ev_xfo(scl):
                    for s4 in range(4):
                        scl.wait_ge(sem[f"d_st{s4}"], 16 * stcnt(NT, s4))
                    for g in range(8):
                        scl.dma_start_transpose(
                            xfo[:, g * QR:(g + 1) * QR],
                            s_lnown.ap()[:, g * 128:(g + 1) * 128]
                        ).then_inc(sem["d_xfo"], 16)
                popmap[6] = [ev_xfo]

                for gg in range(8):
                    def ev_wvost(scl, gg=gg):
                        scl.wait_ge(sem["s_fev"], 8 + gg + 1)
                        scl.dma_start(s_wvo.ap()[:, gg * H:(gg + 1) * H],
                                      wvo_stage[gg][:, :]).then_inc(sem[f"d_wvost{gg % 2}"], 16)
                    popmap.setdefault(10 + gg // 2, []).append(ev_wvost)

                for gp in range(8):
                    def ev_q2(scl, gp=gp):
                        scl.wait_ge(sem[f"d_q2l{gp % 2}"], 16 * (gp // 2 + 1))
                        scl.activation(q2buf[gp % 2][:, :], q2ld[gp % 2][:, :],
                                       AF.Square).then_inc(sem["s_q2"], 1)
                    popmap.setdefault(17 + gp // 2, []).append(ev_q2)

                def ev_negb_s(scl):
                    scl.wait_ge(sem["s_bmm"], 8)
                    for ncx, (off, nw) in enumerate(NCHUNKS):
                        scl.activation(b_row[0:1, off:off + nw], ps8[ncx][0:1, 0:nw],
                                       AF.Sqrt, scale=float(H)).then_inc(sem["s_bsq"], 1)
                    scl.wait_ge(sem["s_bsq"], 5)
                    scl.dma_start(s_negb.ap().rearrange("a p o -> o (a p)"),
                                  b_row[0:1, :]).then_inc(sem["d_nb"], 16)
                    scl.wait_ge(sem["d_nb"], 16)
                    for i in range(NT):
                        scl.dma_start(negb_cols[:, 4 * i:4 * i + 1], s_negb[i]).then_inc(sem["d_nb"], 16)

                for g in range(NG):
                    if g >= 1:
                        scl.wait_ge(sem["s_var"], g)
                        scl.activation(cs4(c_rsd, g - 1), cs4(c_rin, g - 1),
                                       AF.Sqrt).then_inc(sem["s_rstd"], 1)
                        for k in grp_tiles(g - 1):
                            emit_store(scl, k)
                    for k in grp_tiles(g):
                        ld_wait(scl, k)
                        scl.activation(sq_dummy[:, :], xtile(k)[:, :], AF.Square,
                                       accum_out=c_ssq[:, cslot(k):cslot(k) + 1]
                                       ).then_inc(sem["s_sq"], 1)
                    for fn in popmap.pop(g, []):
                        fn(scl)
                g = NG - 1
                scl.wait_ge(sem["s_var"], NG)
                scl.activation(cs4(c_rsd, g), cs4(c_rin, g),
                               AF.Sqrt).then_inc(sem["s_rstd"], 1)
                for k in grp_tiles(g):
                    emit_store(scl, k)
                for g in sorted(popmap):
                    for fn in popmap.pop(g, []):
                        fn(scl)
                ev_negb_s(scl)

        prostk.close()

        arena_stk = contextlib.ExitStack()
        arena = arena_stk.enter_context(nc.sbuf_tensor("arena", [128, 8 * S], BF16))

        if phases >= 2:
          with nc.named_scope("phA"), nc.Block() as blk:
            @blk.sync
            def _(s):
                for s4 in range(4):
                    s.wait_ge(sem[f"d_st{s4}"], 16 * stcnt(NLN, s4))
                for g in range(8):
                    s.dma_start_transpose(arena[:, g * S:(g + 1) * S],
                                          s_xt.ap()[:, g * 128:(g + 1) * 128]
                                          ).then_inc(sem["d_xf"], 16)
                for i in range(NT):
                    p = i % 2
                    if i >= 2:
                        s.wait_ge(sem["s_sc"], 4 * (i - 1))
                    s.dma_start(qpf_tile[p].rearrange("p (g s) -> p g s", s=128),
                                s_qpf.ap()[:, :, i * 128:(i + 1) * 128].transpose([1, 0, 2])
                                ).then_inc(sem[f"d_qt{p}"], 16)

            @blk.tensor
            def _(t):
                t.wait_ge(sem["d_xf"], 16 * 8)
                for i in range(NT):
                    p = i % 2
                    t.wait_ge(sem[f"d_qt{p}"], 16 * (i // 2 + 1))
                    for q in range(4):
                        gq = 4 * i + q
                        grp = ps8[(q % 2) * 4:(q % 2) * 4 + 4]
                        if gq >= 2:
                            t.wait_ge(sem["s_exp"], gq - 1)
                        for gp in range(8):
                            stat = qpf_tile[p][:, gp * 128:(gp + 1) * 128]
                            for j in range(4):
                                kb = q * 2048 + j * 512
                                mm = t.matmul(grp[j][:, :], stat,
                                              arena[:, gp * S + kb: gp * S + kb + 512],
                                              start=(gp == 0), stop=(gp == 7))
                                if gp == 7 and j == 3:
                                    mm.then_inc(sem["s_sc"], 1)

            @blk.scalar
            def _(scl):
                scl.wait_ge(sem["s_bneg"], 1)
                for i in range(NT):
                    for q in range(4):
                        gq = 4 * i + q
                        grp = ps8[(q % 2) * 4:(q % 2) * 4 + 4]
                        scl.wait_ge(sem["s_sc"], gq + 1)
                        if i >= 1:
                            scl.wait_ge(sem[f"d_pst{q}"], 64 * i)
                            scl.wait_ge(sem["s_sv"], 5 * (i - 1) + q + 1)
                        for j in range(4):
                            kb = q * 2048 + j * 512
                            act = scl.activation(p_sb[:, kb:kb + 512], grp[j][:, :],
                                                 AF.Exp, bias=negb_cols[:, 4 * i:4 * i + 1])
                            if j == 3:
                                act.then_inc(sem["s_exp"], 1)
                    scl.wait_ge(sem["s_exp"], 4 * (i + 1))
                    for w in range(16):
                        scl.dma_start(
                            s_p[i, 4 * w:4 * w + 4].transpose([1, 0, 2]),
                            p_sb.rearrange("s (j t) -> s j t", t=128)[:, 4 * w:4 * w + 4, :]
                        ).then_inc(sem[f"d_pst{w // 4}"], 16)

            @blk.vector
            def _(v):
                for i in range(NT):
                    for q in range(4):
                        v.wait_ge(sem["s_exp"], 4 * i + q + 1)
                        v.reduce_sum(s_part[:, 4 * q:4 * q + 1],
                                     p_sb[:, q * 2048:(q + 1) * 2048], axis=AX
                                     ).then_inc(sem["s_sv"], 1)
                    v.tensor_copy(s_part[:, 1:2], s_part[:, 0:1])
                    v.reduce_sum(s_all[:, 4 * i:4 * i + 1], s_part[:, 0:13:4], axis=AX
                                 ).then_inc(sem["s_sv"], 1)

        rstk.close()
        pstk.close()

        rstk2 = contextlib.ExitStack()
        pt_sb = [rstk2.enter_context(nc.sbuf_tensor(f"pt{p}", [128, S // 2], BF16, side="right"))
                 for p in range(2)]
        wvoS = rstk2.enter_context(nc.sbuf_tensor("wvoS", [128, 8 * H], BF16, side="right"))

        pstk2 = contextlib.ExitStack()
        ctx_ps = [[pstk2.enter_context(nc.psum_tensor(f"ctx{p}{h}", [128, 512], F32))
                   for h in range(2)] for p in range(2)]
        trs_ps = pstk2.enter_context(nc.psum_tensor("trs", [128, H], BF16))
        attn_ps = [pstk2.enter_context(nc.psum_tensor(f"attn{h}", [128, 512], F32))
                   for h in range(2)]

        if phases >= 3:
          with nc.named_scope("phB"), nc.Block() as blk:
            @blk.sync
            def _(s):
                for c in range(8):
                    s.dma_start(arena[:, c * S:(c + 1) * S].rearrange("p (g f) -> p g f", f=H),
                                s_xt.rearrange("(g p) f -> p g f", p=128)[:, c * 8:(c + 1) * 8, :]
                                ).then_inc(sem["d_xt"], 16)
                s.wait_ge(sem["d_wvost0"], 16 * 4)
                s.wait_ge(sem["d_wvost1"], 16 * 4)
                s.dma_start(wvoS[:, :], s_wvo.ap()).then_inc(sem["d_wvold"], 16)
                for i in range(NT):
                    p = i % 2
                    for qq in range(4):
                        s.wait_ge(sem[f"d_pst{qq}"], 64 * (i + 1))
                    for half in range(2):
                        hx = 2 * i + half
                        if hx >= 2:
                            s.wait_ge(sem["s_pvh"], hx - 1)
                        s.dma_start_transpose(
                            pt_sb[half][:, :],
                            s_p[i, 32 * half:32 * half + 32].rearrange("a b c -> (a b) c")
                        ).then_inc(sem[f"d_pt{half}"], 16)
                    if i >= 2:
                        s.wait_ge(sem["s_x1v"], i - 1)
                    s.dma_start(xpair[p][:, 0:H], x_own[i * 128:(i + 1) * 128, :]
                                ).then_inc(sem[f"d_xo2{p}"], 16)

            @blk.tensor
            def _(t):
                t.wait_ge(sem["d_wvold"], 16)
                t.wait_ge(sem["d_misc"], 32)
                for i in range(NT):
                    p = i % 2
                    if i >= 2:
                        t.wait_ge(sem["s_ctx"], i - 1)
                    for half in range(2):
                        hx = 2 * i + half
                        t.wait_ge(sem[f"d_pt{half}"], 16 * (i + 1))
                        for jj in range(32):
                            j = 32 * half + jj
                            if i == 0:
                                t.wait_ge(sem["d_xt"], 16 * (j // 8 + 1))
                            stat = pt_sb[half][:, jj * 128:(jj + 1) * 128]
                            t.matmul(ctx_ps[p][0][:, :], stat, arena[:, j * H: j * H + 512],
                                     start=(j == 0), stop=(j == 63))
                            mm = t.matmul(ctx_ps[p][1][:, :], stat,
                                          arena[:, j * H + 512:(j + 1) * H],
                                          start=(j == 0), stop=(j == 63))
                            if jj == 31:
                                mm.then_inc(sem["s_pvh"], 1)
                    t.wait_ge(sem["s_ctx"], i + 1)
                    if i >= 1:
                        t.wait_ge(sem["s_trsV"], i)
                    for c in range(8):
                        tr = t.transpose(trs_ps[:, c * 128:(c + 1) * 128],
                                         ctxX_sb[:, c * 128:(c + 1) * 128], identb[:, :])
                        if c == 7:
                            tr.then_inc(sem["s_trsT"], 1)
                    t.wait_ge(sem["s_trsV"], i + 1)
                    if i >= 1:
                        t.wait_ge(sem["s_x1v"], i)
                    for g in range(8):
                        stat = ctxXf_sb[:, g * 128:(g + 1) * 128]
                        t.matmul(attn_ps[0][:, :], stat, wvoS[:, g * H: g * H + 512],
                                 start=(g == 0), stop=(g == 7))
                        mm = t.matmul(attn_ps[1][:, :], stat, wvoS[:, g * H + 512:(g + 1) * H],
                                      start=(g == 0), stop=(g == 7))
                        if g == 7:
                            mm.then_inc(sem["s_op"], 1)

            @blk.vector
            def _(v):
                for i in range(NT):
                    v.reciprocal(sinv_all[:, 4 * i:4 * i + 1], s_all[:, 4 * i:4 * i + 1])
                for i in range(NT):
                    p = i % 2
                    k = NLN + i
                    v.wait_ge(sem["s_pvh"], 2 * (i + 1))
                    v.tensor_scalar(ctxX_sb[:, 0:512], ctx_ps[p][0][:, :],
                                    sinv_all[:, 4 * i:4 * i + 1], None, OP.mult)
                    v.tensor_scalar(ctxX_sb[:, 512:H], ctx_ps[p][1][:, :],
                                    sinv_all[:, 4 * i:4 * i + 1], None, OP.mult).then_inc(sem["s_ctx"], 1)
                    v.wait_ge(sem["s_trsT"], i + 1)
                    v.tensor_copy(ctxXf_sb[:, :], trs_ps[:, :]).then_inc(sem["s_trsV"], 1)
                    v.wait_ge(sem["s_op"], i + 1)
                    v.wait_ge(sem[f"d_xo2{p}"], 16 * (i // 2 + 1))
                    if i >= 1:
                        v.wait_ge(sem["d_x1st"], 16 * max(0, i - 1))
                        v.wait_ge(sem["s_sq"], k)
                    v.tensor_add(x1_sb[:, 0:512], attn_ps[0][:, :], xpair[p][:, 0:512])
                    v.tensor_add(x1_sb[:, 512:H], attn_ps[1][:, :],
                                 xpair[p][:, 512:H]).then_inc(sem["s_x1v"], 1)
                    ln2_vec(v, i)

            @blk.scalar
            def _(scl):
                for i in range(NT):
                    k = NLN + i
                    scl.wait_ge(sem["s_x1v"], i + 1)
                    if i >= 1:
                        scl.dma_start(s_x1[i], x1_sb[:, :]).then_inc(sem["d_x1st"], 16)
                    ln2_scal(scl, i)

        rstk2.close()
        pstk2.close()
        arena_stk.close()

        cstk = contextlib.ExitStack()
        ln1f = cstk.enter_context(nc.sbuf_tensor("ln1f", [128, 8 * QR], BF16))
        wrTS = cstk.enter_context(nc.sbuf_tensor("wrTS", [128, 8 * H], BF16))
        wkfTS = cstk.enter_context(nc.sbuf_tensor("wkfTS", [128, 8 * H], BF16))
        wvfTS = cstk.enter_context(nc.sbuf_tensor("wvfTS", [128, 8 * H], BF16))
        d_sb = cstk.enter_context(nc.sbuf_tensor("d_sb", [128, 8 * 512], BF16))
        xr_sb = cstk.enter_context(nc.sbuf_tensor("xr_sb", [128, 8 * 512], BF16))
        xk_sb = cstk.enter_context(nc.sbuf_tensor("xk_sb", [128, 8 * 512], BF16))
        k_act = cstk.enter_context(nc.sbuf_tensor("k_act", [128, 8 * 512], BF16))
        rf = cstk.enter_context(nc.sbuf_tensor("rf", [128, 8 * 512], F32))
        relu_tmp = [cstk.enter_context(nc.sbuf_tensor(f"relu{p}", [128, 512], F32)) for p in range(2)]
        ffn_sb = cstk.enter_context(nc.sbuf_tensor("ffn_sb", [128, 8 * 512], F32))

        identf = cstk.enter_context(nc.sbuf_tensor("identf", [128, 128], F32))
        mu_sb = cstk.enter_context(nc.sbuf_tensor("mu_sb", [128, 16], F32))
        psC = [cstk.enter_context(nc.psum_tensor(f"psC{i}", [128, 512], F32)) for i in range(4)]
        fout_ps = [[cstk.enter_context(nc.psum_tensor(f"fout{p}{h}", [128, 512], F32))
                    for h in range(2)] for p in range(2)]

        if phases >= 4:
          with nc.named_scope("phC"), nc.Block() as blk:
            @blk.sync
            def _(s):
                for s4 in range(4):
                    s.wait_ge(sem[f"d_st{s4}"],
                              16 * (stcnt(NLN, s4) + len([j for j in range(NT) if j % 4 == s4])))
                for c in range(8):
                    s.dma_start_transpose(ln1f[:, c * QR:(c + 1) * QR],
                                          s_ln1.ap()[:, c * 128:(c + 1) * 128]
                                          ).then_inc(sem["d_lnf"], 16)
                s.wait_ge(sem["d_x1st"], 16 * 16)
                for u in range(16):
                    p = u % 2
                    if u >= 2:
                        s.wait_ge(sem["s_outv"], u - 1)
                    s.dma_start(xpair[p][:, 0:H], s_x1[u + 1]).then_inc(sem[f"d_x13{p}"], 16)

            @blk.gpsimd
            def _(g):
                g.dma_start(mu_sb[:, 0:8], mu_r_in[:, :]).then_inc(sem["d_wf"], 16)
                g.dma_start(mu_sb[:, 8:16], mu_k_in[:, :]).then_inc(sem["d_wf"], 16)
                g.dma_start(identf[:, :], identf_in[:, :]).then_inc(sem["d_wf"], 16)
                for w_dram, w_sb in ((wrT, wrTS), (wkfT, wkfTS), (wvfT, wvfTS)):
                    g.dma_start(w_sb.rearrange("p (c f) -> p c f", f=H),
                                w_dram.rearrange("(c p) f -> p c f", p=128)
                                ).then_inc(sem["d_wf"], 16)

            @blk.tensor
            def _(t):
                t.wait_ge(sem["d_wf"], 16 * 4)
                for stt in range(4):
                    t.wait_ge(sem["s_xr"], stt + 1)
                    for d in range(8):
                        dd = 8 * stt + d
                        if dd >= 2:
                            t.wait_ge(sem["s_sig"], dd - 1)
                        for c in range(8):
                            mm = t.matmul(psC[d % 2][:, :],
                                          wrTS[:, c * H + d * 128: c * H + (d + 1) * 128],
                                          xr_sb[:, c * 512:(c + 1) * 512],
                                          start=(c == 0), stop=(c == 7))
                            if c == 7:
                                mm.then_inc(sem["s_rp"], 1)
                    if stt == 0:
                        t.wait_ge(sem["d_wf"], 16 * 5)
                    t.wait_ge(sem["s_xk"], stt + 1)
                    for d in range(8):
                        dd = 8 * stt + d
                        if dd >= 2:
                            t.wait_ge(sem["s_rel"], dd - 1)
                        for c in range(8):
                            mm = t.matmul(psC[2 + d % 2][:, :],
                                          wkfTS[:, c * H + d * 128: c * H + (d + 1) * 128],
                                          xk_sb[:, c * 512:(c + 1) * 512],
                                          start=(c == 0), stop=(c == 7))
                            if c == 7:
                                mm.then_inc(sem["s_kp"], 1)
                    if stt == 0:
                        t.wait_ge(sem["d_wf"], 16 * 6)
                    t.wait_ge(sem["s_ka"], 8 * (stt + 1))
                    t.wait_ge(sem["s_sig"], 8 * (stt + 1))
                    for e in range(8):
                        ee = 8 * stt + e
                        if ee >= 2:
                            t.wait_ge(sem["s_ff"], ee - 1)
                        for dd2 in range(8):
                            mm = t.matmul(psC[e % 2][:, :],
                                          wvfTS[:, dd2 * H + e * 128: dd2 * H + (e + 1) * 128],
                                          k_act[:, dd2 * 512:(dd2 + 1) * 512],
                                          start=(dd2 == 0), stop=(dd2 == 7))
                            if dd2 == 7:
                                mm.then_inc(sem["s_vp"], 1)
                    t.wait_ge(sem["s_ff"], 8 * (stt + 1))
                    for sub in range(4):
                        u = 4 * stt + sub
                        if u >= 2:
                            t.wait_ge(sem["s_outv"], u - 1)
                        for e in range(8):
                            tr = t.transpose(fout_ps[u % 2][e // 4][:, (e % 4) * 128:(e % 4 + 1) * 128],
                                             ffn_sb[:, e * 512 + sub * 128: e * 512 + sub * 128 + 128],
                                             identf[:, :])
                            if e == 7:
                                tr.then_inc(sem["s_tr3"], 1)

            @blk.vector
            def _(v):
                v.wait_ge(sem["d_lnf"], 16 * 8)
                v.wait_ge(sem["d_misc"], 32)
                v.wait_ge(sem["d_wf"], 16 * 6)
                for c in range(8):
                    mk = v.tensor_mul(ln1f[:, c * QR + 127: c * QR + 128],
                                      ln1f[:, c * QR + 127: c * QR + 128], hmask[:, 0:1])
                    if c == 7:
                        mk.then_inc(sem["s_mask"], 1)
                v.wait_ge(sem["d_wf"], 16 * 3)
                for stt in range(4):
                    if stt >= 1:
                        v.wait_ge(sem["s_rp"], 8 * stt)
                    for c in range(8):
                        cur = ln1f[:, c * QR + 128 + stt * 512: c * QR + 128 + stt * 512 + 512]
                        prev = ln1f[:, c * QR + 127 + stt * 512: c * QR + 127 + stt * 512 + 512]
                        v.tensor_sub(d_sb[:, c * 512:(c + 1) * 512], cur, prev)
                        sr = v.scalar_tensor_tensor(xr_sb[:, c * 512:(c + 1) * 512],
                                                    d_sb[:, c * 512:(c + 1) * 512],
                                                    mu_sb[:, c:c + 1], prev, OP.mult, OP.add)
                        if c == 7:
                            sr.then_inc(sem["s_xr"], 1)
                    if stt >= 1:
                        v.wait_ge(sem["s_kp"], 8 * stt)
                    for c in range(8):
                        prev = ln1f[:, c * QR + 127 + stt * 512: c * QR + 127 + stt * 512 + 512]
                        stx = v.scalar_tensor_tensor(xk_sb[:, c * 512:(c + 1) * 512],
                                                     d_sb[:, c * 512:(c + 1) * 512],
                                                     mu_sb[:, 8 + c:9 + c], prev, OP.mult, OP.add)
                        if c == 7:
                            stx.then_inc(sem["s_xk"], 1)
                    for d in range(8):
                        dd = 8 * stt + d
                        v.wait_ge(sem["s_rel"], dd + 1)
                        ka = v.scalar_tensor_tensor(k_act[:, d * 512:(d + 1) * 512],
                                                    relu_tmp[d % 2][:, :], 1.0,
                                                    relu_tmp[d % 2][:, :], OP.mult, OP.mult)
                        ka.then_inc(sem["s_ka"], 1)
                    for e in range(8):
                        ee = 8 * stt + e
                        v.wait_ge(sem["s_vp"], ee + 1)
                        fe = v.tensor_tensor(ffn_sb[:, e * 512:(e + 1) * 512],
                                             psC[e % 2][:, :], rf[:, e * 512:(e + 1) * 512],
                                             OP.mult)
                        fe.then_inc(sem["s_ff"], 1)
                    for sub in range(4):
                        u = 4 * stt + sub
                        p = u % 2
                        v.wait_ge(sem["s_tr3"], u + 1)
                        v.wait_ge(sem[f"d_x13{p}"], 16 * (u // 2 + 1))
                        if u >= 2:
                            v.wait_ge(sem[f"d_ost{p}"], 16 * (u // 2))
                        v.tensor_add(out_sb[p][:, 0:512], fout_ps[p][0][:, :],
                                     xpair[p][:, 0:512])
                        v.tensor_add(out_sb[p][:, 512:H], fout_ps[p][1][:, :],
                                     xpair[p][:, 512:H]).then_inc(sem["s_outv"], 1)

            @blk.scalar
            def _(scl):
                for stt in range(4):
                    for d in range(8):
                        dd = 8 * stt + d
                        scl.wait_ge(sem["s_rp"], dd + 1)
                        if d == 0 and stt >= 1:
                            scl.wait_ge(sem["s_ff"], 8 * stt)
                        scl.activation(rf[:, d * 512:(d + 1) * 512], psC[d % 2][:, :],
                                       AF.Sigmoid).then_inc(sem["s_sig"], 1)
                    for d in range(8):
                        dd = 8 * stt + d
                        scl.wait_ge(sem["s_kp"], dd + 1)
                        if dd >= 2:
                            scl.wait_ge(sem["s_ka"], dd - 1)
                        scl.activation(relu_tmp[d % 2][:, :], psC[2 + d % 2][:, :],
                                       AF.Relu).then_inc(sem["s_rel"], 1)
                    for sub in range(4):
                        u = 4 * stt + sub
                        scl.wait_ge(sem["s_outv"], u + 1)
                        scl.dma_start(o_out[u * 128:(u + 1) * 128, :],
                                      out_sb[u % 2][:, :]).then_inc(sem[f"d_ost{u % 2}"], 16)
                scl.wait_ge(sem["d_ost0"], 16 * 8)
                scl.wait_ge(sem["d_ost1"], 16 * 8)

        cstk.close()

    return nc


_NC_CACHE = [None]


def kernel(L0_attn_x, L0_attn_W_qkv, L0_attn_W_out, L0_ffn_W_r, L0_ffn_W_k,
           L0_ffn_W_v, L0_ffn_mu_r, L0_ffn_mu_k):
    x = np.ascontiguousarray(np.asarray(L0_attn_x, dtype=np.float32))
    wqkv = np.asarray(L0_attn_W_qkv, dtype=np.float32)
    wout = np.asarray(L0_attn_W_out, dtype=np.float32)
    wr = np.asarray(L0_ffn_W_r, dtype=np.float32)
    wkf = np.asarray(L0_ffn_W_k, dtype=np.float32)
    wvf = np.asarray(L0_ffn_W_v, dtype=np.float32)
    mur = np.asarray(L0_ffn_mu_r, dtype=np.float32)
    muk = np.asarray(L0_ffn_mu_k, dtype=np.float32)

    wq_h = np.ascontiguousarray(wqkv[0::3, :])
    wk_h = np.ascontiguousarray(wqkv[1::3, :])
    wv_h = np.ascontiguousarray(wqkv[2::3, :])
    woutT_h = np.ascontiguousarray(wout.T)
    wrT_h = np.ascontiguousarray(wr.T)
    wkfT_h = np.ascontiguousarray(wkf.T)
    wvfT_h = np.ascontiguousarray(wvf.T)
    mur_h = np.ascontiguousarray(mur.reshape(8, 128).T)
    muk_h = np.ascontiguousarray(muk.reshape(8, 128).T)
    identb_h = np.eye(128, dtype=np.float32).astype(ml_dtypes.bfloat16)
    identf_h = np.eye(128, dtype=np.float32)

    if _NC_CACHE[0] is None:
        _NC_CACHE[0] = build_nc()
    nc = _NC_CACHE[0]

    in_maps = []
    for c in range(8):
        b, m = c // 4, c % 4
        s0 = 2048 * m
        if m == 0:
            xo = np.concatenate([x[b, 0:128], x[b, 0:2048]], axis=0)
            hm = np.zeros((128, 1), np.float32)
        else:
            xo = x[b, s0 - 128:s0 + 2048]
            hm = np.ones((128, 1), np.float32)
        in_maps.append({
            "x_batch": x[b],
            "x_own": np.ascontiguousarray(xo),
            "wq": wq_h, "wk": wk_h, "wv": wv_h, "woutT": woutT_h,
            "wrT": wrT_h, "wkfT": wkfT_h, "wvfT": wvfT_h,
            "mu_r_in": mur_h, "mu_k_in": muk_h,
            "identb_in": identb_h, "identf_in": identf_h,
            "hmask_in": hm,
        })

    res = run_bass_kernel_spmd(nc, in_maps, core_ids=list(range(8)))
    out = np.empty((2, 8192, 1024), np.float32)
    for c in range(8):
        b, m = c // 4, c % 4
        out[b, 2048 * m:2048 * (m + 1), :] = res.results[c]["o_out"]
    return out


# version 10
# speedup vs baseline: 1.2489x; 1.0193x over previous
import contextlib
import numpy as np
import ml_dtypes
import concourse.bass as bass
import concourse.mybir as mybir
from concourse.bass_utils import run_bass_kernel_spmd

F32 = mybir.dt.float32
BF16 = mybir.dt.bfloat16
AF = mybir.ActivationFunctionType
AX = mybir.AxisListType.X
OP = mybir.AluOpType

H = 1024
S = 8192
NT = 17
NB = 64
NLN = NT + NB
QR = NT * 128
EPS = 1e-5
INV_H = 1.0 / H
NCHUNKS = [(0, 512), (512, 512), (1024, 512), (1536, 512), (2048, 128)]


def build_nc(phases=4, dbg=False):
    nc = bass.Bass("TRN2", target_bir_lowering=False, detect_race_conditions=False)

    x_batch = nc.dram_tensor("x_batch", [S, H], F32, kind="ExternalInput")
    x_own = nc.dram_tensor("x_own", [QR, H], F32, kind="ExternalInput")
    wq = nc.dram_tensor("wq", [H, H], F32, kind="ExternalInput")
    wk = nc.dram_tensor("wk", [H, H], F32, kind="ExternalInput")
    wv = nc.dram_tensor("wv", [H, H], F32, kind="ExternalInput")
    woutT = nc.dram_tensor("woutT", [H, H], F32, kind="ExternalInput")
    wrT = nc.dram_tensor("wrT", [H, H], F32, kind="ExternalInput")
    wkfT = nc.dram_tensor("wkfT", [H, H], F32, kind="ExternalInput")
    wvfT = nc.dram_tensor("wvfT", [H, H], F32, kind="ExternalInput")
    mu_r_in = nc.dram_tensor("mu_r_in", [128, 8], F32, kind="ExternalInput")
    mu_k_in = nc.dram_tensor("mu_k_in", [128, 8], F32, kind="ExternalInput")
    identb_in = nc.dram_tensor("identb_in", [128, 128], BF16, kind="ExternalInput")
    identf_in = nc.dram_tensor("identf_in", [128, 128], F32, kind="ExternalInput")
    hmask_in = nc.dram_tensor("hmask_in", [128, 1], F32, kind="ExternalInput")

    o_out = nc.dram_tensor("o_out", [16 * 128, H], F32, kind="ExternalOutput")

    KD = "ExternalOutput" if dbg else "Internal"
    s_lnown = nc.dram_tensor("s_lnown", [QR, H], BF16, kind=KD)
    s_xt = nc.dram_tensor("s_xt", [S, H], BF16, kind=KD)
    s_p = nc.dram_tensor("s_p", [NT, 128, 8192], BF16, kind=KD)
    s_x1 = nc.dram_tensor("s_x1", [NT, 128, H], F32, kind=KD)
    s_ln1 = nc.dram_tensor("s_ln1", [QR, H], BF16, kind=KD)
    s_negb = nc.dram_tensor("s_negb", [NT, 128, 1], F32, kind=KD)
    s_qpf = nc.dram_tensor("s_qpf", [8, 128, QR], BF16, kind=KD)
    s_wvo = nc.dram_tensor("s_wvo", [128, 8 * H], BF16, kind="Internal")

    with contextlib.ExitStack() as st:
        xpair = [st.enter_context(nc.sbuf_tensor(f"xpair{p}", [128, 2 * H], F32)) for p in range(3)]
        dst_sb = [st.enter_context(nc.sbuf_tensor(f"dstsb{p}", [128, H], BF16)) for p in range(4)]
        scrf32 = [st.enter_context(nc.sbuf_tensor(f"scrf32{p}", [128, H], F32)) for p in range(2)]
        sq_dummy, x1_sb = scrf32[0], scrf32[1]
        out_sb = scrf32
        ctxX_sb = st.enter_context(nc.sbuf_tensor("ctxX", [128, H], BF16))
        ctxXf_sb = st.enter_context(nc.sbuf_tensor("ctxXf", [128, H], BF16))
        identb = st.enter_context(nc.sbuf_tensor("identb", [128, 128], BF16))
        ones_bf = st.enter_context(nc.sbuf_tensor("ones_bf", [128, 1], BF16))
        negb_cols = st.enter_context(nc.sbuf_tensor("negb_cols", [128, 4 * NT], F32))
        s_all = st.enter_context(nc.sbuf_tensor("s_all", [128, 4 * NT], F32))
        s_part = st.enter_context(nc.sbuf_tensor("s_part", [128, 16], F32))
        sinv_all = st.enter_context(nc.sbuf_tensor("sinv_all", [128, 4 * NT], F32))
        hmask = st.enter_context(nc.sbuf_tensor("hmask", [128, 1], F32))
        c_ssq = st.enter_context(nc.sbuf_tensor("c_ssq", [128, 32], F32))
        c_ngs = st.enter_context(nc.sbuf_tensor("c_ngs", [128, 32], F32))
        c_nm = st.enter_context(nc.sbuf_tensor("c_nm", [128, 32], F32))
        c_t1 = st.enter_context(nc.sbuf_tensor("c_t1", [128, 32], F32))
        c_m2 = st.enter_context(nc.sbuf_tensor("c_m2", [128, 32], F32))
        c_vpe = st.enter_context(nc.sbuf_tensor("c_vpe", [128, 32], F32))
        c_rin = st.enter_context(nc.sbuf_tensor("c_rin", [128, 32], F32))
        c_rsd = st.enter_context(nc.sbuf_tensor("c_rsd", [128, 32], F32))

        sem_names = ["d_ld0", "d_ld1", "d_ld2", "d_ld3", "d_ld4", "d_ld5", "d_ld6", "d_ld7",
                     "d_st0", "d_st1", "d_st2", "d_st3", "d_st4", "d_st5", "d_st6", "d_st7", "d_wmov", "d_wcol0",
                     "d_wcol1", "d_misc", "d_wf", "d_xfo", "d_xf", "d_xf2", "d_xt",
                     "d_nb", "d_wvost0", "d_wvost1", "d_wvold", "d_pst0", "d_pst1", "d_pst2", "d_pst3",
                     "d_pt0", "d_pt1", "d_xo20", "d_xo21", "d_x1st", "d_lnf",
                     "d_x130", "d_x131", "d_ost0", "d_ost1", "d_qst0", "d_qst1", "d_qst2",
                     "d_q2l0", "d_q2l1", "d_qt0", "d_qt1", "s_pvh",
                     "s_sq", "s_var", "s_rstd", "s_app",
                     "s_fmm", "s_fev", "s_qmm", "s_qev", "s_q2", "s_bmm",
                     "s_bsq", "s_bneg", "s_sc", "s_exp", "s_sv",
                     "s_pv", "s_ctx", "s_trsT", "s_trsV", "s_op", "s_x1v",
                     "s_mask", "s_xr", "s_xk", "s_rp", "s_sig", "s_kp", "s_rel",
                     "s_ka", "s_vp", "s_ff", "s_tr3", "s_outv"]
        sem = {n: st.enter_context(nc.semaphore(n)) for n in sem_names}

        NG = (NLN + 3) // 4

        def stcnt(upto, s):
            return len([j for j in range(upto) if j % 4 == s])

        def pair_of(k):
            if k < 16:
                return k // 2, k % 2
            if k == 16:
                return 8, 0
            return 9 + (k - 17) // 2, (k - 17) % 2

        def xtile(k):
            pi, off = pair_of(k)
            return xpair[pi % 3][:, off * H:(off + 1) * H]

        def ld_wait(eng, k):
            pi, _ = pair_of(k)
            eng.wait_ge(sem[f"d_ld{pi % 3}"], 16 * (pi // 3 + 1))

        def grp_tiles(g):
            return range(4 * g, min(4 * g + 4, NLN))

        def cslot(k):
            return 4 * (k % 8)

        def cs4(t, g):
            base = 4 * ((4 * g) % 8)
            n = len(list(grp_tiles(g)))
            return t[:, base:base + 4 * n:4]

        def ln_dst_dram(k):
            if k < NT:
                return s_lnown[k * 128:(k + 1) * 128, :]
            if k < NLN:
                kb = k - NT
                return s_xt[kb * 128:(kb + 1) * 128, :]
            kc = k - NLN
            return s_ln1[kc * 128:(kc + 1) * 128, :]

        def ln2_vec(v, i):
            k = NLN + i
            q = 0
            v.reduce_sum(c_ngs[:, q:q + 1], x1_sb[:, :], axis=AX, negate=True)
            v.wait_ge(sem["s_sq"], k + 1)
            v.tensor_scalar(c_t1[:, q:q + 1], c_ssq[:, q:q + 1], INV_H, EPS, OP.mult, OP.add)
            v.tensor_scalar_mul(c_nm[:, q:q + 1], c_ngs[:, q:q + 1], INV_H)
            v.tensor_copy(c_ngs[:, 2:3], c_t1[:, q:q + 1])
            v.tensor_mul(c_m2[:, q:q + 1], c_nm[:, q:q + 1], c_nm[:, q:q + 1])
            v.tensor_copy(c_ngs[:, 3:4], c_t1[:, q:q + 1])
            v.tensor_sub(c_vpe[:, q:q + 1], c_t1[:, q:q + 1], c_m2[:, q:q + 1])
            v.tensor_copy(c_ngs[:, 6:7], c_t1[:, q:q + 1])
            v.reciprocal(c_rin[:, q:q + 1], c_vpe[:, q:q + 1]).then_inc(sem["s_var"], 1)
            v.wait_ge(sem["s_rstd"], NG + i + 1)
            ds = i % 4
            if i >= 4:
                v.wait_ge(sem[f"d_st{ds}"], 16 * (stcnt(NLN, ds) + len([j for j in range(i - 3) if j % 4 == ds])))
            v.tensor_scalar(dst_sb[ds][:, :], x1_sb[:, :],
                            c_nm[:, 0:1], c_rsd[:, 0:1],
                            OP.add, OP.mult).then_inc(sem["s_app"], 1)

        def ln2_scal(scl, i):
            k = NLN + i
            scl.activation(sq_dummy[:, :], x1_sb[:, :], AF.Square,
                           accum_out=c_ssq[:, 0:1]).then_inc(sem["s_sq"], 1)
            scl.wait_ge(sem["s_var"], NG + i + 1)
            scl.activation(c_rsd[:, 0:1], c_rin[:, 0:1],
                           AF.Sqrt).then_inc(sem["s_rstd"], 1)
            scl.wait_ge(sem["s_app"], k + 1)
            scl.dma_start(ln_dst_dram(k), dst_sb[i % 4][:, :]).then_inc(sem[f"d_st{i % 4}"], 16)

        rstk = contextlib.ExitStack()
        qpf_tile = [rstk.enter_context(nc.sbuf_tensor(f"qpt{p}", [128, 8 * 128], BF16, side="right"))
                    for p in range(2)]
        p_sb = rstk.enter_context(nc.sbuf_tensor("p_sb", [128, S], BF16, side="right"))

        pstk = contextlib.ExitStack()
        ps8 = [pstk.enter_context(nc.psum_tensor(f"ps8_{i}", [128, 512], F32)) for i in range(8)]

        prostk = contextlib.ExitStack()
        wqkS = prostk.enter_context(nc.sbuf_tensor("wqkS", [128, 8 * H], BF16))
        movstage = prostk.enter_context(nc.sbuf_tensor("movstage", [128, 8 * H], BF16))
        colbuf = [prostk.enter_context(nc.sbuf_tensor(f"colbuf{p}", [128, 8 * 128], BF16)) for p in range(4)]
        wvo_stage = [prostk.enter_context(nc.sbuf_tensor(f"wvostage{p}", [128, H], BF16)) for p in range(8)]
        xfo = prostk.enter_context(nc.sbuf_tensor("xfo", [128, 8 * QR], BF16))
        q2buf = [prostk.enter_context(nc.sbuf_tensor(f"q2buf{p}", [128, QR], BF16)) for p in range(2)]
        qstage = [prostk.enter_context(nc.sbuf_tensor(f"qstage{p}", [128, 512], BF16)) for p in range(15)]
        b_row = prostk.enter_context(nc.sbuf_tensor("b_row", [1, QR], F32))

        with nc.named_scope("phP"), nc.Block() as blk:
            @blk.sync
            def _(s):
                pairs = ([(2 * m, 2) for m in range(8)] + [(16, 1)]
                         + [(17 + 2 * m, 2) for m in range(32)])
                for pi, (k0, ntl) in enumerate(pairs):
                    if pi >= 3:
                        prev_k0, prev_n = pairs[pi - 3]
                        s.wait_ge(sem["s_app"], prev_k0 + prev_n)
                    if k0 < NT:
                        src = x_own[k0 * 128:(k0 + ntl) * 128, :]
                    else:
                        kb = k0 - NT
                        src = x_batch[kb * 128:(kb + ntl) * 128, :]
                    if ntl == 2:
                        s.dma_start(
                            xpair[pi % 3].rearrange("p (o f) -> p o f", f=H),
                            src.rearrange("(o p) f -> p o f", p=128)
                        ).then_inc(sem[f"d_ld{pi % 3}"], 16)
                    else:
                        s.dma_start(xpair[pi % 3][:, 0:H], src
                                    ).then_inc(sem[f"d_ld{pi % 3}"], 16)

            @blk.gpsimd
            def _(g):
                g.dma_start(movstage.rearrange("p (c f) -> p c f", f=H),
                            wk.rearrange("(c p) f -> p c f", p=128)).then_inc(sem["d_wmov"], 16)
                for gg in range(8):
                    if gg >= 4:
                        g.wait_ge(sem["s_fmm"], gg - 3)
                    g.dma_start(colbuf[gg % 4].rearrange("p (c f) -> p c f", f=128),
                                wq.rearrange("(c p) f -> p c f", p=128)[:, :, gg * 128:(gg + 1) * 128]
                                ).then_inc(sem[f"d_wcol{gg % 2}"], 16)
                g.dma_start(identb[:, :], identb_in[:, :]).then_inc(sem["d_misc"], 16)
                g.dma_start(hmask[:, :], hmask_in[:, :]).then_inc(sem["d_misc"], 16)
                g.wait_ge(sem["s_fmm"], 8)
                g.dma_start(movstage.rearrange("p (c f) -> p c f", f=H),
                            woutT.rearrange("(c p) f -> p c f", p=128)).then_inc(sem["d_wmov"], 16)
                for gg in range(8):
                    g.wait_ge(sem["s_fmm"], 8 if gg < 4 else 8 + gg - 3)
                    g.dma_start(colbuf[gg % 4].rearrange("p (c f) -> p c f", f=128),
                                wv.rearrange("(c p) f -> p c f", p=128)[:, :, gg * 128:(gg + 1) * 128]
                                ).then_inc(sem[f"d_wcol{gg % 2}"], 16)
                for gp in range(8):
                    for ncx, (off, nw) in enumerate(NCHUNKS):
                        idx = gp * 5 + ncx
                        g.wait_ge(sem["s_qev"], idx + 1)
                        g.dma_start(s_qpf[gp][:, off:off + nw],
                                    qstage[(gp % 3) * 5 + ncx][:, 0:nw]).then_inc(sem[f"d_qst{gp % 3}"], 16)

            @blk.tensor
            def _(t):
                t.wait_ge(sem["d_wmov"], 16)
                for gg in range(8):
                    t.wait_ge(sem[f"d_wcol{gg % 2}"], 16 * (gg // 2 + 1))
                    if gg >= 2:
                        t.wait_ge(sem["s_fev"], gg - 1)
                    pe, po = ps8[(gg % 2) * 2], ps8[(gg % 2) * 2 + 1]
                    cb = colbuf[gg % 4]
                    for ig in range(8):
                        stat = cb[:, ig * 128:(ig + 1) * 128]
                        t.matmul(pe[:, :], stat, movstage[:, ig * H:ig * H + 512],
                                 start=(ig == 0), stop=(ig == 7))
                        mm = t.matmul(po[:, :], stat, movstage[:, ig * H + 512:(ig + 1) * H],
                                      start=(ig == 0), stop=(ig == 7))
                        if ig == 7:
                            mm.then_inc(sem["s_fmm"], 1)
                t.wait_ge(sem["d_wmov"], 32)
                for gg in range(8):
                    t.wait_ge(sem[f"d_wcol{gg % 2}"], 16 * (4 + gg // 2 + 1))
                    t.wait_ge(sem["s_fev"], (7 + gg) if gg < 2 else (8 + gg - 1))
                    pe, po = ps8[(gg % 2) * 2], ps8[(gg % 2) * 2 + 1]
                    cb = colbuf[gg % 4]
                    for ig in range(8):
                        stat = cb[:, ig * 128:(ig + 1) * 128]
                        t.matmul(pe[:, :], stat, movstage[:, ig * H:ig * H + 512],
                                 start=(ig == 0), stop=(ig == 7))
                        mm = t.matmul(po[:, :], stat, movstage[:, ig * H + 512:(ig + 1) * H],
                                      start=(ig == 0), stop=(ig == 7))
                        if ig == 7:
                            mm.then_inc(sem["s_fmm"], 1)
                t.wait_ge(sem["d_xfo"], 16 * 8)
                t.wait_ge(sem["s_fev"], 16)
                for gp in range(8):
                    if gp >= 1:
                        t.wait_ge(sem["s_qev"], 5 * gp)
                    for hg in range(8):
                        stat = wqkS[:, hg * H + gp * 128: hg * H + (gp + 1) * 128]
                        for ncx, (off, nw) in enumerate(NCHUNKS):
                            mm = t.matmul(ps8[ncx][:, 0:nw], stat,
                                          xfo[:, hg * QR + off: hg * QR + off + nw],
                                          start=(hg == 0), stop=(hg == 7))
                            if hg == 7 and ncx == 4:
                                mm.then_inc(sem["s_qmm"], 1)
                t.wait_ge(sem["s_qev"], 40)
                for gp in range(8):
                    t.wait_ge(sem["s_q2"], 5 * (gp + 1))
                    for ncx, (off, nw) in enumerate(NCHUNKS):
                        mm = t.matmul(ps8[ncx][0:1, 0:nw], ones_bf[:, 0:1],
                                      q2buf[gp % 2][:, off:off + nw],
                                      start=(gp == 0), stop=(gp == 7))
                        if ncx == 4:
                            mm.then_inc(sem["s_bmm"], 1)

            @blk.vector
            def _(v):
                v.memset(ones_bf[:, :], 1.0)
                v.memset(negb_cols[:, :], 0.0)

                evq = []
                for gg in range(8):
                    def ev_wqk(v, gg=gg):
                        v.wait_ge(sem["s_fmm"], gg + 1)
                        pe, po = ps8[(gg % 2) * 2], ps8[(gg % 2) * 2 + 1]
                        v.tensor_scalar(wqkS[:, gg * H:gg * H + 512], pe[:, :],
                                        1.0 / 32.0, None, OP.mult)
                        v.tensor_scalar(wqkS[:, gg * H + 512:(gg + 1) * H], po[:, :],
                                        1.0 / 32.0, None, OP.mult).then_inc(sem["s_fev"], 1)
                    evq.append(ev_wqk)
                for gg in range(8):
                    def ev_wvo(v, gg=gg):
                        v.wait_ge(sem["s_fmm"], 8 + gg + 1)
                        pe, po = ps8[(gg % 2) * 2], ps8[(gg % 2) * 2 + 1]
                        v.tensor_copy(wvo_stage[gg][:, 0:512], pe[:, :])
                        v.tensor_copy(wvo_stage[gg][:, 512:H], po[:, :]).then_inc(sem["s_fev"], 1)
                    evq.append(ev_wvo)
                for gp in range(8):
                    def ev_qp(v, gp=gp):
                        v.wait_ge(sem["s_qmm"], gp + 1)
                        if gp >= 3:
                            v.wait_ge(sem[f"d_qst{gp % 3}"], 16 * 5 * (gp // 3))
                            v.wait_ge(sem["s_q2"], 5 * (gp - 2))
                        for ncx, (off, nw) in enumerate(NCHUNKS):
                            v.tensor_copy(qstage[(gp % 3) * 5 + ncx][:, 0:nw],
                                          ps8[ncx][:, 0:nw]).then_inc(sem["s_qev"], 1)
                    evq.append(ev_qp)

                def v_apply(v, k, g):
                    ds = k % 4
                    v.wait_ge(sem["s_rstd"], g + 1)
                    if k >= 4:
                        v.wait_ge(sem[f"d_st{ds}"], 16 * stcnt(k - 3, ds))
                    v.tensor_scalar(dst_sb[ds][:, :], xtile(k)[:, :],
                                    c_nm[:, cslot(k):cslot(k) + 1],
                                    c_rsd[:, cslot(k):cslot(k) + 1],
                                    OP.add, OP.mult).then_inc(sem["s_app"], 1)

                for g in range(NG):
                    if g >= 1:
                        for k in grp_tiles(g - 1):
                            v_apply(v, k, g - 1)
                    if g >= 3:
                        for _ in range(2):
                            if evq:
                                evq.pop(0)(v)
                    for k in grp_tiles(g):
                        ld_wait(v, k)
                        v.reduce_sum(c_ngs[:, cslot(k):cslot(k) + 1],
                                     xtile(k)[:, :], axis=AX, negate=True)
                    v.wait_ge(sem["s_sq"], list(grp_tiles(g))[-1] + 1)
                    v.tensor_scalar(cs4(c_t1, g), cs4(c_ssq, g), INV_H, EPS, OP.mult, OP.add)
                    v.tensor_scalar_mul(cs4(c_nm, g), cs4(c_ngs, g), INV_H)
                    v.tensor_copy(c_ngs[:, 2:3], c_t1[:, 0:1])
                    v.tensor_mul(cs4(c_m2, g), cs4(c_nm, g), cs4(c_nm, g))
                    v.tensor_copy(c_ngs[:, 3:4], c_t1[:, 0:1])
                    v.tensor_sub(cs4(c_vpe, g), cs4(c_t1, g), cs4(c_m2, g))
                    v.tensor_copy(c_ngs[:, 6:7], c_t1[:, 0:1])
                    v.reciprocal(cs4(c_rin, g), cs4(c_vpe, g)).then_inc(sem["s_var"], 1)
                for k in grp_tiles(NG - 1):
                    v_apply(v, k, NG - 1)
                while evq:
                    evq.pop(0)(v)
                v.wait_ge(sem["d_nb"], 16 * 18)
                v.tensor_scalar_mul(negb_cols[:, :], negb_cols[:, :], -1.0
                                    ).then_inc(sem["s_bneg"], 1)

            @blk.scalar
            def _(scl):
                def emit_store(scl, k):
                    ds = k % 4
                    scl.wait_ge(sem["s_app"], k + 1)
                    scl.dma_start(ln_dst_dram(k), dst_sb[ds][:, :]).then_inc(sem[f"d_st{ds}"], 16)

                popmap = {}

                def ev_xfo(scl):
                    for s4 in range(4):
                        scl.wait_ge(sem[f"d_st{s4}"], 16 * stcnt(NT, s4))
                    for g in range(8):
                        scl.dma_start_transpose(
                            xfo[:, g * QR:(g + 1) * QR],
                            s_lnown.ap()[:, g * 128:(g + 1) * 128]
                        ).then_inc(sem["d_xfo"], 16)
                popmap[5] = [ev_xfo]

                for gg in range(8):
                    def ev_wvost(scl, gg=gg):
                        scl.wait_ge(sem["s_fev"], 8 + gg + 1)
                        scl.dma_start(s_wvo.ap()[:, gg * H:(gg + 1) * H],
                                      wvo_stage[gg][:, :]).then_inc(sem[f"d_wvost{gg % 2}"], 16)
                    popmap.setdefault(9 + gg // 2, []).append(ev_wvost)

                for gp in range(8):
                    def ev_q2(scl, gp=gp):
                        scl.wait_ge(sem["s_qev"], 5 * (gp + 1))
                        for ncx, (off, nw) in enumerate(NCHUNKS):
                            scl.activation(q2buf[gp % 2][:, off:off + nw],
                                           qstage[(gp % 3) * 5 + ncx][:, 0:nw],
                                           AF.Square).then_inc(sem["s_q2"], 1)
                    popmap.setdefault(12 + gp // 2, []).append(ev_q2)

                def ev_negb_s(scl):
                    scl.wait_ge(sem["s_bmm"], 8)
                    for ncx, (off, nw) in enumerate(NCHUNKS):
                        scl.activation(b_row[0:1, off:off + nw], ps8[ncx][0:1, 0:nw],
                                       AF.Sqrt, scale=float(H)).then_inc(sem["s_bsq"], 1)
                    scl.wait_ge(sem["s_bsq"], 5)
                    scl.dma_start(s_negb.ap().rearrange("a p o -> o (a p)"),
                                  b_row[0:1, :]).then_inc(sem["d_nb"], 16)
                    scl.wait_ge(sem["d_nb"], 16)
                    for i in range(NT):
                        scl.dma_start(negb_cols[:, 4 * i:4 * i + 1], s_negb[i]).then_inc(sem["d_nb"], 16)

                for g in range(NG):
                    if g >= 1:
                        scl.wait_ge(sem["s_var"], g)
                        scl.activation(cs4(c_rsd, g - 1), cs4(c_rin, g - 1),
                                       AF.Sqrt).then_inc(sem["s_rstd"], 1)
                        for k in grp_tiles(g - 1):
                            emit_store(scl, k)
                    for k in grp_tiles(g):
                        ld_wait(scl, k)
                        scl.activation(sq_dummy[:, :], xtile(k)[:, :], AF.Square,
                                       accum_out=c_ssq[:, cslot(k):cslot(k) + 1]
                                       ).then_inc(sem["s_sq"], 1)
                    for fn in popmap.pop(g, []):
                        fn(scl)
                g = NG - 1
                scl.wait_ge(sem["s_var"], NG)
                scl.activation(cs4(c_rsd, g), cs4(c_rin, g),
                               AF.Sqrt).then_inc(sem["s_rstd"], 1)
                for k in grp_tiles(g):
                    emit_store(scl, k)
                for g in sorted(popmap):
                    for fn in popmap.pop(g, []):
                        fn(scl)
                ev_negb_s(scl)

        prostk.close()

        arena_stk = contextlib.ExitStack()
        arena = arena_stk.enter_context(nc.sbuf_tensor("arena", [128, 8 * S], BF16))

        if phases >= 2:
          with nc.named_scope("phA"), nc.Block() as blk:
            @blk.sync
            def _(s):
                for s4 in range(4):
                    s.wait_ge(sem[f"d_st{s4}"], 16 * stcnt(NLN, s4))
                for g in range(8):
                    s.dma_start_transpose(arena[:, g * S:(g + 1) * S],
                                          s_xt.ap()[:, g * 128:(g + 1) * 128]
                                          ).then_inc(sem["d_xf" if g < 4 else "d_xf2"], 16)
                for i in range(NT):
                    p = i % 2
                    if i >= 2:
                        s.wait_ge(sem["s_sc"], 4 * (i - 1))
                    s.dma_start(qpf_tile[p].rearrange("p (g s) -> p g s", s=128),
                                s_qpf.ap()[:, :, i * 128:(i + 1) * 128].transpose([1, 0, 2])
                                ).then_inc(sem[f"d_qt{p}"], 16)

            @blk.tensor
            def _(t):
                t.wait_ge(sem["d_xf"], 16 * 4)
                for i in range(NT):
                    p = i % 2
                    t.wait_ge(sem[f"d_qt{p}"], 16 * (i // 2 + 1))
                    if i == 1:
                        t.wait_ge(sem["d_xf2"], 16 * 4)
                    for q in range(4):
                        gq = 4 * i + q
                        grp = ps8[(q % 2) * 4:(q % 2) * 4 + 4]
                        if gq >= 2:
                            t.wait_ge(sem["s_exp"], gq - 1)
                        for gp in range(8):
                            if i == 0 and gp == 4:
                                t.wait_ge(sem["d_xf2"], 16 * 4)
                            stat = qpf_tile[p][:, gp * 128:(gp + 1) * 128]
                            for j in range(4):
                                kb = q * 2048 + j * 512
                                mm = t.matmul(grp[j][:, :], stat,
                                              arena[:, gp * S + kb: gp * S + kb + 512],
                                              start=(gp == 0), stop=(gp == 7))
                                if gp == 7 and j == 3:
                                    mm.then_inc(sem["s_sc"], 1)

            @blk.scalar
            def _(scl):
                scl.wait_ge(sem["s_bneg"], 1)
                for i in range(NT):
                    for q in range(4):
                        gq = 4 * i + q
                        grp = ps8[(q % 2) * 4:(q % 2) * 4 + 4]
                        scl.wait_ge(sem["s_sc"], gq + 1)
                        if i >= 1 and q == 0:
                            scl.wait_ge(sem["d_pst0"], 16 * i)
                            scl.wait_ge(sem["s_sv"], 5 * i)
                        for j in range(4):
                            kb = q * 2048 + j * 512
                            act = scl.activation(p_sb[:, kb:kb + 512], grp[j][:, :],
                                                 AF.Exp, bias=negb_cols[:, 4 * i:4 * i + 1])
                            if j == 3:
                                act.then_inc(sem["s_exp"], 1)
                    scl.wait_ge(sem["s_exp"], 4 * (i + 1))
                    scl.dma_start(s_p[i], p_sb[:, :]).then_inc(sem["d_pst0"], 16)

            @blk.vector
            def _(v):
                for i in range(NT):
                    for q in range(4):
                        v.wait_ge(sem["s_exp"], 4 * i + q + 1)
                        v.reduce_sum(s_part[:, 4 * q:4 * q + 1],
                                     p_sb[:, q * 2048:(q + 1) * 2048], axis=AX
                                     ).then_inc(sem["s_sv"], 1)
                    v.tensor_copy(s_part[:, 1:2], s_part[:, 0:1])
                    v.reduce_sum(s_all[:, 4 * i:4 * i + 1], s_part[:, 0:13:4], axis=AX
                                 ).then_inc(sem["s_sv"], 1)

        rstk.close()
        pstk.close()

        rstk2 = contextlib.ExitStack()
        pt_sb = [rstk2.enter_context(nc.sbuf_tensor(f"pt{p}", [128, S // 2], BF16, side="right"))
                 for p in range(2)]
        wvoS = rstk2.enter_context(nc.sbuf_tensor("wvoS", [128, 8 * H], BF16, side="right"))

        pstk2 = contextlib.ExitStack()
        ctx_ps = [[pstk2.enter_context(nc.psum_tensor(f"ctx{p}{h}", [128, 512], F32))
                   for h in range(2)] for p in range(2)]
        trs_ps = pstk2.enter_context(nc.psum_tensor("trs", [128, H], BF16))
        attn_ps = [pstk2.enter_context(nc.psum_tensor(f"attn{h}", [128, 512], F32))
                   for h in range(2)]

        if phases >= 3:
          with nc.named_scope("phB"), nc.Block() as blk:
            @blk.sync
            def _(s):
                def xt_chunk(c):
                    s.dma_start(arena[:, c * S:(c + 1) * S].rearrange("p (g f) -> p g f", f=H),
                                s_xt.rearrange("(g p) f -> p g f", p=128)[:, c * 8:(c + 1) * 8, :]
                                ).then_inc(sem["d_xt"], 16)

                def pt_load(i, half):
                    s.wait_ge(sem["d_pst0"], 16 * (i + 1))
                    hx = 2 * i + half
                    if hx >= 2:
                        s.wait_ge(sem["s_pvh"], hx - 1)
                    s.dma_start_transpose(
                        pt_sb[half][:, :].rearrange("p (j s) -> p j s", s=128),
                        s_p[i][:, 4096 * half:4096 * half + 4096
                               ].rearrange("s (j t) -> s j t", t=128)
                    ).then_inc(sem[f"d_pt{half}"], 16)

                def xo2_load(i):
                    if i >= 2:
                        s.wait_ge(sem["s_x1v"], i - 1)
                    s.dma_start(xpair[i % 2][:, 0:H], x_own[i * 128:(i + 1) * 128, :]
                                ).then_inc(sem[f"d_xo2{i % 2}"], 16)

                pt_load(0, 0)
                xt_chunk(0)
                pt_load(0, 1)
                xt_chunk(1)
                xo2_load(0)
                for c in range(2, 8):
                    xt_chunk(c)
                s.wait_ge(sem["d_wvost0"], 16 * 4)
                s.wait_ge(sem["d_wvost1"], 16 * 4)
                s.dma_start(wvoS[:, :], s_wvo.ap()).then_inc(sem["d_wvold"], 16)
                for i in range(1, NT):
                    pt_load(i, 0)
                    pt_load(i, 1)
                    xo2_load(i)

            @blk.tensor
            def _(t):
                t.wait_ge(sem["d_wvold"], 16)
                t.wait_ge(sem["d_misc"], 32)
                for i in range(NT):
                    p = i % 2
                    if i >= 2:
                        t.wait_ge(sem["s_ctx"], i - 1)
                    for half in range(2):
                        hx = 2 * i + half
                        t.wait_ge(sem[f"d_pt{half}"], 16 * (i + 1))
                        for jj in range(32):
                            j = 32 * half + jj
                            if i == 0:
                                t.wait_ge(sem["d_xt"], 16 * min(8, j // 8 + 2))
                            stat = pt_sb[half][:, jj * 128:(jj + 1) * 128]
                            t.matmul(ctx_ps[p][0][:, :], stat, arena[:, j * H: j * H + 512],
                                     start=(j == 0), stop=(j == 63))
                            mm = t.matmul(ctx_ps[p][1][:, :], stat,
                                          arena[:, j * H + 512:(j + 1) * H],
                                          start=(j == 0), stop=(j == 63))
                            if jj == 31:
                                mm.then_inc(sem["s_pvh"], 1)
                    t.wait_ge(sem["s_ctx"], i + 1)
                    if i >= 1:
                        t.wait_ge(sem["s_trsV"], i)
                    for c in range(8):
                        tr = t.transpose(trs_ps[:, c * 128:(c + 1) * 128],
                                         ctxX_sb[:, c * 128:(c + 1) * 128], identb[:, :])
                        if c == 7:
                            tr.then_inc(sem["s_trsT"], 1)
                    t.wait_ge(sem["s_trsV"], i + 1)
                    if i >= 1:
                        t.wait_ge(sem["s_x1v"], i)
                    for g in range(8):
                        stat = ctxXf_sb[:, g * 128:(g + 1) * 128]
                        t.matmul(attn_ps[0][:, :], stat, wvoS[:, g * H: g * H + 512],
                                 start=(g == 0), stop=(g == 7))
                        mm = t.matmul(attn_ps[1][:, :], stat, wvoS[:, g * H + 512:(g + 1) * H],
                                      start=(g == 0), stop=(g == 7))
                        if g == 7:
                            mm.then_inc(sem["s_op"], 1)

            @blk.vector
            def _(v):
                for i in range(NT):
                    v.reciprocal(sinv_all[:, 4 * i:4 * i + 1], s_all[:, 4 * i:4 * i + 1])
                for i in range(NT):
                    p = i % 2
                    k = NLN + i
                    v.wait_ge(sem["s_pvh"], 2 * (i + 1))
                    v.tensor_scalar(ctxX_sb[:, 0:512], ctx_ps[p][0][:, :],
                                    sinv_all[:, 4 * i:4 * i + 1], None, OP.mult)
                    v.tensor_scalar(ctxX_sb[:, 512:H], ctx_ps[p][1][:, :],
                                    sinv_all[:, 4 * i:4 * i + 1], None, OP.mult).then_inc(sem["s_ctx"], 1)
                    v.wait_ge(sem["s_trsT"], i + 1)
                    v.tensor_copy(ctxXf_sb[:, :], trs_ps[:, :]).then_inc(sem["s_trsV"], 1)
                    v.wait_ge(sem["s_op"], i + 1)
                    v.wait_ge(sem[f"d_xo2{p}"], 16 * (i // 2 + 1))
                    if i >= 1:
                        v.wait_ge(sem["d_x1st"], 16 * max(0, i - 1))
                        v.wait_ge(sem["s_sq"], k)
                    v.tensor_add(x1_sb[:, 0:512], attn_ps[0][:, :], xpair[p][:, 0:512])
                    v.tensor_add(x1_sb[:, 512:H], attn_ps[1][:, :],
                                 xpair[p][:, 512:H]).then_inc(sem["s_x1v"], 1)
                    ln2_vec(v, i)

            @blk.scalar
            def _(scl):
                for i in range(NT):
                    k = NLN + i
                    scl.wait_ge(sem["s_x1v"], i + 1)
                    if i >= 1:
                        scl.dma_start(s_x1[i], x1_sb[:, :]).then_inc(sem["d_x1st"], 16)
                    ln2_scal(scl, i)

        rstk2.close()
        pstk2.close()
        arena_stk.close()

        cstk = contextlib.ExitStack()
        ln1f = cstk.enter_context(nc.sbuf_tensor("ln1f", [128, 8 * QR], BF16))
        wrTS = cstk.enter_context(nc.sbuf_tensor("wrTS", [128, 8 * H], BF16))
        wkfTS = cstk.enter_context(nc.sbuf_tensor("wkfTS", [128, 8 * H], BF16))
        wvfTS = cstk.enter_context(nc.sbuf_tensor("wvfTS", [128, 8 * H], BF16))
        d_sb = cstk.enter_context(nc.sbuf_tensor("d_sb", [128, 8 * 512], BF16))
        xr_sb = cstk.enter_context(nc.sbuf_tensor("xr_sb", [128, 8 * 512], BF16))
        xk_sb = cstk.enter_context(nc.sbuf_tensor("xk_sb", [128, 8 * 512], BF16))
        k_act = cstk.enter_context(nc.sbuf_tensor("k_act", [128, 8 * 512], BF16))
        rf = cstk.enter_context(nc.sbuf_tensor("rf", [128, 8 * 512], F32))
        relu_tmp = [cstk.enter_context(nc.sbuf_tensor(f"relu{p}", [128, 512], F32)) for p in range(2)]
        ffn_sb = cstk.enter_context(nc.sbuf_tensor("ffn_sb", [128, 8 * 512], F32))

        identf = cstk.enter_context(nc.sbuf_tensor("identf", [128, 128], F32))
        mu_sb = cstk.enter_context(nc.sbuf_tensor("mu_sb", [128, 16], F32))
        psC = [cstk.enter_context(nc.psum_tensor(f"psC{i}", [128, 512], F32)) for i in range(4)]
        fout_ps = [[cstk.enter_context(nc.psum_tensor(f"fout{p}{h}", [128, 512], F32))
                    for h in range(2)] for p in range(2)]

        if phases >= 4:
          with nc.named_scope("phC"), nc.Block() as blk:
            @blk.sync
            def _(s):
                for s4 in range(4):
                    s.wait_ge(sem[f"d_st{s4}"],
                              16 * (stcnt(NLN, s4) + len([j for j in range(NT) if j % 4 == s4])))
                for c in range(8):
                    s.dma_start_transpose(ln1f[:, c * QR:(c + 1) * QR],
                                          s_ln1.ap()[:, c * 128:(c + 1) * 128]
                                          ).then_inc(sem["d_lnf"], 16)
                s.wait_ge(sem["d_x1st"], 16 * 16)
                for u in range(16):
                    p = u % 2
                    if u >= 2:
                        s.wait_ge(sem["s_outv"], u - 1)
                    s.dma_start(xpair[p][:, 0:H], s_x1[u + 1]).then_inc(sem[f"d_x13{p}"], 16)

            @blk.gpsimd
            def _(g):
                g.dma_start(mu_sb[:, 0:8], mu_r_in[:, :]).then_inc(sem["d_wf"], 16)
                g.dma_start(mu_sb[:, 8:16], mu_k_in[:, :]).then_inc(sem["d_wf"], 16)
                g.dma_start(identf[:, :], identf_in[:, :]).then_inc(sem["d_wf"], 16)
                for w_dram, w_sb in ((wrT, wrTS), (wkfT, wkfTS), (wvfT, wvfTS)):
                    g.dma_start(w_sb.rearrange("p (c f) -> p c f", f=H),
                                w_dram.rearrange("(c p) f -> p c f", p=128)
                                ).then_inc(sem["d_wf"], 16)

            @blk.tensor
            def _(t):
                t.wait_ge(sem["d_wf"], 16 * 4)
                for stt in range(4):
                    t.wait_ge(sem["s_xr"], stt + 1)
                    for d in range(8):
                        dd = 8 * stt + d
                        if dd >= 2:
                            t.wait_ge(sem["s_sig"], dd - 1)
                        for c in range(8):
                            mm = t.matmul(psC[d % 2][:, :],
                                          wrTS[:, c * H + d * 128: c * H + (d + 1) * 128],
                                          xr_sb[:, c * 512:(c + 1) * 512],
                                          start=(c == 0), stop=(c == 7))
                            if c == 7:
                                mm.then_inc(sem["s_rp"], 1)
                    if stt == 0:
                        t.wait_ge(sem["d_wf"], 16 * 5)
                    t.wait_ge(sem["s_xk"], stt + 1)
                    for d in range(8):
                        dd = 8 * stt + d
                        if dd >= 2:
                            t.wait_ge(sem["s_rel"], dd - 1)
                        for c in range(8):
                            mm = t.matmul(psC[2 + d % 2][:, :],
                                          wkfTS[:, c * H + d * 128: c * H + (d + 1) * 128],
                                          xk_sb[:, c * 512:(c + 1) * 512],
                                          start=(c == 0), stop=(c == 7))
                            if c == 7:
                                mm.then_inc(sem["s_kp"], 1)
                    if stt == 0:
                        t.wait_ge(sem["d_wf"], 16 * 6)
                    t.wait_ge(sem["s_ka"], 8 * (stt + 1))
                    t.wait_ge(sem["s_sig"], 8 * (stt + 1))
                    for e in range(8):
                        ee = 8 * stt + e
                        if ee >= 2:
                            t.wait_ge(sem["s_ff"], ee - 1)
                        for dd2 in range(8):
                            mm = t.matmul(psC[e % 2][:, :],
                                          wvfTS[:, dd2 * H + e * 128: dd2 * H + (e + 1) * 128],
                                          k_act[:, dd2 * 512:(dd2 + 1) * 512],
                                          start=(dd2 == 0), stop=(dd2 == 7))
                            if dd2 == 7:
                                mm.then_inc(sem["s_vp"], 1)
                    t.wait_ge(sem["s_ff"], 8 * (stt + 1))
                    for sub in range(4):
                        u = 4 * stt + sub
                        if u >= 2:
                            t.wait_ge(sem["s_outv"], u - 1)
                        for e in range(8):
                            tr = t.transpose(fout_ps[u % 2][e // 4][:, (e % 4) * 128:(e % 4 + 1) * 128],
                                             ffn_sb[:, e * 512 + sub * 128: e * 512 + sub * 128 + 128],
                                             identf[:, :])
                            if e == 7:
                                tr.then_inc(sem["s_tr3"], 1)

            @blk.vector
            def _(v):
                v.wait_ge(sem["d_lnf"], 16 * 8)
                v.wait_ge(sem["d_misc"], 32)
                v.wait_ge(sem["d_wf"], 16 * 6)
                for c in range(8):
                    mk = v.tensor_mul(ln1f[:, c * QR + 127: c * QR + 128],
                                      ln1f[:, c * QR + 127: c * QR + 128], hmask[:, 0:1])
                    if c == 7:
                        mk.then_inc(sem["s_mask"], 1)
                v.wait_ge(sem["d_wf"], 16 * 3)

                def emit_xrxk(v, stt):
                    if stt >= 1:
                        v.wait_ge(sem["s_rp"], 8 * stt)
                    for c in range(8):
                        cur = ln1f[:, c * QR + 128 + stt * 512: c * QR + 128 + stt * 512 + 512]
                        prev = ln1f[:, c * QR + 127 + stt * 512: c * QR + 127 + stt * 512 + 512]
                        v.tensor_sub(d_sb[:, c * 512:(c + 1) * 512], cur, prev)
                        sr = v.scalar_tensor_tensor(xr_sb[:, c * 512:(c + 1) * 512],
                                                    d_sb[:, c * 512:(c + 1) * 512],
                                                    mu_sb[:, c:c + 1], prev, OP.mult, OP.add)
                        if c == 7:
                            sr.then_inc(sem["s_xr"], 1)
                    if stt >= 1:
                        v.wait_ge(sem["s_kp"], 8 * stt)
                    for c in range(8):
                        prev = ln1f[:, c * QR + 127 + stt * 512: c * QR + 127 + stt * 512 + 512]
                        stx = v.scalar_tensor_tensor(xk_sb[:, c * 512:(c + 1) * 512],
                                                     d_sb[:, c * 512:(c + 1) * 512],
                                                     mu_sb[:, 8 + c:9 + c], prev, OP.mult, OP.add)
                        if c == 7:
                            stx.then_inc(sem["s_xk"], 1)

                emit_xrxk(v, 0)
                for stt in range(4):
                    for d in range(8):
                        dd = 8 * stt + d
                        v.wait_ge(sem["s_rel"], dd + 1)
                        ka = v.scalar_tensor_tensor(k_act[:, d * 512:(d + 1) * 512],
                                                    relu_tmp[d % 2][:, :], 1.0,
                                                    relu_tmp[d % 2][:, :], OP.mult, OP.mult)
                        ka.then_inc(sem["s_ka"], 1)
                    if stt + 1 < 4:
                        emit_xrxk(v, stt + 1)
                    for e in range(8):
                        ee = 8 * stt + e
                        v.wait_ge(sem["s_vp"], ee + 1)
                        fe = v.tensor_tensor(ffn_sb[:, e * 512:(e + 1) * 512],
                                             psC[e % 2][:, :], rf[:, e * 512:(e + 1) * 512],
                                             OP.mult)
                        fe.then_inc(sem["s_ff"], 1)
                    for sub in range(4):
                        u = 4 * stt + sub
                        p = u % 2
                        v.wait_ge(sem["s_tr3"], u + 1)
                        v.wait_ge(sem[f"d_x13{p}"], 16 * (u // 2 + 1))
                        if u >= 2:
                            v.wait_ge(sem[f"d_ost{p}"], 16 * (u // 2))
                        v.tensor_add(out_sb[p][:, 0:512], fout_ps[p][0][:, :],
                                     xpair[p][:, 0:512])
                        v.tensor_add(out_sb[p][:, 512:H], fout_ps[p][1][:, :],
                                     xpair[p][:, 512:H]).then_inc(sem["s_outv"], 1)

            @blk.scalar
            def _(scl):
                for stt in range(4):
                    for d in range(8):
                        dd = 8 * stt + d
                        scl.wait_ge(sem["s_rp"], dd + 1)
                        if d == 0 and stt >= 1:
                            scl.wait_ge(sem["s_ff"], 8 * stt)
                        scl.activation(rf[:, d * 512:(d + 1) * 512], psC[d % 2][:, :],
                                       AF.Sigmoid).then_inc(sem["s_sig"], 1)
                    for d in range(8):
                        dd = 8 * stt + d
                        scl.wait_ge(sem["s_kp"], dd + 1)
                        if dd >= 2:
                            scl.wait_ge(sem["s_ka"], dd - 1)
                        scl.activation(relu_tmp[d % 2][:, :], psC[2 + d % 2][:, :],
                                       AF.Relu).then_inc(sem["s_rel"], 1)
                    for sub in range(4):
                        u = 4 * stt + sub
                        scl.wait_ge(sem["s_outv"], u + 1)
                        scl.dma_start(o_out[u * 128:(u + 1) * 128, :],
                                      out_sb[u % 2][:, :]).then_inc(sem[f"d_ost{u % 2}"], 16)
                scl.wait_ge(sem["d_ost0"], 16 * 8)
                scl.wait_ge(sem["d_ost1"], 16 * 8)

        cstk.close()

    return nc


_NC_CACHE = [None]


def kernel(L0_attn_x, L0_attn_W_qkv, L0_attn_W_out, L0_ffn_W_r, L0_ffn_W_k,
           L0_ffn_W_v, L0_ffn_mu_r, L0_ffn_mu_k):
    x = np.ascontiguousarray(np.asarray(L0_attn_x, dtype=np.float32))
    wqkv = np.asarray(L0_attn_W_qkv, dtype=np.float32)
    wout = np.asarray(L0_attn_W_out, dtype=np.float32)
    wr = np.asarray(L0_ffn_W_r, dtype=np.float32)
    wkf = np.asarray(L0_ffn_W_k, dtype=np.float32)
    wvf = np.asarray(L0_ffn_W_v, dtype=np.float32)
    mur = np.asarray(L0_ffn_mu_r, dtype=np.float32)
    muk = np.asarray(L0_ffn_mu_k, dtype=np.float32)

    wq_h = np.ascontiguousarray(wqkv[0::3, :])
    wk_h = np.ascontiguousarray(wqkv[1::3, :])
    wv_h = np.ascontiguousarray(wqkv[2::3, :])
    woutT_h = np.ascontiguousarray(wout.T)
    wrT_h = np.ascontiguousarray(wr.T)
    wkfT_h = np.ascontiguousarray(wkf.T)
    wvfT_h = np.ascontiguousarray(wvf.T)
    mur_h = np.ascontiguousarray(mur.reshape(8, 128).T)
    muk_h = np.ascontiguousarray(muk.reshape(8, 128).T)
    identb_h = np.eye(128, dtype=np.float32).astype(ml_dtypes.bfloat16)
    identf_h = np.eye(128, dtype=np.float32)

    if _NC_CACHE[0] is None:
        _NC_CACHE[0] = build_nc()
    nc = _NC_CACHE[0]

    in_maps = []
    for c in range(8):
        b, m = c // 4, c % 4
        s0 = 2048 * m
        if m == 0:
            xo = np.concatenate([x[b, 0:128], x[b, 0:2048]], axis=0)
            hm = np.zeros((128, 1), np.float32)
        else:
            xo = x[b, s0 - 128:s0 + 2048]
            hm = np.ones((128, 1), np.float32)
        in_maps.append({
            "x_batch": x[b],
            "x_own": np.ascontiguousarray(xo),
            "wq": wq_h, "wk": wk_h, "wv": wv_h, "woutT": woutT_h,
            "wrT": wrT_h, "wkfT": wkfT_h, "wvfT": wvfT_h,
            "mu_r_in": mur_h, "mu_k_in": muk_h,
            "identb_in": identb_h, "identf_in": identf_h,
            "hmask_in": hm,
        })

    res = run_bass_kernel_spmd(nc, in_maps, core_ids=list(range(8)))
    out = np.empty((2, 8192, 1024), np.float32)
    for c in range(8):
        b, m = c // 4, c % 4
        out[b, 2048 * m:2048 * (m + 1), :] = res.results[c]["o_out"]
    return out
